# version 03
# baseline (speedup 1.0000x reference)
import math
from contextlib import ExitStack, contextmanager

import numpy as np
import concourse.bass as bass
import concourse.mybir as mybir
from concourse.bass_utils import run_bass_kernel_spmd

F32 = mybir.dt.float32
BF16 = mybir.dt.bfloat16
I32 = mybir.dt.int32
AF = mybir.ActivationFunctionType
ALU = mybir.AluOpType
AX = mybir.AxisListType
PE = mybir.EngineType.PE

ISD = 1.0 / math.sqrt(128.0)
LN_EPS = 1e-5
ROPE_THETA = 10000.0

CFG_FULL = dict(S=4096, D=4096, NH=32, NKV=8, I=11008, NB=4)


@contextmanager
def kfori(nc, n_iters, engine_type):
    engines = [engine_type]
    loop_id = nc.next_id()
    loop_start = f"kfori_{loop_id}_loop"
    loop_end = f"kfori_{loop_id}_end"
    regs = nc.alloc_registers(f"kfori_{loop_id}_i", engines=engines)
    nc.regs_mov(regs, 0)
    nc.br(loop_start, engines=engines)
    with nc.body(loop_start, valid_engines=engines):
        yield None
        nc.regs_alu(regs, regs, 1, op=ALU.add)
        nc.br_lt(regs, n_iters, on_true=loop_start, on_false=loop_end,
                 engines=engines)
    nc.switch_bb(loop_end)
    for h in regs:
        nc.free_register(h)


class EvtDict(dict):
    recording = False

    def __missing__(self, key):
        if EvtDict.recording:
            return 0
        raise KeyError(key)


def own_tile_order(NT):
    order = []
    a = 0
    while a * 8 < NT:
        order.extend(range(8 * a + 7, 8 * a - 1, -1))
        a += 1
    return order


def build_program(cfg, debug=()):
    S, D, NH, NKV, I = cfg["S"], cfg["D"], cfg["NH"], cfg["NKV"], cfg["I"]
    HC = D // 128
    ST = S // 128
    OWN = S // 2
    NT = OWN // 128
    OC = NH
    IC = I // 128
    QGN = OWN // 512
    NPAIR = NT // 8
    GRP = NH // NKV
    assert NT % 8 == 0 and OC % 2 == 0 and HC % 2 == 0
    KVW = NKV * 128
    NKVP = cfg.get("NKVP", min(4, NKV))
    KVQ = KVW // NKVP
    NGK = KVQ // 128
    KW = 2 * KVQ
    NSEG_K = S // 512
    NQH = max(1, OWN // 1024)

    nc = bass.Bass("TRN2", target_bir_lowering=False, debug=False)

    def din(name, shape, dt):
        return nc.dram_tensor(name, shape, dt, kind="ExternalInput").ap()

    xt_full = din("xt_full", [ST, 128, HC * 128], F32)
    xt_own = din("xt_own", [128, HC * OWN], F32)
    wq_t = din("wq_t", [OC // 2, 128, HC * 256], F32)
    wkv_t = din("wkv_t", [NKVP, 128, HC * KW], F32)
    wo_t = din("wo_t", [HC, 128, OC * 128], F32)
    wg_t = din("wg_t", [IC, 128, HC * 128], I32)
    wu_t = din("wu_t", [IC, 128, HC * 128], I32)
    wd_t = din("wd_t", [HC // 2, 128, IC * 256], I32)
    sg_in = din("sg", [128, IC], F32)
    su_in = din("su", [128, IC], F32)
    sd_in = din("sd", [128, HC], F32)
    cosk = din("cosk", [S, NGK * 64], F32)
    sink = din("sink", [S, NGK * 64], F32)
    cosq = din("cosq", [64, OWN], F32)
    sinq = din("sinq", [64, OWN], F32)
    maskA_in = din("maskA", [128, 128], BF16)
    maskB_in = din("maskB", [128, 128], BF16)
    identf_in = din("identf", [128, 128], F32)
    onesb_in = din("onesb", [128, 8], BF16)
    onesf_in = din("onesf", [1, 128], F32)
    eps_in = din("epsv", [1, 1], F32)

    out = nc.dram_tensor("out", [OWN, D], F32, kind="ExternalOutput").ap()

    def scr(name, shape, dt):
        kind = "ExternalOutput" if name in debug else "Internal"
        return nc.dram_tensor(name, shape, dt, kind=kind).ap()

    xt_full_bf = scr("xt_full_bf", [ST, 128, HC * 128], BF16)
    xt_own_bf = scr("xt_own_bf", [128, HC * OWN], BF16)
    wq_bf = scr("wq_bf", [OC // 2, 128, HC * 256], BF16)
    wkv_bf = scr("wkv_bf", [NKVP, 128, HC * KW], BF16)
    wo_bf = scr("wo_bf", [HC, 128, OC * 128], BF16)
    wg_bf = scr("wg_bf", [IC, 128, HC * 128], BF16)
    wu_bf = scr("wu_bf", [IC, 128, HC * 128], BF16)
    wd_bf = scr("wd_bf", [HC // 2, 128, IC * 256], BF16)
    q_scr = scr("q_scr", [NH, 128, OWN], BF16)
    ktm_scr = scr("ktm_scr", [NKV, S, 128], BF16)
    av_scr = scr("av_scr", [NH, 128, OWN], BF16)
    y1_scr = scr("y1_scr", [128, HC * OWN], F32)
    hact_scr = scr("hact_scr", [IC, 128, OWN], BF16)
    y2_scr = scr("y2_scr", [128, HC * OWN], F32)
    kfm_dbg = scr("kfm_dbg", [128, NKV * S], BF16) if "kfm_dbg" in debug else None
    vtm_dbg = scr("vtm_dbg", [128, ST * KVW], BF16) if "vtm_dbg" in debug else None
    ln1_dbg = scr("ln1_dbg", [128, 2 * OWN], F32) if "ln1_dbg" in debug else None

    es = ExitStack()
    with es:
        SB = lambda n, s, d: es.enter_context(nc.sbuf_tensor(n, s, d))

        mA = SB("mA", [128, 128], BF16)
        mB = SB("mB", [128, 128], BF16)
        idf = SB("idf", [128, 128], F32)
        ob_sb = SB("ob_sb", [128, 8], BF16)
        of_sb = SB("of_sb", [1, 128], F32)
        sd_sb = SB("sd_sb", [128, HC], F32)
        sg_sb = SB("sg_sb", [128, IC], F32)
        su_sb = SB("su_sb", [128, IC], F32)
        eps_sb = SB("eps_sb", [1, 1], F32)
        dvm_sb = SB("dvm_sb", [1, 8], F32)

        ps_a = es.enter_context(nc.psum_tensor("ps_a", [128, 2048], F32))
        ps_b = es.enter_context(nc.psum_tensor("ps_b", [128, 2048], F32))

        sems = {}
        for nm in ("cast", "pe", "dv", "ac"):
            sems[nm] = es.enter_context(nc.semaphore("s_" + nm))
        CNT = {k: 0 for k in sems}

        def inc(kind, instr, k=1):
            instr.then_inc(sems[kind], k)
            CNT[kind] += k
            return CNT[kind]

        STREAMS = {}

        def mkstream(name, nslots):
            STREAMS[name] = [es.enter_context(nc.semaphore(f"q_{name}{i}"))
                             for i in range(nslots)]
            for i in range(nslots):
                CNT[("S", name, i)] = 0

        for _nm, _ns in (("ld0", 1), ("wq", 2), ("wkv", 1), ("xtb", 2),
                         ("cksk", 2), ("qh", 1), ("wo", 2), ("xtt", 4),
                         ("y1d", 2), ("wgu", 2), ("wd", 2), ("hat", 2),
                         ("y1e", 4), ("y2f", 2), ("avh", 1),
                         ("qst", 2), ("kstst", 2), ("avst", 2), ("y1st", 4),
                         ("hst", 4), ("y2st", 4), ("ost", 4)):
            mkstream(_nm, _ns)

        def sfill_g(name, slot, instr, k=16):
            key = ("S", name, slot)
            CNT[key] += k
            instr.then_inc(STREAMS[name][slot], k)
            return (name, slot, CNT[key])

        def swait_g(eng_obj, token):
            if token:
                name, slot, v = token
                eng_obj.wait_ge(STREAMS[name][slot], v)

        block = es.enter_context(nc.Block())

        precast_list = [
            ("xt_own_bf", xt_own_bf, xt_own, 2),
            ("wq_bf", wq_bf, wq_t, max(2, OC // 8)),
            ("wkv_bf", wkv_bf, wkv_t, NKVP),
            ("xt_full_bf", xt_full_bf, xt_full, 4),
            ("wo_bf", wo_bf, wo_t, 4),
            ("wg_bf", wg_bf, wg_t, 4),
            ("wu_bf", wu_bf, wu_t, 4),
            ("wd_bf", wd_bf, wd_t, 2),
        ]
        cast_done = {}

        @block.gpsimd
        def _(g):
            for name, dst, src, nch in precast_list:
                n0 = src.shape[0]
                step = max(1, n0 // nch)
                for i0 in range(0, n0, step):
                    i1 = min(n0, i0 + step)
                    inc("cast", g.dma_start(out=dst[i0:i1], in_=src[i0:i1]), 16)
                    g.wait_ge(sems["cast"], CNT["cast"])
                cast_done[name] = CNT["cast"]

        @block.sync
        def _(sy):
            for dst, src in ((mA[:], maskA_in), (mB[:], maskB_in),
                             (idf[:], identf_in), (ob_sb[:], onesb_in),
                             (of_sb[:], onesf_in), (eps_sb[:], eps_in),
                             (sd_sb[:], sd_in),
                             (sg_sb[:], sg_in), (su_sb[:], su_in)):
                sfill_g("ld0", 0, sy.dma_start(out=dst, in_=src))

        esA = ExitStack()
        xtq = esA.enter_context(nc.sbuf_tensor("xtq", [128, HC * OWN], BF16))
        wqb = [esA.enter_context(nc.sbuf_tensor(f"wqb{i}", [128, HC * 256], BF16))
               for i in range(2)]
        qstage = [esA.enter_context(nc.sbuf_tensor(f"qstage{i}", [128, OWN], BF16))
                  for i in range(2)]
        ropeq = esA.enter_context(nc.sbuf_tensor("ropeq", [64, 2 * OWN], F32))
        cq_sb = esA.enter_context(nc.sbuf_tensor("cq_sb", [64, OWN], F32))
        sq_sb = esA.enter_context(nc.sbuf_tensor("sq_sb", [64, OWN], F32))

        @block.sync
        def _(sy):
            sfill_g("ld0", 0, sy.dma_start(out=cq_sb[:], in_=cosq))
            sfill_g("ld0", 0, sy.dma_start(out=sq_sb[:], in_=sinq))
            sy.wait_ge(sems["cast"], cast_done["xt_own_bf"])
            sfill_g("ld0", 0, sy.dma_start(out=xtq[:], in_=xt_own_bf[:]))
            sy.wait_ge(sems["cast"], cast_done["wq_bf"])
            for ob in range(OC // 2):
                if ob >= 2:
                    sy.wait_ge(sems["pe"], (ob - 2) * 2 + 2)
                sfill_g("wq", ob % 2, sy.dma_start(out=wqb[ob % 2][:],
                                                   in_=wq_bf[ob]))
        aq_ld0_tok = ("ld0", 0, CNT[("S", "ld0", 0)])

        @block.tensor
        def _(tn):
            def aq_block(ob, wv, ev):
                if wv is None:
                    if ob < 2:
                        swait_g(tn, aq_ld0_tok)
                    swait_g(tn, ("wq", ob % 2, 16 * (ob // 2 + 1)))
                else:
                    if ob % 2 == 0:
                        tn.wait_ge(STREAMS["wq"][0], wv)
                    else:
                        tn.wait_ge(STREAMS["wq"][1], wv)
                        nc.regs_alu(wv, wv, 16, op=ALU.add)
                for oc in range(2):
                    gidx = 2 * ob + oc
                    pg = (ps_a, ps_b)[oc]
                    if ev is None:
                        if gidx >= 2:
                            tn.wait_ge(sems["dv"], 4 * (gidx - 2) + 3)
                    else:
                        tn.wait_ge(sems["dv"], ev)
                        nc.regs_alu(ev, ev, 4, op=ALU.add)
                    mm = None
                    for hc in range(HC):
                        for qg in range(QGN):
                            mm = tn.matmul(
                                pg[:, qg * 512:(qg + 1) * 512],
                                wqb[ob % 2][:, hc * 256 + oc * 128:
                                            hc * 256 + oc * 128 + 128],
                                xtq[:, hc * OWN + qg * 512:
                                    hc * OWN + (qg + 1) * 512],
                                start=(hc == 0), stop=(hc == HC - 1))
                    inc("pe", mm, 1)

            NOB = OC // 2
            aq_block(0, None, None)
            if NOB > 1:
                aq_block(1, None, None)
            npairs = max(0, (NOB - 2) // 2)
            if npairs > 0:
                wv = nc.alloc_registers("aq_wv", engines=[PE])
                ev = nc.alloc_registers("aq_ev", engines=[PE])
                nc.regs_mov(wv, 16 * 2)
                nc.regs_mov(ev, 11)
                pe_before = CNT["pe"]
                with kfori(nc, npairs, PE):
                    aq_block(2, wv, ev)
                    aq_block(3, wv, ev)
                CNT["pe"] = pe_before + 4 * npairs
                for r in (wv, ev):
                    for h in r:
                        nc.free_register(h)
            for ob in range(2 + 2 * npairs, NOB):
                aq_block(ob, None, None)

        aq_pe_end = CNT["pe"]
        assert aq_pe_end == OC, (aq_pe_end, OC)

        T1 = ropeq[:, 0 * OWN:1 * OWN]
        T2 = ropeq[:, 1 * OWN:2 * OWN]

        @block.vector
        def _(v):
            for gidx in range(OC):
                pg = (ps_a, ps_b)[gidx % 2]
                qs = qstage[gidx % 2]
                v.wait_ge(sems["pe"], gidx + 1)
                if gidx >= 2:
                    swait_g(v, ("qst", gidx % 2, 16 * ((gidx - 2) // 2 + 1)))
                if gidx >= 1:
                    v.wait_ge(sems["dv"], 4 * gidx)
                v.tensor_mul(T1, pg[0:64, 0:OWN], cq_sb[:])
                m2 = v.tensor_mul(T2, pg[64:128, 0:OWN], sq_sb[:])
                inc("dv", m2, 1)
                v.wait_ge(sems["dv"], CNT["dv"])
                c = v.tensor_sub(qs[0:64, :], T1, T2)
                inc("dv", c, 1)
                v.wait_ge(sems["dv"], CNT["dv"])
                v.tensor_mul(T1, pg[64:128, 0:OWN], cq_sb[:])
                m4 = v.tensor_mul(T2, pg[0:64, 0:OWN], sq_sb[:])
                inc("dv", m4, 1)
                v.wait_ge(sems["dv"], CNT["dv"])
                a2 = v.tensor_add(qs[64:128, :], T1, T2)
                inc("dv", a2, 1)

        aq_dv_end = CNT["dv"]

        @block.gpsimd
        def _(g):
            for gidx in range(OC):
                g.wait_ge(sems["dv"], 4 * gidx + 4)
                sfill_g("qst", gidx % 2,
                        g.dma_start(out=q_scr[gidx], in_=qstage[gidx % 2][:]))

        esA.close()

        esKV = ExitStack()
        kfm = esKV.enter_context(nc.sbuf_tensor("kfm", [128, NKV * S], BF16))
        vtm = esKV.enter_context(nc.sbuf_tensor("vtm", [128, ST * KVW], BF16))
        esK = ExitStack()
        wkvb = esK.enter_context(nc.sbuf_tensor("wkvb", [128, HC * KW], BF16))
        xtb = [esK.enter_context(nc.sbuf_tensor(f"xtb{i}", [128, HC * 128], BF16))
               for i in range(2)]
        kst = [esK.enter_context(nc.sbuf_tensor(f"kst{i}", [128, KVQ], BF16))
               for i in range(2)]
        ck_sb = [esK.enter_context(nc.sbuf_tensor(f"ck{i}", [128, NGK * 64], F32))
                 for i in range(2)]
        sk_sb = [esK.enter_context(nc.sbuf_tensor(f"sk{i}", [128, NGK * 64], F32))
                 for i in range(2)]
        kvt = esK.enter_context(nc.sbuf_tensor("kvt", [128, 4 * KVQ], F32))

        @block.sync
        def _(sy):
            sy.wait_ge(sems["cast"], cast_done["wkv_bf"])
            sy.wait_ge(sems["cast"], cast_done["xt_full_bf"])
            sy.wait_ge(sems["pe"], aq_pe_end)
            sy.wait_ge(sems["dv"], aq_dv_end)
            swait_g(sy, ("qst", 0, 16 * ((OC + 1) // 2)))
            swait_g(sy, ("qst", 1, 16 * (OC // 2)))
            for p in range(NKVP):
                if p > 0:
                    sy.wait_ge(sems["pe"], aq_pe_end + p * ST)
                sfill_g("wkv", 0, sy.dma_start(out=wkvb[:], in_=wkv_bf[p]))
                for st in range(ST):
                    k = p * ST + st
                    if k >= 2:
                        sy.wait_ge(sems["pe"], aq_pe_end + k - 1)
                        sy.wait_ge(sems["dv"], aq_dv_end + 2 * k - 2)
                    sfill_g("xtb", k % 2,
                            sy.dma_start(out=xtb[k % 2][:], in_=xt_full_bf[st]))
                    sfill_g("cksk", k % 2,
                            sy.dma_start(out=ck_sb[k % 2][:],
                                         in_=cosk[st * 128:(st + 1) * 128, :]))
                    sfill_g("cksk", k % 2,
                            sy.dma_start(out=sk_sb[k % 2][:],
                                         in_=sink[st * 128:(st + 1) * 128, :]))

        @block.tensor
        def _(tn):
            def akv_tile(p, st, wv, ev):
                k = p * ST + st
                pg = (ps_a, ps_b)[k % 2]
                if wv is None:
                    swait_g(tn, ("wkv", 0, 16 * (p + 1)))
                    swait_g(tn, ("xtb", k % 2, 16 * (k // 2 + 1)))
                else:
                    if k % 2 == 0:
                        tn.wait_ge(STREAMS["xtb"][0], wv)
                    else:
                        tn.wait_ge(STREAMS["xtb"][1], wv)
                        nc.regs_alu(wv, wv, 16, op=ALU.add)
                if ev is None:
                    if k >= 2:
                        tn.wait_ge(sems["dv"], aq_dv_end + 2 * (k - 2) + 1)
                else:
                    tn.wait_ge(sems["dv"], ev)
                    nc.regs_alu(ev, ev, 2, op=ALU.add)
                mm = None
                for hc in range(HC):
                    for w0 in range(0, KW, 512):
                        w1 = min(KW, w0 + 512)
                        mm = tn.matmul(
                            pg[:, w0:w1],
                            xtb[k % 2][:, hc * 128:(hc + 1) * 128],
                            wkvb[:, hc * KW + w0: hc * KW + w1],
                            start=(hc == 0), stop=(hc == HC - 1))
                inc("pe", mm, 1)

            for p in range(NKVP):
                akv_tile(p, 0, None, None)
                if ST > 1:
                    akv_tile(p, 1, None, None)
                nst = max(0, (ST - 2) // 2)
                if nst > 0:
                    wv = nc.alloc_registers(f"akv_wv{p}", engines=[PE])
                    ev = nc.alloc_registers(f"akv_ev{p}", engines=[PE])
                    nc.regs_mov(wv, 16 * (p * ST // 2 + 2))
                    nc.regs_mov(ev, aq_dv_end + 2 * (p * ST) + 1)
                    pe_before = CNT["pe"]
                    tn.wait_ge(STREAMS["wkv"][0], 16 * (p + 1))
                    with kfori(nc, nst, PE):
                        akv_tile(p, 2, wv, ev)
                        akv_tile(p, 3, wv, ev)
                    CNT["pe"] = pe_before + 2 * nst
                    for r in (wv, ev):
                        for h in r:
                            nc.free_register(h)
                for st in range(2 + 2 * max(0, (ST - 2) // 2), ST):
                    akv_tile(p, st, None, None)

        akv_pe_end = CNT["pe"]
        assert akv_pe_end == aq_pe_end + NKVP * ST

        @block.vector
        def _(v):
            for p in range(NKVP):
                for st in range(ST):
                    k = p * ST + st
                    pg = (ps_a, ps_b)[k % 2]
                    ks = kst[k % 2]
                    v.wait_ge(sems["pe"], aq_pe_end + k + 1)
                    swait_g(v, ("cksk", k % 2, 32 * (k // 2 + 1)))
                    if k >= 2:
                        swait_g(v, ("kstst", k % 2,
                                    16 * NGK * ((k - 2) // 2 + 1)))
                    if k >= 1:
                        v.wait_ge(sems["dv"], aq_dv_end + 2 * k)
                    v.tensor_copy(vtm[:, st * KVW + p * KVQ:
                                      st * KVW + (p + 1) * KVQ],
                                  pg[:, KVQ:KW])
                    pk = pg[:, 0:KVQ].rearrange("p (g d) -> p g d", d=128)
                    t = kvt[:, 0:4 * NGK * 64].rearrange("p (i g d) -> p i g d", i=4, d=64)
                    ck3 = ck_sb[k % 2][:, :].rearrange("p (g d) -> p g d", d=64)
                    sk3 = sk_sb[k % 2][:, :].rearrange("p (g d) -> p g d", d=64)
                    ko = ks[:, :].rearrange("p (g d) -> p g d", d=128)
                    v.tensor_mul(t[:, 0], pk[:, :, 0:64], ck3)
                    v.tensor_mul(t[:, 1], pk[:, :, 64:128], sk3)
                    v.tensor_mul(t[:, 2], pk[:, :, 64:128], ck3)
                    m4 = v.tensor_mul(t[:, 3], pk[:, :, 0:64], sk3)
                    inc("dv", m4, 1)
                    v.wait_ge(sems["dv"], CNT["dv"])
                    v.tensor_sub(ko[:, :, 0:64], t[:, 0], t[:, 1])
                    a2 = v.tensor_add(ko[:, :, 64:128], t[:, 2], t[:, 3])
                    inc("dv", a2, 1)

        akv_dv_end = CNT["dv"]

        @block.gpsimd
        def _(g):
            for p in range(NKVP):
                for st in range(ST):
                    k = p * ST + st
                    g.wait_ge(sems["dv"], aq_dv_end + 2 * k + 2)
                    for gg in range(NGK):
                        gabs = p * NGK + gg
                        sfill_g("kstst", k % 2, g.dma_start(
                            out=ktm_scr[gabs, st * 128:(st + 1) * 128, :],
                            in_=kst[k % 2][:, gg * 128:(gg + 1) * 128]))
        kst_tok = [("kstst", i, CNT[("S", "kstst", i)]) for i in range(2)]

        @block.sync
        def _(sy):
            swait_g(sy, kst_tok[0])
            swait_g(sy, kst_tok[1])
            swait_g(sy, aq_ld0_tok)
            for gabs in range(NKV):
                sfill_g("ld0", 0,
                        sy.dma_start(out=kfm[:, gabs * S:(gabs + 1) * S],
                                     in_=ktm_scr[gabs], transpose=True))
        akv_kfm_tok = ("ld0", 0, CNT[("S", "ld0", 0)])

        esK.close()

        if kfm_dbg is not None or vtm_dbg is not None:
            @block.sync
            def _(sy):
                swait_g(sy, akv_kfm_tok)
                if kfm_dbg is not None:
                    sfill_g("ld0", 0, sy.dma_start(out=kfm_dbg, in_=kfm[:]))
                if vtm_dbg is not None:
                    sfill_g("ld0", 0, sy.dma_start(out=vtm_dbg, in_=vtm[:]))
                sy.wait_ge(STREAMS["ld0"][0], CNT[("S", "ld0", 0)])

        esB = ExitStack()
        qh_sb = esB.enter_context(nc.sbuf_tensor("qh_sb", [128, OWN], BF16))
        pt_sb = [esB.enter_context(nc.sbuf_tensor(f"pt{i}", [128, 1024], BF16))
                 for i in range(2)]
        avb_sb = [esB.enter_context(nc.sbuf_tensor(f"avb{i}", [128, 1024], BF16))
                  for i in range(2)]
        rb_sb = esB.enter_context(nc.sbuf_tensor("rb_sb", [128, 1024], F32))
        rl_sb = esB.enter_context(nc.sbuf_tensor("rl_sb", [1, 1024], F32))
        lr_sb = esB.enter_context(nc.sbuf_tensor("lr_sb", [1, 1024], F32))
        nrm_sb = esB.enter_context(nc.sbuf_tensor("nrm_sb", [128, max(OWN, S)],
                                                  BF16))
        mx_sb = esB.enter_context(nc.sbuf_tensor("mx_sb", [1, 64], F32))
        bias_sb = esB.enter_context(nc.sbuf_tensor("bias_sb", [128, 1], F32))
        mk2_sb = esB.enter_context(nc.sbuf_tensor("mk2_sb", [1, NKV + 1], F32))

        B_sT = [ps_a[:, 0:1024], ps_a[:, 1024:2048]]
        B_av = ps_b[:, 0:1024]
        B_l0 = ps_b[0:36, 1024:1536]
        B_r = ps_b[:, 1536:2048]

        def pair_tiles(a):
            return list(range(8 * a + 7, 8 * a - 1, -1))

        def lmax(a):
            return 2 * (8 * a + 7) + 2

        def ncols(a, kc):
            return 128 * sum(1 for t in pair_tiles(a) if 2 * t + 2 > kc)

        def mask_of(a, kc):
            t = kc // 2
            if t in pair_tiles(a):
                return pair_tiles(a).index(t), (mA if kc % 2 == 0 else mB)
            return None, None

        b_base = dict(CNT)
        EVT = EvtDict()

        def emit_B(recording, engines=None):
            EvtDict.recording = recording
            cnt = dict(b_base)

            class E:
                def wait(self2, eng_obj, kind, val):
                    if not recording and eng_obj is not None:
                        eng_obj.wait_ge(sems[kind], val)

                def bump(self2, kind, key=None, k=1):
                    cnt[kind] += k
                    if recording and key is not None:
                        EVT[key] = cnt[kind]
                    return cnt[kind]

                def sfill(self2, name, slot, instr, key=None, k=16):
                    skey = ("S", name, slot)
                    cnt[skey] += k
                    if instr is not None:
                        instr.then_inc(STREAMS[name][slot], k)
                    if recording and key is not None:
                        EVT[key] = (name, slot, cnt[skey])
                    return (name, slot, cnt[skey])

                def swait(self2, eng_obj, token):
                    if not recording and eng_obj is not None and token:
                        eng_obj.wait_ge(STREAMS[token[0]][token[1]], token[2])

            ev = E()

            tn = engines.get("tn") if engines else None
            vv = engines.get("v") if engines else None
            ss = engines.get("s") if engines else None
            sy = engines.get("sy") if engines else None
            gp = engines.get("g") if engines else None

            do_pe = recording or tn is not None
            do_dv = recording or vv is not None
            do_ac = recording or ss is not None
            do_sy = recording or sy is not None
            do_gp = recording or gp is not None

            if do_sy:
                for h in range(NH):
                    if h > 0:
                        ev.wait(sy, "pe", EVT.get(("pe_head_end", h - 1), 0)
                                if not recording else 0)
                    d = (sy.dma_start(out=qh_sb[:], in_=q_scr[h])
                         if not recording else None)
                    ev.sfill("qh", 0, d, ("ld_q", h))

            for g in range(NKV):
                kslice = kfm[:, g * S:(g + 1) * S]
                nround = NSEG_K // 4 if NSEG_K % 4 == 0 else (NSEG_K + 3) // 4
                for rnd in range(nround):
                    c0 = rnd * 2048
                    c1 = min(S, c0 + 2048)
                    if do_dv:
                        if g == 0 and rnd == 0:
                            ev.swait(vv, akv_kfm_tok)
                        ev.wait(vv, "ac", EVT.get(("acg", g, rnd), 0))
                        ev.wait(vv, "pe", EVT.get(("peg_prev", g, rnd), 0))
                        if not recording:
                            d = vv.tensor_mul(nrm_sb[:, 0:c1 - c0],
                                              kslice[:, c0:c1], kslice[:, c0:c1])
                            d.then_inc(sems["dv"], 1)
                        ev.bump("dv", ("dv_ksq", g, rnd))
                    if do_pe:
                        ev.wait(tn, "dv", EVT[("dv_ksq", g, rnd)])
                        for sseg in range((c1 - c0) // 512):
                            if not recording:
                                mm = tn.matmul(ps_a[0:4, sseg * 512:(sseg + 1) * 512],
                                               ob_sb[:, 0:4],
                                               nrm_sb[:, sseg * 512:(sseg + 1) * 512],
                                               start=True, stop=True)
                                mm.then_inc(sems["pe"], 1)
                            ev.bump("pe", ("pe_knorm", g, rnd, sseg))
                    if do_dv:
                        ev.wait(vv, "pe", EVT[("pe_knorm", g, rnd,
                                               (c1 - c0) // 512 - 1)])
                        if not recording:
                            d = vv.reduce_max(mx_sb[0:1, 8:9],
                                              ps_a[0:1, 0:c1 - c0], axis=AX.X)
                            d.then_inc(sems["dv"], 1)
                        ev.bump("dv", ("dv_kred", g, rnd))
                        if not recording:
                            vv.wait_ge(sems["dv"], cnt["dv"])
                            if rnd == 0:
                                d = vv.tensor_copy(mk2_sb[0:1, g:g + 1],
                                                   mx_sb[0:1, 8:9])
                            else:
                                d = vv.tensor_max(mk2_sb[0:1, g:g + 1],
                                                  mk2_sb[0:1, g:g + 1],
                                                  mx_sb[0:1, 8:9])
                            d.then_inc(sems["dv"], 1)
                        ev.bump("dv", ("dv_kcmb", g, rnd))

                for hh in range(GRP):
                    h = g * GRP + hh
                    if do_dv:
                        ev.swait(vv, EVT[("ld_q", h)])
                        ev.wait(vv, "pe", EVT.get(("pe_qnorm_prev", h), 0))
                        ev.wait(vv, "ac", EVT.get(("ac_head_end", h - 1), 0))
                        if not recording:
                            d = vv.tensor_mul(nrm_sb[:, 0:OWN], qh_sb[:], qh_sb[:])
                            d.then_inc(sems["dv"], 1)
                        ev.bump("dv", ("dv_qsq", h))
                    if do_pe:
                        ev.wait(tn, "dv", EVT[("dv_qsq", h)])
                        ev.wait(tn, "ac", EVT.get(("ac_head_end", h - 1), 0))
                        for sseg in range(OWN // 512):
                            if not recording:
                                mm = tn.matmul(ps_a[0:4, sseg * 512:(sseg + 1) * 512],
                                               ob_sb[:, 0:4],
                                               nrm_sb[:, sseg * 512:(sseg + 1) * 512],
                                               start=True, stop=True)
                                mm.then_inc(sems["pe"], 1)
                            ev.bump("pe", ("pe_qnorm", h, sseg))
                        if recording:
                            EVT[("pe_qnorm_prev", h + 1)] = cnt["pe"]
                    if do_dv:
                        ev.wait(vv, "pe", EVT[("pe_qnorm", h, OWN // 512 - 1)])
                        if not recording:
                            d = vv.reduce_max(mx_sb[0:1, 0:1], ps_a[0:1, 0:OWN],
                                              axis=AX.X)
                            d.then_inc(sems["dv"], 1)
                            vv.wait_ge(sems["dv"], cnt["dv"] + 1)
                            d = vv.tensor_add(mx_sb[0:1, 1:2], mx_sb[0:1, 0:1],
                                              mk2_sb[0:1, g:g + 1])
                            d.then_inc(sems["dv"], 1)
                            vv.wait_ge(sems["dv"], cnt["dv"] + 2)
                            d = vv.tensor_scalar_mul(mx_sb[0:1, 2:3],
                                                     mx_sb[0:1, 1:2],
                                                     -0.5 * ISD)
                            d.then_inc(sems["dv"], 1)
                        ev.bump("dv")
                        ev.bump("dv")
                        ev.bump("dv", ("dv_bval", h))
                    if do_pe:
                        ev.wait(tn, "dv", EVT[("dv_bval", h)])
                        if not recording:
                            mm = tn.matmul(B_r[:, 0:1], of_sb[:, :],
                                           mx_sb[0:1, 2:3], start=True, stop=True)
                            mm.then_inc(sems["pe"], 1)
                        ev.bump("pe", ("pe_bcast", h))
                    if do_dv:
                        ev.wait(vv, "pe", EVT[("pe_bcast", h)])
                        if not recording:
                            d = vv.tensor_copy(bias_sb[:], B_r[:, 0:1])
                            d.then_inc(sems["dv"], 1)
                        ev.bump("dv", ("dv_bias", h))

                    for a in range(NPAIR):
                        L = lmax(a)

                        def emit_avl(kc):
                            N = ncols(a, kc)
                            pos, mk = mask_of(a, kc)
                            if do_pe:
                                if mk is None:
                                    ev.wait(tn, "ac", EVT[("ac_exp", h, a, kc)])
                                else:
                                    ev.wait(tn, "dv", EVT[("dv_mask", h, a, kc)])
                                if kc == 0:
                                    ev.wait(tn, "dv",
                                            EVT.get(("dv_avev", h, a - 1),
                                                    EVT.get(("dv_avev_last", h - 1),
                                                            0)))
                                first = (kc == 0)
                                last = (kc == L - 1)
                                for w0 in range(0, N, 512):
                                    w1 = min(N, w0 + 512)
                                    if not recording:
                                        tn.matmul(
                                            B_av[:, w0:w1],
                                            vtm[:, kc * KVW + g * 128:
                                                kc * KVW + (g + 1) * 128],
                                            pt_sb[kc % 2][:, w0:w1],
                                            start=first, stop=last,
                                            skip_group_check=True)
                                        lrow = (B_l0[0:4, 0:w1 - w0] if w0 == 0
                                                else B_l0[32:36, 0:w1 - w0])
                                        mm = tn.matmul(
                                            lrow, ob_sb[:, 0:4],
                                            pt_sb[kc % 2][:, w0:w1],
                                            start=first, stop=last,
                                            skip_group_check=True)
                                        mm.then_inc(sems["pe"], 1)
                                    ev.bump("pe", ("pe_avl", h, a, kc, w0))
                                if recording:
                                    EVT[("pe_avl_last", h, a, kc)] = cnt["pe"]

                        for kc in range(L):
                            N = ncols(a, kc)
                            if do_pe:
                                if kc >= 2:
                                    ev.wait(tn, "ac", EVT[("ac_exp", h, a, kc - 2)])
                                elif a > 0:
                                    ev.wait(tn, "ac", EVT[("ac_exp", h, a - 1,
                                                           lmax(a - 1) - 1)])
                                for w0 in range(0, N, 512):
                                    w1 = min(N, w0 + 512)
                                    if not recording:
                                        mm = tn.matmul(
                                            B_sT[kc % 2][:, w0:w1],
                                            kfm[:, g * S + kc * 128:
                                                g * S + (kc + 1) * 128],
                                            qh_sb[:, a * 1024 + w0:a * 1024 + w1],
                                            start=True, stop=True)
                                        mm.then_inc(sems["pe"], 1)
                                    ev.bump("pe", ("pe_sc", h, a, kc, w0))
                                if recording:
                                    EVT[("pe_sc_last", h, a, kc)] = cnt["pe"]
                            if do_ac:
                                ev.wait(ss, "pe", EVT[("pe_sc_last", h, a, kc)])
                                if kc >= 2:
                                    ev.wait(ss, "pe", EVT[("pe_avl_last", h, a,
                                                           kc - 2)])
                                elif a > 0:
                                    ev.wait(ss, "pe", EVT[("pe_avl_last", h, a - 1,
                                                           lmax(a - 1) - 1)])
                                elif h > 0:
                                    ev.wait(ss, "pe", EVT[("pe_avl_last", h - 1,
                                                           NPAIR - 1,
                                                           lmax(NPAIR - 1) - 1)])
                                ev.wait(ss, "dv", EVT[("dv_bias", h)])
                                if not recording:
                                    d = ss.activation(pt_sb[kc % 2][:, 0:N],
                                                      B_sT[kc % 2][:, 0:N],
                                                      AF.Exp, bias=bias_sb[:],
                                                      scale=ISD)
                                    d.then_inc(sems["ac"], 1)
                                ev.bump("ac", ("ac_exp", h, a, kc))
                            pos, mk = mask_of(a, kc)
                            if pos is not None and do_dv:
                                ev.wait(vv, "ac", EVT[("ac_exp", h, a, kc)])
                                if not recording:
                                    d = vv.tensor_mul(
                                        pt_sb[kc % 2][:, pos * 128:(pos + 1) * 128],
                                        pt_sb[kc % 2][:, pos * 128:(pos + 1) * 128],
                                        mk[:])
                                    d.then_inc(sems["dv"], 1)
                                ev.bump("dv", ("dv_mask", h, a, kc))
                            if kc >= 1:
                                emit_avl(kc - 1)
                        emit_avl(L - 1)

                        if do_ac:
                            ev.wait(ss, "pe", EVT[("pe_avl_last", h, a, L - 1)])
                            if not recording:
                                d = ss.activation(lr_sb[0:1, 0:512],
                                                  B_l0[0:1, 0:512], AF.Ln)
                                d.then_inc(sems["ac"], 1)
                            ev.bump("ac", ("ac_ln0", h, a))
                            if not recording:
                                d = ss.activation(lr_sb[0:1, 512:1024],
                                                  B_l0[32:33, 0:512], AF.Ln)
                                d.then_inc(sems["ac"], 1)
                            ev.bump("ac", ("ac_ln1", h, a))
                            if not recording:
                                ss.wait_ge(sems["ac"], cnt["ac"])
                                d = ss.activation(rl_sb[0:1, 0:1024],
                                                  lr_sb[0:1, 0:1024], AF.Exp,
                                                  scale=-1.0)
                                d.then_inc(sems["ac"], 1)
                            ev.bump("ac", ("ac_rinv", h, a))
                        if do_pe:
                            ev.wait(tn, "ac", EVT[("ac_rinv", h, a)])
                            ev.wait(tn, "dv", EVT.get(("dv_bias", h), 0))
                            for seg in range(2):
                                if seg == 1:
                                    ev.wait(tn, "dv", EVT[("dv_rcp", h, a, 0)])
                                if not recording:
                                    mm = tn.matmul(
                                        B_r[:, 0:512], of_sb[:, :],
                                        rl_sb[0:1, seg * 512:(seg + 1) * 512],
                                        start=True, stop=True)
                                    mm.then_inc(sems["pe"], 1)
                                ev.bump("pe", ("pe_rb", h, a, seg))
                        if do_dv:
                            for seg in range(2):
                                ev.wait(vv, "pe", EVT[("pe_rb", h, a, seg)])
                                if not recording:
                                    d = vv.tensor_copy(
                                        rb_sb[:, seg * 512:(seg + 1) * 512],
                                        B_r[:, 0:512])
                                    d.then_inc(sems["dv"], 1)
                                ev.bump("dv", ("dv_rcp", h, a, seg))
                            gpair = h * NPAIR + a
                            ev.swait(vv, EVT.get(("gp_av", gpair - 2), None))
                            if not recording:
                                vv.wait_ge(sems["dv"], cnt["dv"])
                                d = vv.tensor_mul(avb_sb[gpair % 2][:],
                                                  B_av[:, :], rb_sb[:])
                                d.then_inc(sems["dv"], 1)
                            ev.bump("dv", ("dv_avev", h, a))
                            if recording and a == NPAIR - 1:
                                EVT[("dv_avev_last", h)] = cnt["dv"]
                        if do_gp:
                            gpair = h * NPAIR + a
                            ev.wait(gp, "dv", EVT[("dv_avev", h, a)])
                            d = (gp.dma_start(
                                out=av_scr[h, :, a * 1024:(a + 1) * 1024],
                                in_=avb_sb[gpair % 2][:])
                                if not recording else None)
                            ev.sfill("avst", gpair % 2, d, ("gp_av", gpair))
                    if recording:
                        EVT[("pe_head_end", h)] = cnt["pe"]
                        EVT[("ac_head_end", h)] = cnt["ac"]
                if recording:
                    EVT[("acg", g + 1, 0)] = cnt["ac"]
                    for rnd in range(1, 8):
                        EVT[("acg", g + 1, rnd)] = cnt["ac"]
                    EVT[("peg_prev", g + 1, 0)] = cnt["pe"]
                    for rnd in range(1, 8):
                        EVT[("peg_prev", g + 1, rnd)] = cnt["pe"]
            return cnt

        b_final = emit_B(True)

        @block.sync
        def _(sy):
            emit_B(False, {"sy": sy})

        @block.tensor
        def _(tn):
            emit_B(False, {"tn": tn})

        @block.scalar
        def _(ss):
            emit_B(False, {"s": ss})

        @block.vector
        def _(vv):
            emit_B(False, {"v": vv})

        @block.gpsimd
        def _(gp):
            emit_B(False, {"g": gp})

        for k in CNT:
            CNT[k] = b_final[k]
        b_pe_end = CNT["pe"]
        b_dv_end = CNT["dv"]

        b_ac_end = CNT["ac"]

        esB.close()
        esKV.close()

        mb1_sb = SB("mb1_sb", [128, OWN], F32)
        rb1_sb = SB("rb1_sb", [128, OWN], F32)
        esC = ExitStack()
        avh = esC.enter_context(nc.sbuf_tensor("avh", [128, OC * 1024], BF16))
        wob = [esC.enter_context(nc.sbuf_tensor(f"wob{i}", [128, OC * 128], BF16))
               for i in range(2)]
        xtt = [esC.enter_context(nc.sbuf_tensor(f"xtt{i}", [128, 512], F32))
               for i in range(4)]
        y1t = [esC.enter_context(nc.sbuf_tensor(f"y1t{i}", [128, 512], F32))
               for i in range(4)]
        ybf = [esC.enter_context(nc.sbuf_tensor(f"ybf{i}", [128, 512], BF16))
               for i in range(2)]
        ysq = [esC.enter_context(nc.sbuf_tensor(f"ysq{i}", [128, 512], BF16))
               for i in range(2)]
        mrow = esC.enter_context(nc.sbuf_tensor("mrow", [1, 1024], F32))
        vrow = esC.enter_context(nc.sbuf_tensor("vrow", [1, 1024], F32))
        rrow = esC.enter_context(nc.sbuf_tensor("rrow", [1, 1024], F32))

        def emit_ln_block(qh, scr_y, mb_sb_, rb_sb_, wsrc, which):
            pass

        c_base = dict(CNT)
        CEVT = EvtDict()

        def emit_C(recording, engines=None):
            EvtDict.recording = recording
            cnt = dict(c_base)

            def bump(kind, key=None, k=1):
                cnt[kind] += k
                if recording and key is not None:
                    CEVT[key] = cnt[kind]
                return cnt[kind]

            def wait(eng_obj, kind, val):
                if not recording and eng_obj is not None:
                    eng_obj.wait_ge(sems[kind], val)

            def sfill(name, slot, instr, key=None, k=16):
                skey = ("S", name, slot)
                cnt[skey] += k
                if instr is not None:
                    instr.then_inc(STREAMS[name][slot], k)
                if recording and key is not None:
                    CEVT[key] = (name, slot, cnt[skey])
                return (name, slot, cnt[skey])

            def swait(eng_obj, token):
                if not recording and eng_obj is not None and token:
                    eng_obj.wait_ge(STREAMS[token[0]][token[1]], token[2])

            tn = engines.get("tn") if engines else None
            vv = engines.get("v") if engines else None
            ss = engines.get("s") if engines else None
            sy = engines.get("sy") if engines else None
            gp = engines.get("g") if engines else None
            do_pe = recording or tn is not None
            do_dv = recording or vv is not None
            do_ac = recording or ss is not None
            do_sy = recording or sy is not None
            do_gp = recording or gp is not None

            for qh in range(NQH):
                if do_sy:
                    if qh == 0:
                        wait(sy, "cast", cast_done["wo_bf"])
                        swait(sy, EVT[("gp_av", NH * NPAIR - 1)])
                        swait(sy, EVT[("gp_av", NH * NPAIR - 2)])
                        wait(sy, "pe", b_pe_end)
                    else:
                        wait(sy, "pe", CEVT[("pe_proj", qh - 1, HC - 1, 1)])
                    for oc in range(OC):
                        d = (sy.dma_start(
                            out=avh[:, oc * 1024:(oc + 1) * 1024],
                            in_=av_scr[oc, :, qh * 1024:(qh + 1) * 1024])
                            if not recording else None)
                        sfill("avh", 0, d, ("ld_av", qh, oc))
                    for hc in range(HC):
                        if hc >= 2:
                            wait(sy, "pe", CEVT[("pe_proj", qh, hc - 2, 1)])
                        d = (sy.dma_start(out=wob[hc % 2][:], in_=wo_bf[hc])
                             if not recording else None)
                        sfill("wo", hc % 2, d, ("ld_wo", qh, hc))
                        for qg in range(2):
                            ti = (hc * 2 + qg) % 4
                            jg = qh * HC * 2 + hc * 2 + qg
                            if jg >= 4:
                                swait(sy, CEVT[("gp_y1j", jg - 4)])
                            d = (sy.dma_start(
                                out=xtt[ti][:],
                                in_=xt_own[:, hc * OWN + qh * 1024 + qg * 512:
                                           hc * OWN + qh * 1024 + qg * 512 + 512])
                                if not recording else None)
                            sfill("xtt", ti, d, ("ld_xt", qh, hc, qg))

                def emit_stats(hc):
                    if not do_pe:
                        return
                    for qg in range(2):
                        wait(tn, "dv", CEVT[("dv_ysq", qh, hc, qg)])
                        for which in range(2):
                            src = (ybf, ysq)[which][(hc * 2 + qg) % 2]
                            rows = (ps_b[0:4, qg * 512:(qg + 1) * 512] if which == 0
                                    else ps_b[32:36, qg * 512:(qg + 1) * 512])
                            if not recording:
                                mm = tn.matmul(rows, ob_sb[:, 0:4], src[:],
                                               start=(hc == 0), stop=(hc == HC - 1))
                                mm.then_inc(sems["pe"], 1)
                            bump("pe", ("pe_st", qh, hc, qg, which))

                if do_pe:
                    for hc in range(HC):
                        swait(tn, CEVT[("ld_wo", qh, hc)])
                        if hc == 0:
                            swait(tn, CEVT[("ld_av", qh, OC - 1)])
                        if hc >= 2:
                            wait(tn, "dv", CEVT[("dv_evac", qh, hc - 2, 1)])
                        elif qh >= 1:
                            wait(tn, "dv", CEVT[("dv_lnb", qh - 1, 1, 1)])
                        for qg in range(2):
                            slot = (hc % 2) * 2 + qg
                            mm = None
                            for oc in range(OC):
                                if not recording:
                                    mm = tn.matmul(
                                        ps_a[:, slot * 512:(slot + 1) * 512],
                                        wob[hc % 2][:, oc * 128:(oc + 1) * 128],
                                        avh[:, oc * 1024 + qg * 512:
                                            oc * 1024 + qg * 512 + 512],
                                        start=(oc == 0), stop=(oc == OC - 1))
                            if not recording:
                                mm.then_inc(sems["pe"], 1)
                            bump("pe", ("pe_proj", qh, hc, qg))
                        if hc >= 1:
                            emit_stats(hc - 1)
                    emit_stats(HC - 1)
                    wait(tn, "dv", CEVT[("dv_lnrow", qh)])
                    for seg in range(2):
                        for which in range(2):
                            src = (mrow, rrow)[which]
                            if not recording:
                                mm = tn.matmul(
                                    ps_a[:, (seg * 2 + which) * 512:
                                         (seg * 2 + which + 1) * 512],
                                    of_sb[:, :],
                                    src[0:1, seg * 512:(seg + 1) * 512],
                                    start=True, stop=True)
                                mm.then_inc(sems["pe"], 1)
                            bump("pe", ("pe_lnb", qh, seg, which))

                if do_dv:
                    for hc in range(HC):
                        for qg in range(2):
                            ti = (hc * 2 + qg) % 4
                            wait(vv, "pe", CEVT[("pe_proj", qh, hc, qg)])
                            swait(vv, CEVT[("ld_xt", qh, hc, qg)])
                            if hc * 2 + qg >= 2:
                                wait(vv, "pe", CEVT.get(("pe_st", qh, hc - 1, qg, 1),
                                                        0))
                            slot = (hc % 2) * 2 + qg
                            if not recording:
                                d = vv.tensor_add(y1t[ti][:],
                                                  ps_a[:, slot * 512:(slot + 1) * 512],
                                                  xtt[ti][:])
                                d.then_inc(sems["dv"], 1)
                            bump("dv", ("dv_evac", qh, hc, qg))
                            if not recording:
                                vv.wait_ge(sems["dv"], cnt["dv"])
                                d = vv.tensor_copy(ybf[(hc * 2 + qg) % 2][:],
                                                   y1t[ti][:])
                                d.then_inc(sems["dv"], 1)
                            bump("dv", ("dv_ybf", qh, hc, qg))
                            if not recording:
                                vv.wait_ge(sems["dv"], cnt["dv"])
                                d = vv.tensor_mul(ysq[(hc * 2 + qg) % 2][:],
                                                  ybf[(hc * 2 + qg) % 2][:],
                                                  ybf[(hc * 2 + qg) % 2][:])
                                d.then_inc(sems["dv"], 1)
                            bump("dv", ("dv_ysq", qh, hc, qg))
                    wait(vv, "pe", CEVT[("pe_st", qh, HC - 1, 1, 1)])
                    if not recording:
                        d = vv.tensor_scalar_mul(mrow[0:1, :], ps_b[0:1, 0:1024],
                                                 1.0 / D)
                        d.then_inc(sems["dv"], 1)
                        vv.wait_ge(sems["dv"], cnt["dv"] + 1)
                        d = vv.tensor_scalar_mul(vrow[0:1, :], ps_b[32:33, 0:1024],
                                                 1.0 / D)
                        d.then_inc(sems["dv"], 1)
                        vv.wait_ge(sems["dv"], cnt["dv"] + 2)
                        d = vv.tensor_mul(rrow[0:1, :], mrow[0:1, :], mrow[0:1, :])
                        d.then_inc(sems["dv"], 1)
                        vv.wait_ge(sems["dv"], cnt["dv"] + 3)
                        d = vv.tensor_sub(vrow[0:1, :], vrow[0:1, :], rrow[0:1, :])
                        d.then_inc(sems["dv"], 1)
                    for _ in range(4):
                        bump("dv")
                    if recording:
                        CEVT[("dv_var", qh)] = cnt["dv"]
                if do_ac:
                    wait(ss, "dv", CEVT[("dv_var", qh)])
                    if not recording:
                        d = ss.activation(rrow[0:1, :], vrow[0:1, :], AF.Ln,
                                          bias=eps_sb[0:1, :])
                        d.then_inc(sems["ac"], 1)
                        ss.wait_ge(sems["ac"], cnt["ac"] + 1)
                        d = ss.activation(rrow[0:1, :], rrow[0:1, :], AF.Exp,
                                          scale=-0.5)
                        d.then_inc(sems["ac"], 1)
                    bump("ac")
                    bump("ac", ("ac_rstd", qh))
                if do_dv:
                    wait(vv, "ac", CEVT[("ac_rstd", qh)])
                    if recording:
                        CEVT[("dv_lnrow", qh)] = cnt["dv"]
                    if not recording:
                        d = vv.tensor_copy(dvm_sb[0:1, 0:1], rrow[0:1, 0:1])
                        d.then_inc(sems["dv"], 1)
                    bump("dv", ("dv_lnrow", qh))
                    for seg in range(2):
                        for which in range(2):
                            wait(vv, "pe", CEVT[("pe_lnb", qh, seg, which)])
                            dstb = (mb1_sb, rb1_sb)[which]
                            if not recording:
                                d = vv.tensor_copy(
                                    dstb[:, qh * 1024 + seg * 512:
                                         qh * 1024 + (seg + 1) * 512],
                                    ps_a[:, (seg * 2 + which) * 512:
                                         (seg * 2 + which + 1) * 512])
                                d.then_inc(sems["dv"], 1)
                            bump("dv", ("dv_lnb", qh, seg, which))

                if do_gp:
                    for hc in range(HC):
                        for qg in range(2):
                            ti = (hc * 2 + qg) % 4
                            wait(gp, "dv", CEVT[("dv_ybf", qh, hc, qg)])
                            d = (gp.dma_start(
                                out=y1_scr[:, hc * OWN + qh * 1024 + qg * 512:
                                           hc * OWN + qh * 1024 + qg * 512 + 512],
                                in_=y1t[ti][:])
                                if not recording else None)
                            tok = sfill("y1st", ti, d, ("gp_y1", qh, hc, qg))
                            if recording:
                                CEVT[("gp_y1j", qh * HC * 2 + hc * 2 + qg)] = tok
            return cnt

        c_final = emit_C(True)

        @block.sync
        def _(sy):
            emit_C(False, {"sy": sy})

        @block.tensor
        def _(tn):
            emit_C(False, {"tn": tn})

        @block.scalar
        def _(ss):
            emit_C(False, {"s": ss})

        @block.vector
        def _(vv):
            emit_C(False, {"v": vv})

        @block.gpsimd
        def _(gp):
            emit_C(False, {"g": gp})

        for k in CNT:
            CNT[k] = c_final[k]
        c_pe_end = CNT["pe"]
        c_dv_end = CNT["dv"]


        if ln1_dbg is not None:
            @block.sync
            def _(sy):
                sy.wait_ge(sems["dv"], c_dv_end)
                sfill_g("ld0", 0, sy.dma_start(out=ln1_dbg[:, 0:OWN], in_=mb1_sb[:]))
                sfill_g("ld0", 0, sy.dma_start(out=ln1_dbg[:, OWN:2 * OWN], in_=rb1_sb[:]))

        esC.close()

        esD = ExitStack()
        x1b = esD.enter_context(nc.sbuf_tensor("x1b", [128, HC * OWN], BF16))
        ytmp = [esD.enter_context(nc.sbuf_tensor(f"ytmp{i}", [128, OWN], F32))
                for i in range(2)]
        wgb = [esD.enter_context(nc.sbuf_tensor(f"wgb{i}", [128, HC * 128], BF16))
               for i in range(2)]
        wub = [esD.enter_context(nc.sbuf_tensor(f"wub{i}", [128, HC * 128], BF16))
               for i in range(2)]
        hsil = [esD.enter_context(nc.sbuf_tensor(f"hsil{i}", [128, 1024], BF16))
                for i in range(2)]
        hup = [esD.enter_context(nc.sbuf_tensor(f"hup{i}", [128, 1024], BF16))
               for i in range(2)]
        hout = [esD.enter_context(nc.sbuf_tensor(f"hout{i}", [128, 1024], BF16))
                for i in range(2)]

        d_base = dict(CNT)
        DEVT = EvtDict()

        def emit_D(recording, engines=None):
            EvtDict.recording = recording
            cnt = dict(d_base)

            def bump(kind, key=None, k=1):
                cnt[kind] += k
                if recording and key is not None:
                    DEVT[key] = cnt[kind]
                return cnt[kind]

            def wait(eng_obj, kind, val):
                if not recording and eng_obj is not None:
                    eng_obj.wait_ge(sems[kind], val)

            def sfill(name, slot, instr, key=None, k=16):
                skey = ("S", name, slot)
                cnt[skey] += k
                if instr is not None:
                    instr.then_inc(STREAMS[name][slot], k)
                if recording and key is not None:
                    DEVT[key] = (name, slot, cnt[skey])
                return (name, slot, cnt[skey])

            def swait(eng_obj, token):
                if not recording and eng_obj is not None and token:
                    eng_obj.wait_ge(STREAMS[token[0]][token[1]], token[2])

            tn = engines.get("tn") if engines else None
            vv = engines.get("v") if engines else None
            ss = engines.get("s") if engines else None
            sy = engines.get("sy") if engines else None
            gp = engines.get("g") if engines else None
            do_pe = recording or tn is not None
            do_dv = recording or vv is not None
            do_ac = recording or ss is not None
            do_sy = recording or sy is not None
            do_gp = recording or gp is not None

            if do_sy:
                swait(sy, CEVT[("gp_y1", NQH - 1, HC - 1, 0)])
                swait(sy, CEVT[("gp_y1", NQH - 1, HC - 1, 1)])
                swait(sy, CEVT[("gp_y1", NQH - 1, HC - 2, 0)])
                swait(sy, CEVT[("gp_y1", NQH - 1, HC - 2, 1)])
                wait(sy, "pe", c_pe_end)
                for hc in range(HC):
                    if hc >= 2:
                        wait(sy, "dv", DEVT[("dv_x1", hc - 2)])
                    d = (sy.dma_start(out=ytmp[hc % 2][:],
                                      in_=y1_scr[:, hc * OWN:(hc + 1) * OWN])
                         if not recording else None)
                    sfill("y1d", hc % 2, d, ("ld_y1", hc))
            if do_dv:
                for hc in range(HC):
                    swait(vv, DEVT[("ld_y1", hc)])
                    if hc == 0:
                        wait(vv, "dv", CEVT[("dv_lnb", NQH - 1, 1, 1)])
                    if not recording:
                        if hc >= 1:
                            vv.wait_ge(sems["dv"], cnt["dv"])
                        d = vv.tensor_sub(ytmp[hc % 2][:], ytmp[hc % 2][:],
                                          mb1_sb[:])
                        d.then_inc(sems["dv"], 1)
                    bump("dv")
                    if not recording:
                        vv.wait_ge(sems["dv"], cnt["dv"])
                        d = vv.tensor_mul(x1b[:, hc * OWN:(hc + 1) * OWN],
                                          ytmp[hc % 2][:], rb1_sb[:])
                        d.then_inc(sems["dv"], 1)
                    bump("dv", ("dv_x1", hc))
            if recording:
                DEVT["dv_x1_end"] = cnt["dv"]

            if do_sy:
                wait(sy, "cast", cast_done["wu_bf"])
                for ib in range(IC):
                    if ib >= 2:
                        wait(sy, "pe", DEVT[("pe_blk", ib - 2)])
                    d = (sy.dma_start(out=wgb[ib % 2][:], in_=wg_bf[ib])
                         if not recording else None)
                    sfill("wgu", ib % 2, d)
                    d = (sy.dma_start(out=wub[ib % 2][:], in_=wu_bf[ib])
                         if not recording else None)
                    sfill("wgu", ib % 2, d, ("ld_wu", ib))

            if do_pe:
                def d_block(ib, wv, ev):
                    if wv is None:
                        swait(tn, DEVT[("ld_wu", ib)])
                        if ib == 0:
                            wait(tn, "dv", DEVT["dv_x1_end"])
                    else:
                        if ib % 2 == 0:
                            tn.wait_ge(STREAMS["wgu"][0], wv)
                        else:
                            tn.wait_ge(STREAMS["wgu"][1], wv)
                            nc.regs_alu(wv, wv, 32, op=ALU.add)
                    for qh2 in range(NQH):
                        it = NQH * ib + qh2
                        par = it % 2
                        pg = (ps_a, ps_b)[par]
                        if ev is None:
                            if it >= 2:
                                wait(tn, "dv", DEVT[("dv_hact", it - 2)])
                        else:
                            tn.wait_ge(sems["dv"], ev)
                            nc.regs_alu(ev, ev, 2, op=ALU.add)
                        mm = None
                        for hc in range(HC):
                            for qg in range(2):
                                if not recording:
                                    mm = tn.matmul(
                                        pg[:, qg * 512:(qg + 1) * 512],
                                        wgb[ib % 2][:, hc * 128:(hc + 1) * 128],
                                        x1b[:, hc * OWN + qh2 * 1024 + qg * 512:
                                            hc * OWN + qh2 * 1024 + qg * 512 + 512],
                                        start=(hc == 0), stop=(hc == HC - 1))
                        for hc in range(HC):
                            for qg in range(2):
                                if not recording:
                                    mm = tn.matmul(
                                        pg[:, 1024 + qg * 512:1024 + (qg + 1) * 512],
                                        wub[ib % 2][:, hc * 128:(hc + 1) * 128],
                                        x1b[:, hc * OWN + qh2 * 1024 + qg * 512:
                                            hc * OWN + qh2 * 1024 + qg * 512 + 512],
                                        start=(hc == 0), stop=(hc == HC - 1))
                        if not recording:
                            mm.then_inc(sems["pe"], 1)
                        bump("pe", ("pe_dit", it))
                    if recording:
                        DEVT[("pe_blk", ib)] = cnt["pe"]

                d_block(0, None, None)
                if IC > 1:
                    d_block(1, None, None)
                npr = max(0, (IC - 2) // 2)
                if recording:
                    for ib in range(2, IC):
                        d_block(ib, None, None)
                else:
                    if npr > 0:
                        wv = nc.alloc_registers("d_wv", engines=[PE])
                        ev = nc.alloc_registers("d_ev", engines=[PE])
                        nc.regs_mov(wv, DEVT[("ld_wu", 2)][2])
                        nc.regs_mov(ev, DEVT[("dv_hact", 2 * NQH - 2)])
                        with kfori(nc, npr, PE):
                            d_block(2, wv, ev)
                            d_block(3, wv, ev)
                        for r in (wv, ev):
                            for h in r:
                                nc.free_register(h)
                    for ib in range(2 + 2 * npr, IC):
                        d_block(ib, None, None)

            for ib in range(IC):
                for qh2 in range(NQH):
                    it = NQH * ib + qh2
                    par = it % 2
                    pg = (ps_a, ps_b)[par]
                    if do_ac:
                        wait(ss, "pe", DEVT[("pe_dit", it)])
                        if it >= 2:
                            wait(ss, "dv", DEVT[("dv_hup", it - 2)])
                        if not recording:
                            d = ss.activation(hsil[par][:], pg[:, 0:1024], AF.Silu,
                                              scale=sg_sb[:, ib:ib + 1])
                            d.then_inc(sems["ac"], 1)
                        bump("ac", ("ac_sil", it))
                    if do_dv:
                        wait(vv, "pe", DEVT[("pe_dit", it)])
                        if it >= 2:
                            swait(vv, DEVT[("gp_h", it - 2)] if it >= 2 else None)
                        if not recording:
                            d = vv.tensor_scalar_mul(hup[par][:], pg[:, 1024:2048],
                                                     su_sb[:, ib:ib + 1])
                            d.then_inc(sems["dv"], 1)
                        bump("dv", ("dv_hup", it))
                        wait(vv, "ac", DEVT[("ac_sil", it)])
                        if not recording:
                            vv.wait_ge(sems["dv"], cnt["dv"])
                            d = vv.tensor_mul(hout[it % 2][:], hsil[par][:],
                                              hup[par][:])
                            d.then_inc(sems["dv"], 1)
                        bump("dv", ("dv_hact", it))
                    if do_gp:
                        wait(gp, "dv", DEVT[("dv_hact", it)])
                        d = (gp.dma_start(
                            out=hact_scr[ib, :, qh2 * 1024:(qh2 + 1) * 1024],
                            in_=hout[it % 2][:])
                            if not recording else None)
                        sfill("hst", it % 2, d, ("gp_h", it))
            return cnt

        d_final = emit_D(True)

        @block.sync
        def _(sy):
            emit_D(False, {"sy": sy})

        @block.tensor
        def _(tn):
            emit_D(False, {"tn": tn})

        @block.scalar
        def _(ss):
            emit_D(False, {"s": ss})

        @block.vector
        def _(vv):
            emit_D(False, {"v": vv})

        @block.gpsimd
        def _(gp):
            emit_D(False, {"g": gp})

        for k in CNT:
            CNT[k] = d_final[k]
        d_pe_end = CNT["pe"]


        esD.close()

        mb2_sb = SB("mb2_sb", [128, OWN], F32)
        rb2_sb = SB("rb2_sb", [128, OWN], F32)
        esE = ExitStack()
        wdb = [esE.enter_context(nc.sbuf_tensor(f"wdb{i}", [128, IC * 256], BF16))
               for i in range(2)]
        hat = [esE.enter_context(nc.sbuf_tensor(f"hat{i}", [128, 1024], BF16))
               for i in range(4)]
        y1e = [esE.enter_context(nc.sbuf_tensor(f"y1e{i}", [128, 512], F32))
               for i in range(4)]
        x1e = esE.enter_context(nc.sbuf_tensor("x1e", [128, 512], F32))
        y2t = [esE.enter_context(nc.sbuf_tensor(f"y2t{i}", [128, 512], F32))
               for i in range(4)]
        y2b = [esE.enter_context(nc.sbuf_tensor(f"y2b{i}", [128, 512], BF16))
               for i in range(4)]
        y2s = [esE.enter_context(nc.sbuf_tensor(f"y2s{i}", [128, 512], BF16))
               for i in range(4)]
        mrow2 = esE.enter_context(nc.sbuf_tensor("mrow2", [1, 1024], F32))
        vrow2 = esE.enter_context(nc.sbuf_tensor("vrow2", [1, 1024], F32))
        rrow2 = esE.enter_context(nc.sbuf_tensor("rrow2", [1, 1024], F32))

        hat = [esE.enter_context(nc.sbuf_tensor(f"hatg{i}", [128, 4 * 1024], BF16))
               for i in range(2)]
        x2f = [esE.enter_context(nc.sbuf_tensor(f"x2f{i}", [128, OWN], F32))
               for i in range(2)]
        ostg = [esE.enter_context(nc.sbuf_tensor(f"ostg{i}", [128, 128], F32))
                for i in range(2)]
        macc = esE.enter_context(nc.sbuf_tensor("macc", [1, 1024], F32))
        vacc = esE.enter_context(nc.sbuf_tensor("vacc", [1, 1024], F32))

        IGS = [(i0, min(IC, i0 + 4)) for i0 in range(0, IC, 4)]
        NIG = len(IGS)

        e_base = dict(CNT)
        EEVT = EvtDict()

        def emit_E(recording, engines=None):
            EvtDict.recording = recording
            cnt = dict(e_base)

            def bump(kind, key=None, k=1):
                cnt[kind] += k
                if recording and key is not None:
                    EEVT[key] = cnt[kind]
                return cnt[kind]

            def wait(eng_obj, kind, val):
                if not recording and eng_obj is not None:
                    eng_obj.wait_ge(sems[kind], val)

            def sfill(name, slot, instr, key=None, k=16):
                skey = ("S", name, slot)
                cnt[skey] += k
                if instr is not None:
                    instr.then_inc(STREAMS[name][slot], k)
                if recording and key is not None:
                    EEVT[key] = (name, slot, cnt[skey])
                return (name, slot, cnt[skey])

            def swait(eng_obj, token):
                if not recording and eng_obj is not None and token:
                    eng_obj.wait_ge(STREAMS[token[0]][token[1]], token[2])

            tn = engines.get("tn") if engines else None
            vv = engines.get("v") if engines else None
            ss = engines.get("s") if engines else None
            sy = engines.get("sy") if engines else None
            gp = engines.get("g") if engines else None
            do_pe = recording or tn is not None
            do_dv = recording or vv is not None
            do_ac = recording or ss is not None
            do_sy = recording or sy is not None
            do_gp = recording or gp is not None

            for qh in range(NQH):
                if do_sy:
                    if qh == 0:
                        wait(sy, "cast", cast_done["wd_bf"])
                        wait(sy, "pe", d_pe_end)
                        for _s in range(4):
                            _tok = DEVT.get(("gp_h", NQH * IC - 1 - _s), None)
                            swait(sy, _tok)
                    for hpr in range(HC // 2):
                        idx = qh * (HC // 2) + hpr
                        if idx >= 2:
                            wait(sy, "pe", EEVT[("pe_idx", idx - 2)])
                        d = (sy.dma_start(out=wdb[idx % 2][:], in_=wd_bf[hpr])
                             if not recording else None)
                        sfill("wd", idx % 2, d, ("ld_wd", idx))
                        for ig in range(NIG):
                            gi = idx * NIG + ig
                            i0, i1 = IGS[ig]
                            if gi >= 2:
                                wait(sy, "pe", EEVT[("pe_ig", gi - 2)])
                            src = hact_scr[i0:i1, :, qh * 1024:(qh + 1) * 1024]
                            src = src.rearrange("i p q -> p i q")
                            d = (sy.dma_start(
                                out=hat[gi % 2][:, 0:(i1 - i0) * 1024]
                                .rearrange("p (i q) -> p i q", q=1024),
                                in_=src)
                                if not recording else None)
                            sfill("hat", gi % 2, d, ("ld_ha", idx, ig))
                        for hc2 in range(2):
                            hc = 2 * hpr + hc2
                            for qg in range(2):
                                ti = (idx * 4 + hc2 * 2 + qg) % 4
                                if idx * 4 + hc2 * 2 + qg >= 4:
                                    wait(sy, "dv",
                                         EEVT[("dv_y2", (idx * 4 + hc2 * 2 + qg)
                                               - 4)])
                                d = (sy.dma_start(
                                    out=y1e[ti][:],
                                    in_=y1_scr[:, hc * OWN + qh * 1024
                                               + qg * 512:
                                               hc * OWN + qh * 1024
                                               + qg * 512 + 512])
                                    if not recording else None)
                                sfill("y1e", ti, d, ("ld_y1e", idx, hc2, qg))

                def emit_e_stats(idx):
                    if not do_pe:
                        return
                    for hc2 in range(2):
                        for qg in range(2):
                            j = idx * 4 + hc2 * 2 + qg
                            wait(tn, "dv", EEVT[("dv_y2s", j)])
                            pg = (ps_a, ps_b)[idx % 2]
                            for which in range(2):
                                srcb = (y2b, y2s)[which][j % 4]
                                rows = pg[0:4, (hc2 * 2 + qg) * 512:
                                          (hc2 * 2 + qg) * 512 + 512]
                                if which == 1:
                                    rows = pg[32:36, (hc2 * 2 + qg) * 512:
                                              (hc2 * 2 + qg) * 512 + 512]
                                if not recording:
                                    mm = tn.matmul(rows, ob_sb[:, 0:4], srcb[:],
                                                   start=True, stop=True)
                                    mm.then_inc(sems["pe"], 1)
                                bump("pe", ("pe_est", j, which))

                def emit_e_statadd(idx, qh_):
                    if not do_dv:
                        return
                    first = (idx == qh_ * (HC // 2))
                    for hc2 in range(2):
                        for qg in range(2):
                            j = idx * 4 + hc2 * 2 + qg
                            pg = (ps_a, ps_b)[idx % 2]
                            wait(vv, "pe", EEVT[("pe_est", j, 1)])
                            col = qg * 512
                            if not recording:
                                vv.wait_ge(sems["dv"], cnt["dv"])
                                if first and hc2 == 0:
                                    d = vv.tensor_copy(
                                        macc[0:1, col:col + 512],
                                        pg[0:1, (hc2 * 2 + qg) * 512:
                                           (hc2 * 2 + qg) * 512 + 512])
                                else:
                                    d = vv.tensor_add(
                                        macc[0:1, col:col + 512],
                                        macc[0:1, col:col + 512],
                                        pg[0:1, (hc2 * 2 + qg) * 512:
                                           (hc2 * 2 + qg) * 512 + 512])
                                d.then_inc(sems["dv"], 1)
                            bump("dv")
                            if not recording:
                                vv.wait_ge(sems["dv"], cnt["dv"])
                                if first and hc2 == 0:
                                    d = vv.tensor_copy(
                                        vacc[0:1, col:col + 512],
                                        pg[32:33, (hc2 * 2 + qg) * 512:
                                           (hc2 * 2 + qg) * 512 + 512])
                                else:
                                    d = vv.tensor_add(
                                        vacc[0:1, col:col + 512],
                                        vacc[0:1, col:col + 512],
                                        pg[32:33, (hc2 * 2 + qg) * 512:
                                           (hc2 * 2 + qg) * 512 + 512])
                                d.then_inc(sems["dv"], 1)
                            bump("dv", ("dv_esta", j))

                if do_pe:
                    for hpr in range(HC // 2):
                        idx = qh * (HC // 2) + hpr
                        pg = (ps_a, ps_b)[idx % 2]
                        swait(tn, EEVT[("ld_wd", idx)])
                        if idx >= 2:
                            wait(tn, "dv", EEVT[("dv_evdone", idx - 2)])
                            wait(tn, "dv", EEVT.get(("dv_esta", (idx - 2) * 4 + 3),
                                                    0))
                        if qh >= 1 and hpr < 2:
                            wait(tn, "dv", EEVT[("dv_lnb2", qh - 1, 1, 1)])
                        for ig in range(NIG):
                            gi = idx * NIG + ig
                            i0, i1 = IGS[ig]
                            swait(tn, EEVT[("ld_ha", idx, ig)])
                            mm = None
                            for ic in range(i0, i1):
                                for hc2 in range(2):
                                    for qg in range(2):
                                        if not recording:
                                            mm = tn.matmul(
                                                pg[:, (hc2 * 2 + qg) * 512:
                                                   (hc2 * 2 + qg + 1) * 512],
                                                wdb[idx % 2][:, ic * 256
                                                             + hc2 * 128:
                                                             ic * 256 + hc2 * 128
                                                             + 128],
                                                hat[gi % 2][:, (ic - i0) * 1024
                                                            + qg * 512:
                                                            (ic - i0) * 1024
                                                            + qg * 512 + 512],
                                                start=(ic == 0),
                                                stop=(ic == IC - 1))
                            if not recording:
                                mm.then_inc(sems["pe"], 1)
                            bump("pe", ("pe_ig", gi))
                        if recording:
                            EEVT[("pe_idx", idx)] = cnt["pe"]
                        if idx >= qh * (HC // 2) + 1:
                            emit_e_stats(idx - 1)
                    emit_e_stats(qh * (HC // 2) + HC // 2 - 1)
                    wait(tn, "dv", EEVT[("dv_lnrow2", qh)])
                    for seg in range(2):
                        for which in range(2):
                            srcr = (mrow2, rrow2)[which]
                            if not recording:
                                mm = tn.matmul(
                                    ps_a[:, (seg * 2 + which) * 512:
                                         (seg * 2 + which + 1) * 512],
                                    of_sb[:, :],
                                    srcr[0:1, seg * 512:(seg + 1) * 512],
                                    start=True, stop=True)
                                mm.then_inc(sems["pe"], 1)
                            bump("pe", ("pe_lnb2", qh, seg, which))

                if do_dv:
                    for hpr in range(HC // 2):
                        idx = qh * (HC // 2) + hpr
                        pg = (ps_a, ps_b)[idx % 2]
                        for hc2 in range(2):
                            hc = 2 * hpr + hc2
                            for qg in range(2):
                                j = idx * 4 + hc2 * 2 + qg
                                ti = j % 4
                                wait(vv, "pe", EEVT[("pe_idx", idx)])
                                swait(vv, EEVT[("ld_y1e", idx, hc2, qg)])
                                if idx * 4 + hc2 * 2 + qg >= 4:
                                    wait(vv, "pe",
                                         EEVT.get(("pe_est",
                                                   idx * 4 + hc2 * 2 + qg - 4, 1),
                                                  0))
                                if idx * 4 + hc2 * 2 + qg >= 4:
                                    swait(vv, EEVT[("gp_y2j",
                                                    idx * 4 + hc2 * 2 + qg - 4)])
                                if not recording:
                                    if j > 0:
                                        vv.wait_ge(sems["dv"], cnt["dv"])
                                    d = vv.tensor_sub(x1e[:], y1e[ti][:],
                                                      mb1_sb[:, qh * 1024
                                                             + qg * 512:
                                                             qh * 1024 + qg * 512
                                                             + 512])
                                    d.then_inc(sems["dv"], 1)
                                bump("dv")
                                if not recording:
                                    vv.wait_ge(sems["dv"], cnt["dv"])
                                    d = vv.tensor_mul(x1e[:], x1e[:],
                                                      rb1_sb[:, qh * 1024
                                                             + qg * 512:
                                                             qh * 1024 + qg * 512
                                                             + 512])
                                    d.then_inc(sems["dv"], 1)
                                bump("dv")
                                if not recording:
                                    vv.wait_ge(sems["dv"], cnt["dv"])
                                    d = vv.scalar_tensor_tensor(
                                        y2t[ti][:],
                                        pg[:, (hc2 * 2 + qg) * 512:
                                           (hc2 * 2 + qg + 1) * 512],
                                        sd_sb[:, hc:hc + 1], x1e[:],
                                        ALU.mult, ALU.add)
                                    d.then_inc(sems["dv"], 1)
                                bump("dv", ("dv_y2", j))
                                if not recording:
                                    vv.wait_ge(sems["dv"], cnt["dv"])
                                    d = vv.tensor_copy(y2b[j % 4][:], y2t[ti][:])
                                    d.then_inc(sems["dv"], 1)
                                bump("dv", ("dv_y2b", j))
                                if not recording:
                                    vv.wait_ge(sems["dv"], cnt["dv"])
                                    d = vv.tensor_mul(y2s[j % 4][:],
                                                      y2b[j % 4][:], y2b[j % 4][:])
                                    d.then_inc(sems["dv"], 1)
                                bump("dv", ("dv_y2s", j))
                        if recording:
                            EEVT[("dv_evdone", idx)] = cnt["dv"]
                        if idx >= qh * (HC // 2) + 1:
                            emit_e_statadd(idx - 1, qh)
                    emit_e_statadd(qh * (HC // 2) + HC // 2 - 1, qh)
                    if not recording:
                        vv.wait_ge(sems["dv"], cnt["dv"])
                        d = vv.tensor_scalar_mul(mrow2[0:1, :], macc[0:1, :],
                                                 1.0 / D)
                        d.then_inc(sems["dv"], 1)
                        vv.wait_ge(sems["dv"], cnt["dv"] + 1)
                        d = vv.tensor_scalar_mul(vrow2[0:1, :], vacc[0:1, :],
                                                 1.0 / D)
                        d.then_inc(sems["dv"], 1)
                        vv.wait_ge(sems["dv"], cnt["dv"] + 2)
                        d = vv.tensor_mul(rrow2[0:1, :], mrow2[0:1, :],
                                          mrow2[0:1, :])
                        d.then_inc(sems["dv"], 1)
                        vv.wait_ge(sems["dv"], cnt["dv"] + 3)
                        d = vv.tensor_sub(vrow2[0:1, :], vrow2[0:1, :],
                                          rrow2[0:1, :])
                        d.then_inc(sems["dv"], 1)
                    for _ in range(4):
                        bump("dv")
                    if recording:
                        EEVT[("dv_var2", qh)] = cnt["dv"]

                if do_ac:
                    wait(ss, "dv", EEVT[("dv_var2", qh)])
                    if not recording:
                        d = ss.activation(rrow2[0:1, :], vrow2[0:1, :], AF.Ln,
                                          bias=eps_sb[0:1, :])
                        d.then_inc(sems["ac"], 1)
                        ss.wait_ge(sems["ac"], cnt["ac"] + 1)
                        d = ss.activation(rrow2[0:1, :], rrow2[0:1, :], AF.Exp,
                                          scale=-0.5)
                        d.then_inc(sems["ac"], 1)
                    bump("ac")
                    bump("ac", ("ac_rstd2", qh))
                if do_dv:
                    wait(vv, "ac", EEVT[("ac_rstd2", qh)])
                    if not recording:
                        d = vv.tensor_copy(dvm_sb[0:1, 1:2], rrow2[0:1, 0:1])
                        d.then_inc(sems["dv"], 1)
                    bump("dv", ("dv_lnrow2", qh))
                    for seg in range(2):
                        for which in range(2):
                            wait(vv, "pe", EEVT[("pe_lnb2", qh, seg, which)])
                            dstb = (mb2_sb, rb2_sb)[which]
                            if not recording:
                                d = vv.tensor_copy(
                                    dstb[:, qh * 1024 + seg * 512:
                                         qh * 1024 + (seg + 1) * 512],
                                    ps_a[:, (seg * 2 + which) * 512:
                                         (seg * 2 + which + 1) * 512])
                                d.then_inc(sems["dv"], 1)
                            bump("dv", ("dv_lnb2", qh, seg, which))

                if do_gp:
                    for hpr in range(HC // 2):
                        idx = qh * (HC // 2) + hpr
                        for hc2 in range(2):
                            hc = 2 * hpr + hc2
                            for qg in range(2):
                                j = idx * 4 + hc2 * 2 + qg
                                ti = j % 4
                                wait(gp, "dv", EEVT[("dv_y2b", j)])
                                d = (gp.dma_start(
                                    out=y2_scr[:, hc * OWN + qh * 1024
                                               + qg * 512:
                                               hc * OWN + qh * 1024
                                               + qg * 512 + 512],
                                    in_=y2t[ti][:])
                                    if not recording else None)
                                tok = sfill("y2st", ti, d, ("gp_y2", qh, hc, qg))
                                if recording:
                                    EEVT[("gp_y2j", qh * 4 * (HC // 2)
                                          + (hc // 2) * 4
                                          + (hc % 2) * 2 + qg)] = tok

            for hc in range(HC):
                if do_sy:
                    if hc >= 2:
                        wait(sy, "dv", EEVT[("dv_x2", hc - 2)])
                        wait(sy, "pe", EEVT[("pe_tr", hc - 2, NT - 1)])
                    swait(sy, EEVT[("gp_y2", NQH - 1, hc, 1)])
                    swait(sy, EEVT[("gp_y2", NQH - 1, hc, 0)])
                    d = (sy.dma_start(out=x2f[hc % 2][:],
                                      in_=y2_scr[:, hc * OWN:(hc + 1) * OWN])
                         if not recording else None)
                    sfill("y2f", hc % 2, d, ("ld_y2f", hc))
                if do_dv:
                    swait(vv, EEVT[("ld_y2f", hc)])
                    wait(vv, "dv", EEVT[("dv_lnb2", NQH - 1, 1, 1)])
                    if not recording:
                        vv.wait_ge(sems["dv"], cnt["dv"])
                        d = vv.tensor_sub(x2f[hc % 2][:], x2f[hc % 2][:],
                                          mb2_sb[:])
                        d.then_inc(sems["dv"], 1)
                    bump("dv")
                    if not recording:
                        vv.wait_ge(sems["dv"], cnt["dv"])
                        d = vv.tensor_mul(x2f[hc % 2][:], x2f[hc % 2][:],
                                          rb2_sb[:])
                        d.then_inc(sems["dv"], 1)
                    bump("dv", ("dv_x2", hc))
                for qt in range(NT):
                    slot = qt % 4
                    oslot = qt % 2
                    psv = (ps_a, ps_b)[slot // 2][:, (slot % 2) * 512:
                                                  (slot % 2) * 512 + 128]
                    if do_pe:
                        wait(tn, "dv", EEVT[("dv_x2", hc)])
                        if hc * NT + qt >= 4:
                            wait(tn, "dv", EEVT[("dv_ost", hc * NT + qt - 4)])
                        if hc == 0 and qt < 4:
                            wait(tn, "dv", EEVT[("dv_lnb2", NQH - 1, 1, 1)])
                        if not recording:
                            mm = tn.transpose(psv, x2f[hc % 2][:, qt * 128:
                                                              (qt + 1) * 128],
                                              idf[:])
                            mm.then_inc(sems["pe"], 1)
                        bump("pe", ("pe_tr", hc, qt))
                    if do_dv:
                        wait(vv, "pe", EEVT[("pe_tr", hc, qt)])
                        if hc * NT + qt >= 2:
                            swait(vv, EEVT[("gp_out", hc * NT + qt - 2)])
                        if not recording:
                            d = vv.tensor_copy(ostg[oslot][:], psv)
                            d.then_inc(sems["dv"], 1)
                        bump("dv", ("dv_ost", hc * NT + qt))
                    if do_gp:
                        wait(gp, "dv", EEVT[("dv_ost", hc * NT + qt)])
                        d = (gp.dma_start(
                            out=out[qt * 128:(qt + 1) * 128,
                                    hc * 128:(hc + 1) * 128],
                            in_=ostg[oslot][:])
                            if not recording else None)
                        sfill("ost", oslot, d, ("gp_out", hc * NT + qt))
            if do_gp and not recording:
                for slot in range(2):
                    gp.wait_ge(STREAMS["ost"][slot], cnt[("S", "ost", slot)])
            return cnt

        e_final = emit_E(True)

        @block.sync
        def _(sy):
            emit_E(False, {"sy": sy})

        @block.tensor
        def _(tn):
            emit_E(False, {"tn": tn})

        @block.scalar
        def _(ss):
            emit_E(False, {"s": ss})

        @block.vector
        def _(vv):
            emit_E(False, {"v": vv})

        @block.gpsimd
        def _(gp):
            emit_E(False, {"g": gp})

        esE.close()

    return nc


def _rope_tables(S, dtype=np.float32):
    pos = np.arange(S, dtype=np.float64)[:, None]
    freqs = np.exp(-np.arange(0, 128, 2, dtype=np.float64)
                   * (math.log(ROPE_THETA) / 128.0))
    ang = pos * freqs
    return np.cos(ang).astype(dtype), np.sin(ang).astype(dtype)


def host_prep(cfg, inputs, core):
    S, D, NH, NKV, I = cfg["S"], cfg["D"], cfg["NH"], cfg["NKV"], cfg["I"]
    HC = D // 128
    ST = S // 128
    OWN = S // 2
    NT = OWN // 128
    OC = NH
    IC = I // 128
    KVW = NKV * 128
    NKVP = cfg.get("NKVP", min(4, NKV))
    KVQ = KVW // NKVP
    NGK = KVQ // 128
    KW = 2 * KVQ

    b, j = core // 2, core % 2
    x = np.asarray(inputs["L0_attn_hidden_states"][b])
    order = own_tile_order(NT)
    own_abs_tiles = [2 * t + j for t in order]
    own_cols = np.concatenate([np.arange(ta * 128, (ta + 1) * 128)
                               for ta in own_abs_tiles])

    xt = np.ascontiguousarray(x.T)
    xt_full = np.ascontiguousarray(
        xt.reshape(HC, 128, ST, 128).transpose(2, 1, 0, 3)
    ).reshape(ST, 128, HC * 128)
    xo = xt[:, own_cols]
    xt_own = np.ascontiguousarray(
        xo.reshape(HC, 128, OWN).transpose(1, 0, 2)).reshape(128, HC * OWN)

    wq = np.asarray(inputs["L0_attn_Wq"])
    wqT = np.ascontiguousarray(wq.T)
    wq_t = np.ascontiguousarray(
        wqT.reshape(HC, 128, OC // 2, 256).transpose(2, 1, 0, 3)
    ).reshape(OC // 2, 128, HC * 256)

    wk = np.asarray(inputs["L0_attn_Wk"])
    wv = np.asarray(inputs["L0_attn_Wv"])
    wkT, wvT = wk.T, wv.T
    wkv_t = np.empty((NKVP, 128, HC * KW), np.float32)
    for p in range(NKVP):
        blkk = wkT[:, p * KVQ:(p + 1) * KVQ]
        blkv = wvT[:, p * KVQ:(p + 1) * KVQ]
        cat = np.concatenate([blkk, blkv], axis=1)
        wkv_t[p] = np.ascontiguousarray(
            cat.reshape(HC, 128, KW).transpose(1, 0, 2)).reshape(128, HC * KW)

    wo = np.asarray(inputs["L0_attn_Wo"])
    woT = np.ascontiguousarray(wo.T)
    wo_t = np.ascontiguousarray(
        woT.reshape(OC, 128, HC, 128).transpose(2, 1, 0, 3)
    ).reshape(HC, 128, OC * 128)

    wg = np.asarray(inputs["L0_ffn_w_gate"])
    wu = np.asarray(inputs["L0_ffn_w_up"])
    wgT, wuT = wg.T, wu.T
    wg_t = np.ascontiguousarray(
        wgT.reshape(HC, 128, IC, 128).transpose(2, 1, 0, 3)
    ).reshape(IC, 128, HC * 128).astype(np.int32)
    wu_t = np.ascontiguousarray(
        wuT.reshape(HC, 128, IC, 128).transpose(2, 1, 0, 3)
    ).reshape(IC, 128, HC * 128).astype(np.int32)

    wd = np.asarray(inputs["L0_ffn_w_down"])
    wdT = np.ascontiguousarray(wd.T)
    wd_t = np.ascontiguousarray(
        wdT.reshape(IC, 128, HC // 2, 256).transpose(2, 1, 0, 3)
    ).reshape(HC // 2, 128, IC * 256).astype(np.int32)

    sg = np.ascontiguousarray(
        np.asarray(inputs["L0_ffn_s_gate"]).reshape(IC, 128).T)
    su = np.ascontiguousarray(
        np.asarray(inputs["L0_ffn_s_up"]).reshape(IC, 128).T)
    sd = np.ascontiguousarray(
        np.asarray(inputs["L0_ffn_s_down"]).reshape(HC, 128).T)

    cosf, sinf = _rope_tables(S)
    cosk = np.ascontiguousarray(np.tile(cosf, (1, NGK)))
    sink = np.ascontiguousarray(np.tile(sinf, (1, NGK)))
    cosq = np.ascontiguousarray(cosf[own_cols].T)
    sinq = np.ascontiguousarray(sinf[own_cols].T)

    tri = np.tril(np.ones((128, 128), np.float32))
    m_tril = np.triu(np.ones((128, 128), np.float32)).T
    m_tril = (np.arange(128)[:, None] <= np.arange(128)[None, :]).astype(np.float32)
    import ml_dtypes
    if j == 0:
        maskA = m_tril.astype(ml_dtypes.bfloat16)
        maskB = np.zeros((128, 128), ml_dtypes.bfloat16)
    else:
        maskA = np.ones((128, 128), ml_dtypes.bfloat16)
        maskB = m_tril.astype(ml_dtypes.bfloat16)

    identf = np.eye(128, dtype=np.float32)
    onesb = np.ones((128, 8), ml_dtypes.bfloat16)
    onesf = np.ones((1, 128), np.float32)

    return {
        "xt_full": xt_full, "xt_own": xt_own, "wq_t": wq_t, "wkv_t": wkv_t,
        "wo_t": wo_t, "wg_t": wg_t, "wu_t": wu_t, "wd_t": wd_t,
        "sg": sg, "su": su, "sd": sd,
        "cosk": cosk, "sink": sink, "cosq": cosq, "sinq": sinq,
        "maskA": np.asarray(maskA), "maskB": np.asarray(maskB),
        "identf": identf, "onesb": np.asarray(onesb), "onesf": onesf,
        "epsv": np.full((1, 1), LN_EPS, np.float32),
    }, own_abs_tiles


def kernel(**inputs):
    cfg = dict(CFG_FULL)
    S, D = cfg["S"], cfg["D"]
    OWN = S // 2
    NT = OWN // 128
    nc = build_program(cfg)
    in_maps = []
    tiles_per_core = []
    for core in range(8):
        m, own_abs = host_prep(cfg, inputs, core)
        in_maps.append(m)
        tiles_per_core.append(own_abs)
    res = run_bass_kernel_spmd(nc, in_maps, core_ids=list(range(8)))
    NB = cfg["NB"]
    outp = np.empty((NB, S, D), np.float32)
    for core in range(8):
        b = core // 2
        o = res.results[core]["out"]
        for idx, ta in enumerate(tiles_per_core[core]):
            outp[b, ta * 128:(ta + 1) * 128, :] = o[idx * 128:(idx + 1) * 128, :]
    return outp


# version 06
# speedup vs baseline: 1.1036x; 1.1036x over previous
import math
from contextlib import ExitStack, contextmanager

import numpy as np
import concourse.bass as bass
import concourse.mybir as mybir
from concourse.bass_utils import run_bass_kernel_spmd

F32 = mybir.dt.float32
BF16 = mybir.dt.bfloat16
I32 = mybir.dt.int32
AF = mybir.ActivationFunctionType
ALU = mybir.AluOpType
AX = mybir.AxisListType
PE = mybir.EngineType.PE

ISD = 1.0 / math.sqrt(128.0)
LN_EPS = 1e-5
ROPE_THETA = 10000.0

CFG_FULL = dict(S=4096, D=4096, NH=32, NKV=8, I=11008, NB=4)


@contextmanager
def kfori(nc, n_iters, engine_type):
    engines = [engine_type]
    loop_id = nc.next_id()
    loop_start = f"kfori_{loop_id}_loop"
    loop_end = f"kfori_{loop_id}_end"
    regs = nc.alloc_registers(f"kfori_{loop_id}_i", engines=engines)
    nc.regs_mov(regs, 0)
    nc.br(loop_start, engines=engines)
    with nc.body(loop_start, valid_engines=engines):
        yield None
        nc.regs_alu(regs, regs, 1, op=ALU.add)
        nc.br_lt(regs, n_iters, on_true=loop_start, on_false=loop_end,
                 engines=engines)
    nc.switch_bb(loop_end)
    for h in regs:
        nc.free_register(h)


class EvtDict(dict):
    recording = False

    def __missing__(self, key):
        if EvtDict.recording:
            return 0
        raise KeyError(key)


def own_tile_order(NT):
    order = []
    a = 0
    while a * 8 < NT:
        order.extend(range(8 * a + 7, 8 * a - 1, -1))
        a += 1
    return order


def build_program(cfg, debug=()):
    S, D, NH, NKV, I = cfg["S"], cfg["D"], cfg["NH"], cfg["NKV"], cfg["I"]
    HC = D // 128
    ST = S // 128
    OWN = S // 2
    NT = OWN // 128
    OC = NH
    IC = I // 128
    QGN = OWN // 512
    NPAIR = NT // 8
    GRP = NH // NKV
    assert NT % 8 == 0 and OC % 2 == 0 and HC % 2 == 0
    KVW = NKV * 128
    NKVP = cfg.get("NKVP", min(4, NKV))
    KVQ = KVW // NKVP
    NGK = KVQ // 128
    KW = 2 * KVQ
    NSEG_K = S // 512
    NQH = max(1, OWN // 1024)

    nc = bass.Bass("TRN2", target_bir_lowering=False, debug=False)

    def din(name, shape, dt):
        return nc.dram_tensor(name, shape, dt, kind="ExternalInput").ap()

    xt_full = din("xt_full", [ST, 128, HC * 128], F32)
    xt_own = din("xt_own", [128, HC * OWN], F32)
    wq_t = din("wq_t", [OC // 2, 128, HC * 256], F32)
    wkv_t = din("wkv_t", [NKVP, 128, HC * KW], F32)
    wo_t = din("wo_t", [HC, 128, OC * 128], F32)
    wg_t = din("wg_t", [IC, 128, HC * 128], I32)
    wu_t = din("wu_t", [IC, 128, HC * 128], I32)
    HP4 = 4 if HC % 4 == 0 else 2
    wd_t = din("wd_t", [HC // HP4, 128, IC * HP4 * 128], I32)
    sg_in = din("sg", [128, IC], F32)
    su_in = din("su", [128, IC], F32)
    sd_in = din("sd", [128, HC], F32)
    cosk = din("cosk", [S, NGK * 64], F32)
    sink = din("sink", [S, NGK * 64], F32)
    cosq = din("cosq", [64, OWN], F32)
    sinq = din("sinq", [64, OWN], F32)
    maskA_in = din("maskA", [128, 128], BF16)
    maskB_in = din("maskB", [128, 128], BF16)
    identf_in = din("identf", [128, 128], F32)
    onesb_in = din("onesb", [128, 8], BF16)
    onesf_in = din("onesf", [1, 128], F32)
    eps_in = din("epsv", [1, 1], F32)

    out = nc.dram_tensor("out", [OWN, D], F32, kind="ExternalOutput").ap()

    def scr(name, shape, dt):
        kind = "ExternalOutput" if name in debug else "Internal"
        return nc.dram_tensor(name, shape, dt, kind=kind).ap()

    wo_bf = scr("wo_bf", [HC, 128, OC * 128], BF16)
    wg_bf = scr("wg_bf", [IC, 128, HC * 128], BF16)
    wu_bf = scr("wu_bf", [IC, 128, HC * 128], BF16)
    wd_bf = scr("wd_bf", [HC // HP4, 128, IC * HP4 * 128], BF16)
    q_scr = scr("q_scr", [NH, 128, OWN], BF16)
    ktm_scr = scr("ktm_scr", [NKV, S, 128], BF16)
    av_scr = scr("av_scr", [NH, 128, OWN], BF16)
    y1_scr = scr("y1_scr", [128, HC * OWN], F32)
    hact_scr = scr("hact_scr", [IC, 128, OWN], BF16)
    y2_scr = scr("y2_scr", [128, HC * OWN], F32)
    kfm_dbg = None
    vtm_dbg = scr("vtm_dbg", [128, ST * KVW], BF16) if "vtm_dbg" in debug else None
    ln1_dbg = scr("ln1_dbg", [128, 2 * OWN], F32) if "ln1_dbg" in debug else None

    es = ExitStack()
    with es:
        SB = lambda n, s, d: es.enter_context(nc.sbuf_tensor(n, s, d))

        mA = SB("mA", [128, 128], BF16)
        mB = SB("mB", [128, 128], BF16)
        idf = SB("idf", [128, 128], F32)
        ob_sb = SB("ob_sb", [128, 8], BF16)
        of_sb = SB("of_sb", [1, 128], F32)
        sd_sb = SB("sd_sb", [128, HC], F32)
        sg_sb = SB("sg_sb", [128, IC], F32)
        su_sb = SB("su_sb", [128, IC], F32)
        eps_sb = SB("eps_sb", [1, 1], F32)
        dvm_sb = SB("dvm_sb", [1, 8], F32)

        ps_a = es.enter_context(nc.psum_tensor("ps_a", [128, 2048], F32))
        ps_b = es.enter_context(nc.psum_tensor("ps_b", [128, 2048], F32))

        sems = {}
        for nm in ("cast", "pe", "dv", "ac", "cwo", "cwg", "cwu", "cwd"):
            sems[nm] = es.enter_context(nc.semaphore("s_" + nm))
        CNT = {k: 0 for k in sems}

        def inc(kind, instr, k=1):
            instr.then_inc(sems[kind], k)
            CNT[kind] += k
            return CNT[kind]

        STREAMS = {}

        def mkstream(name, nslots):
            STREAMS[name] = [es.enter_context(nc.semaphore(f"q_{name}{i}"))
                             for i in range(nslots)]
            for i in range(nslots):
                CNT[("S", name, i)] = 0

        for _nm, _ns in (("ld0", 1), ("xtq", 1), ("wq", 2), ("wkv", 1), ("xtb", 2),
                         ("cksk", 2), ("qh", 1), ("kfg", 2), ("wo", 2), ("xtt", 4),
                         ("y1d", 2), ("wgu", 2), ("wd", 4), ("hat", 2),
                         ("y1e", 4), ("y2f", 2), ("avh", 1),
                         ("qst", 2), ("kstst", 2), ("avst", 2), ("y1st", 4),
                         ("hst", 4), ("y2st", 4), ("ost", 4)):
            mkstream(_nm, _ns)

        def sfill_g(name, slot, instr, k=16):
            key = ("S", name, slot)
            CNT[key] += k
            instr.then_inc(STREAMS[name][slot], k)
            return (name, slot, CNT[key])

        def swait_g(eng_obj, token):
            if token:
                name, slot, v = token
                eng_obj.wait_ge(STREAMS[name][slot], v)

        block = es.enter_context(nc.Block())

        esA = ExitStack()
        xtq = esA.enter_context(nc.sbuf_tensor("xtq", [128, HC * OWN], BF16))
        wqb = [esA.enter_context(nc.sbuf_tensor(f"wqb{i}", [128, HC * 256], BF16))
               for i in range(2)]
        qstage = [esA.enter_context(nc.sbuf_tensor(f"qstage{i}", [128, OWN], BF16))
                  for i in range(2)]
        ropeq = esA.enter_context(nc.sbuf_tensor("ropeq", [64, 2 * OWN], F32))
        cq_sb = esA.enter_context(nc.sbuf_tensor("cq_sb", [64, OWN], F32))
        sq_sb = esA.enter_context(nc.sbuf_tensor("sq_sb", [64, OWN], F32))
        AQ_XTQ = [xtq[:]]
        AQ_WQB = wqb

        precast_list = [
            ("wo_bf", "cwo", wo_bf, wo_t, 4),
            ("wg_bf", "cwg", wg_bf, wg_t, 8),
            ("wu_bf", "cwu", wu_bf, wu_t, 8),
            ("wd_bf", "cwd", wd_bf, wd_t, 8),
        ]
        prechunks = []
        cast_done = {}
        for _name, _sem, _dst, _src, _nch in precast_list:
            _n0 = _src.shape[0]
            _step = max(1, -(-_n0 // _nch))
            _cc = 0
            for _i0 in range(0, _n0, _step):
                prechunks.append((_sem, _dst, _src, _i0, min(_n0, _i0 + _step)))
                _cc += 16
            cast_done[_name] = (_sem, _cc)

        @block.gpsimd
        def _(g):
            sfill_g("xtq", 0, g.dma_start(out=AQ_XTQ[0], in_=xt_own[:]))
            for ob in range(OC // 2):
                if ob >= 2:
                    g.wait_ge(sems["pe"], (ob - 2) * 2 + 2)
                sfill_g("wq", ob % 2, g.dma_start(out=AQ_WQB[ob % 2][:],
                                                  in_=wq_t[ob]))

        @block.sync
        def _(sy):
            for dst, src in ((mA[:], maskA_in), (mB[:], maskB_in),
                             (idf[:], identf_in), (ob_sb[:], onesb_in),
                             (of_sb[:], onesf_in), (eps_sb[:], eps_in),
                             (sd_sb[:], sd_in),
                             (sg_sb[:], sg_in), (su_sb[:], su_in)):
                sfill_g("ld0", 0, sy.dma_start(out=dst, in_=src))


        @block.sync
        def _(sy):
            sfill_g("ld0", 0, sy.dma_start(out=cq_sb[:], in_=cosq))
            sfill_g("ld0", 0, sy.dma_start(out=sq_sb[:], in_=sinq))
        aq_ld0_tok = ("ld0", 0, CNT[("S", "ld0", 0)])



        @block.tensor
        def _(tn):
            def aq_block(ob, wv, ev):
                if wv is None:
                    if ob < 2:
                        swait_g(tn, aq_ld0_tok)
                        swait_g(tn, ("xtq", 0, 16))
                    swait_g(tn, ("wq", ob % 2, 16 * (ob // 2 + 1)))
                else:
                    if ob % 2 == 0:
                        tn.wait_ge(STREAMS["wq"][0], wv)
                    else:
                        tn.wait_ge(STREAMS["wq"][1], wv)
                        nc.regs_alu(wv, wv, 16, op=ALU.add)
                for oc in range(2):
                    gidx = 2 * ob + oc
                    pg = (ps_a, ps_b)[oc]
                    if ev is None:
                        if gidx >= 2:
                            tn.wait_ge(sems["dv"], 4 * (gidx - 2) + 3)
                    else:
                        tn.wait_ge(sems["dv"], ev)
                        nc.regs_alu(ev, ev, 4, op=ALU.add)
                    mm = None
                    for hc in range(HC):
                        for qg in range(QGN):
                            mm = tn.matmul(
                                pg[:, qg * 512:(qg + 1) * 512],
                                wqb[ob % 2][:, hc * 256 + oc * 128:
                                            hc * 256 + oc * 128 + 128],
                                xtq[:, hc * OWN + qg * 512:
                                    hc * OWN + (qg + 1) * 512],
                                start=(hc == 0), stop=(hc == HC - 1))
                    inc("pe", mm, 1)

            NOB = OC // 2
            aq_block(0, None, None)
            if NOB > 1:
                aq_block(1, None, None)
            npairs = max(0, (NOB - 2) // 2)
            if npairs > 0:
                wv = nc.alloc_registers("aq_wv", engines=[PE])
                ev = nc.alloc_registers("aq_ev", engines=[PE])
                nc.regs_mov(wv, 16 * 2)
                nc.regs_mov(ev, 11)
                pe_before = CNT["pe"]
                with kfori(nc, npairs, PE):
                    aq_block(2, wv, ev)
                    aq_block(3, wv, ev)
                CNT["pe"] = pe_before + 4 * npairs
                for r in (wv, ev):
                    for h in r:
                        nc.free_register(h)
            for ob in range(2 + 2 * npairs, NOB):
                aq_block(ob, None, None)

        aq_pe_end = CNT["pe"]
        assert aq_pe_end == OC, (aq_pe_end, OC)

        T1 = ropeq[:, 0 * OWN:1 * OWN]
        T2 = ropeq[:, 1 * OWN:2 * OWN]

        @block.vector
        def _(v):
            for gidx in range(OC):
                pg = (ps_a, ps_b)[gidx % 2]
                qs = qstage[gidx % 2]
                v.wait_ge(sems["pe"], gidx + 1)
                if gidx >= 2:
                    swait_g(v, ("qst", gidx % 2, 16 * ((gidx - 2) // 2 + 1)))
                if gidx >= 1:
                    v.wait_ge(sems["dv"], 4 * gidx)
                v.tensor_mul(T1, pg[0:64, 0:OWN], cq_sb[:])
                m2 = v.tensor_mul(T2, pg[64:128, 0:OWN], sq_sb[:])
                inc("dv", m2, 1)
                v.wait_ge(sems["dv"], CNT["dv"])
                c = v.tensor_sub(qs[0:64, :], T1, T2)
                inc("dv", c, 1)
                v.wait_ge(sems["dv"], CNT["dv"])
                v.tensor_mul(T1, pg[64:128, 0:OWN], cq_sb[:])
                m4 = v.tensor_mul(T2, pg[0:64, 0:OWN], sq_sb[:])
                inc("dv", m4, 1)
                v.wait_ge(sems["dv"], CNT["dv"])
                a2 = v.tensor_add(qs[64:128, :], T1, T2)
                inc("dv", a2, 1)

        aq_dv_end = CNT["dv"]

        @block.sync
        def _(sy):
            for gidx in range(OC):
                sy.wait_ge(sems["dv"], 4 * gidx + 4)
                sfill_g("qst", gidx % 2,
                        sy.dma_start(out=q_scr[gidx], in_=qstage[gidx % 2][:]))

        esA.close()

        esKV = ExitStack()
        vtm = esKV.enter_context(nc.sbuf_tensor("vtm", [128, ST * KVW], BF16))
        esK = ExitStack()
        wkvb = esK.enter_context(nc.sbuf_tensor("wkvb", [128, HC * KW], BF16))
        xtb = [esK.enter_context(nc.sbuf_tensor(f"xtb{i}", [128, HC * 128], BF16))
               for i in range(2)]
        kst = [esK.enter_context(nc.sbuf_tensor(f"kst{i}", [128, KVQ], BF16))
               for i in range(2)]
        ck_sb = [esK.enter_context(nc.sbuf_tensor(f"ck{i}", [128, NGK * 64], F32))
                 for i in range(2)]
        sk_sb = [esK.enter_context(nc.sbuf_tensor(f"sk{i}", [128, NGK * 64], F32))
                 for i in range(2)]
        kvt = esK.enter_context(nc.sbuf_tensor("kvt", [128, 4 * KVQ], F32))

        @block.sync
        def _(sy):
            sy.wait_ge(sems["pe"], aq_pe_end)
            sy.wait_ge(sems["dv"], aq_dv_end)
            swait_g(sy, ("qst", 0, 16 * ((OC + 1) // 2)))
            swait_g(sy, ("qst", 1, 16 * (OC // 2)))
            for p in range(NKVP):
                for st in range(ST):
                    k = p * ST + st
                    if k >= 2:
                        sy.wait_ge(sems["dv"], aq_dv_end + 2 * k - 2)
                    sfill_g("cksk", k % 2,
                            sy.dma_start(out=ck_sb[k % 2][:],
                                         in_=cosk[st * 128:(st + 1) * 128, :]))
                    sfill_g("cksk", k % 2,
                            sy.dma_start(out=sk_sb[k % 2][:],
                                         in_=sink[st * 128:(st + 1) * 128, :]))

        @block.tensor
        def _(tn):
            def akv_tile(p, st, wv, ev):
                k = p * ST + st
                pg = (ps_a, ps_b)[k % 2]
                if wv is None:
                    swait_g(tn, ("wkv", 0, 16 * (p + 1)))
                    swait_g(tn, ("xtb", k % 2, 16 * (k // 2 + 1)))
                else:
                    if k % 2 == 0:
                        tn.wait_ge(STREAMS["xtb"][0], wv)
                    else:
                        tn.wait_ge(STREAMS["xtb"][1], wv)
                        nc.regs_alu(wv, wv, 16, op=ALU.add)
                if ev is None:
                    if k >= 2:
                        tn.wait_ge(sems["dv"], aq_dv_end + 2 * (k - 2) + 1)
                else:
                    tn.wait_ge(sems["dv"], ev)
                    nc.regs_alu(ev, ev, 2, op=ALU.add)
                mm = None
                for hc in range(HC):
                    for w0 in range(0, KW, 512):
                        w1 = min(KW, w0 + 512)
                        mm = tn.matmul(
                            pg[:, w0:w1],
                            xtb[k % 2][:, hc * 128:(hc + 1) * 128],
                            wkvb[:, hc * KW + w0: hc * KW + w1],
                            start=(hc == 0), stop=(hc == HC - 1))
                inc("pe", mm, 1)

            for p in range(NKVP):
                akv_tile(p, 0, None, None)
                if ST > 1:
                    akv_tile(p, 1, None, None)
                nst = max(0, (ST - 2) // 2)
                if nst > 0:
                    wv = nc.alloc_registers(f"akv_wv{p}", engines=[PE])
                    ev = nc.alloc_registers(f"akv_ev{p}", engines=[PE])
                    nc.regs_mov(wv, 16 * (p * ST // 2 + 2))
                    nc.regs_mov(ev, aq_dv_end + 2 * (p * ST) + 1)
                    pe_before = CNT["pe"]
                    tn.wait_ge(STREAMS["wkv"][0], 16 * (p + 1))
                    with kfori(nc, nst, PE):
                        akv_tile(p, 2, wv, ev)
                        akv_tile(p, 3, wv, ev)
                    CNT["pe"] = pe_before + 2 * nst
                    for r in (wv, ev):
                        for h in r:
                            nc.free_register(h)
                for st in range(2 + 2 * max(0, (ST - 2) // 2), ST):
                    akv_tile(p, st, None, None)

        akv_pe_end = CNT["pe"]
        assert akv_pe_end == aq_pe_end + NKVP * ST

        @block.vector
        def _(v):
            for p in range(NKVP):
                for st in range(ST):
                    k = p * ST + st
                    pg = (ps_a, ps_b)[k % 2]
                    ks = kst[k % 2]
                    v.wait_ge(sems["pe"], aq_pe_end + k + 1)
                    swait_g(v, ("cksk", k % 2, 32 * (k // 2 + 1)))
                    if k >= 2:
                        swait_g(v, ("kstst", k % 2,
                                    16 * NGK * ((k - 2) // 2 + 1)))
                    if k >= 1:
                        v.wait_ge(sems["dv"], aq_dv_end + 2 * k)
                    v.tensor_copy(vtm[:, st * KVW + p * KVQ:
                                      st * KVW + (p + 1) * KVQ],
                                  pg[:, KVQ:KW])
                    pk = pg[:, 0:KVQ].rearrange("p (g d) -> p g d", d=128)
                    t = kvt[:, 0:4 * NGK * 64].rearrange("p (i g d) -> p i g d", i=4, d=64)
                    ck3 = ck_sb[k % 2][:, :].rearrange("p (g d) -> p g d", d=64)
                    sk3 = sk_sb[k % 2][:, :].rearrange("p (g d) -> p g d", d=64)
                    ko = ks[:, :].rearrange("p (g d) -> p g d", d=128)
                    v.tensor_mul(t[:, 0], pk[:, :, 0:64], ck3)
                    v.tensor_mul(t[:, 1], pk[:, :, 64:128], sk3)
                    v.tensor_mul(t[:, 2], pk[:, :, 64:128], ck3)
                    m4 = v.tensor_mul(t[:, 3], pk[:, :, 0:64], sk3)
                    inc("dv", m4, 1)
                    v.wait_ge(sems["dv"], CNT["dv"])
                    v.tensor_sub(ko[:, :, 0:64], t[:, 0], t[:, 1])
                    a2 = v.tensor_add(ko[:, :, 64:128], t[:, 2], t[:, 3])
                    inc("dv", a2, 1)

        akv_dv_end = CNT["dv"]

        @block.gpsimd
        def _(g):
            def kst_store(g, j):
                pj, stj = divmod(j, ST)
                for gg in range(NGK):
                    gabs = pj * NGK + gg
                    sfill_g("kstst", j % 2, g.dma_start(
                        out=ktm_scr[gabs, stj * 128:(stj + 1) * 128, :],
                        in_=kst[j % 2][:, gg * 128:(gg + 1) * 128]))

            g.wait_ge(sems["pe"], aq_pe_end)
            g.wait_ge(sems["dv"], aq_dv_end)
            swait_g(g, ("qst", 0, 16 * ((OC + 1) // 2)))
            swait_g(g, ("qst", 1, 16 * (OC // 2)))
            for p in range(NKVP):
                if p > 0:
                    g.wait_ge(sems["pe"], aq_pe_end + p * ST)
                sfill_g("wkv", 0, g.dma_start(out=wkvb[:], in_=wkv_t[p]))
                for st in range(ST):
                    k = p * ST + st
                    if k >= 2:
                        g.wait_ge(sems["pe"], aq_pe_end + k - 1)
                        g.wait_ge(sems["dv"], aq_dv_end + 2 * k - 2)
                    sfill_g("xtb", k % 2,
                            g.dma_start(out=xtb[k % 2][:], in_=xt_full[st]))
                    if k >= 2:
                        kst_store(g, k - 2)
            for j in (NKVP * ST - 2, NKVP * ST - 1):
                if j >= 0:
                    g.wait_ge(sems["dv"], aq_dv_end + 2 * j + 2)
                    kst_store(g, j)
        kst_tok = [("kstst", i, CNT[("S", "kstst", i)]) for i in range(2)]

        akv_kfm_tok = None

        esK.close()

        if vtm_dbg is not None:
            @block.sync
            def _(sy):
                swait_g(sy, kst_tok[0])
                swait_g(sy, kst_tok[1])
                sy.wait_ge(sems["dv"], akv_dv_end)
                sfill_g("ld0", 0, sy.dma_start(out=vtm_dbg, in_=vtm[:]))
                sy.wait_ge(STREAMS["ld0"][0], CNT[("S", "ld0", 0)])

        esAV = ExitStack()
        avh = esAV.enter_context(nc.sbuf_tensor("avh", [128, OC * 1024], BF16, side="right"))
        esB = ExitStack()
        kfm_g = [esB.enter_context(nc.sbuf_tensor(f"kfm_g{i}", [128, S], BF16))
                 for i in range(2)]
        qh_sb = esB.enter_context(nc.sbuf_tensor("qh_sb", [128, OWN], BF16))
        pt_sb = [esB.enter_context(nc.sbuf_tensor(f"pt{i}", [128, 1024], BF16))
                 for i in range(2)]
        avb_sb = [esB.enter_context(nc.sbuf_tensor(f"avb{i}", [128, 1024], BF16))
                  for i in range(2)]
        rb_sb = esB.enter_context(nc.sbuf_tensor("rb_sb", [128, 1024], F32))
        lr_sb = esB.enter_context(nc.sbuf_tensor("lr_sb", [1, 1024], F32))
        rl_sb = lr_sb
        nrm_sb = esB.enter_context(nc.sbuf_tensor("nrm_sb", [128, 2048], BF16))
        mx_sb = esB.enter_context(nc.sbuf_tensor("mx_sb", [1, 64], F32))
        bias_sb = esB.enter_context(nc.sbuf_tensor("bias_sb", [128, 1], F32))
        mk2_sb = esB.enter_context(nc.sbuf_tensor("mk2_sb", [1, NKV + 1], F32))

        B_sT = [ps_a[:, 0:1024], ps_a[:, 1024:2048]]
        B_av = ps_b[:, 0:1024]
        B_l0 = ps_b[0:36, 1024:1536]
        B_r = ps_b[:, 1536:2048]

        def pair_tiles(a):
            return list(range(8 * a + 7, 8 * a - 1, -1))

        def lmax(a):
            return 2 * (8 * a + 7) + 2

        def ncols(a, kc):
            return 128 * sum(1 for t in pair_tiles(a) if 2 * t + 2 > kc)

        def mask_of(a, kc):
            t = kc // 2
            if t in pair_tiles(a):
                return pair_tiles(a).index(t), (mA if kc % 2 == 0 else mB)
            return None, None

        b_base = dict(CNT)
        EVT = EvtDict()

        _pq = {"i": 0}

        def emit_pchunk(gp):
            if _pq["i"] >= len(prechunks):
                return
            sm, dst, src, i0, i1 = prechunks[_pq["i"]]
            gp.dma_start(out=dst[i0:i1],
                         in_=src[i0:i1]).then_inc(sems[sm], 16)
            _pq["i"] += 1

        def emit_B(recording, engines=None):
            EvtDict.recording = recording
            cnt = dict(b_base)

            class E:
                def wait(self2, eng_obj, kind, val):
                    if not recording and eng_obj is not None:
                        eng_obj.wait_ge(sems[kind], val)

                def bump(self2, kind, key=None, k=1):
                    cnt[kind] += k
                    if recording and key is not None:
                        EVT[key] = cnt[kind]
                    return cnt[kind]

                def sfill(self2, name, slot, instr, key=None, k=16):
                    skey = ("S", name, slot)
                    cnt[skey] += k
                    if instr is not None:
                        instr.then_inc(STREAMS[name][slot], k)
                    if recording and key is not None:
                        EVT[key] = (name, slot, cnt[skey])
                    return (name, slot, cnt[skey])

                def swait(self2, eng_obj, token):
                    if not recording and eng_obj is not None and token:
                        eng_obj.wait_ge(STREAMS[token[0]][token[1]], token[2])

            ev = E()

            tn = engines.get("tn") if engines else None
            vv = engines.get("v") if engines else None
            ss = engines.get("s") if engines else None
            sy = engines.get("sy") if engines else None
            gp = engines.get("g") if engines else None

            do_pe = recording or tn is not None
            do_dv = recording or vv is not None
            do_ac = recording or ss is not None
            do_sy = recording or sy is not None
            do_gp = recording or gp is not None

            if do_sy:
                def ld_kf(g):
                    if g == 0:
                        ev.swait(sy, kst_tok[0] if not recording else None)
                        ev.swait(sy, kst_tok[1] if not recording else None)
                    d = (sy.dma_start(out=kfm_g[g % 2][:], in_=ktm_scr[g],
                                      transpose=True)
                         if not recording else None)
                    ev.sfill("kfg", g % 2, d, ("ld_kf", g))

                ld_kf(0)
                if NKV > 1:
                    ld_kf(1)
                for h in range(NH):
                    if h > 0:
                        ev.wait(sy, "pe", EVT.get(("pe_head_end", h - 1), 0)
                                if not recording else 0)
                        if h % GRP == 0 and h // GRP + 1 < NKV:
                            ld_kf(h // GRP + 1)
                    d = (sy.dma_start(out=qh_sb[:], in_=q_scr[h])
                         if not recording else None)
                    ev.sfill("qh", 0, d, ("ld_q", h))

            for g in range(NKV):
                kslice = kfm_g[g % 2]
                nround = NSEG_K // 4 if NSEG_K % 4 == 0 else (NSEG_K + 3) // 4
                for rnd in range(nround):
                    c0 = rnd * 2048
                    c1 = min(S, c0 + 2048)
                    if do_dv:
                        if rnd == 0:
                            ev.swait(vv, EVT[("ld_kf", g)])
                        ev.wait(vv, "ac", EVT.get(("acg", g, rnd), 0))
                        ev.wait(vv, "pe", EVT.get(("peg_prev", g, rnd), 0))
                        if not recording:
                            d = vv.tensor_mul(nrm_sb[:, 0:c1 - c0],
                                              kslice[:, c0:c1], kslice[:, c0:c1])
                            d.then_inc(sems["dv"], 1)
                        ev.bump("dv", ("dv_ksq", g, rnd))
                    if do_pe:
                        ev.wait(tn, "dv", EVT[("dv_ksq", g, rnd)])
                        for sseg in range((c1 - c0) // 512):
                            if not recording:
                                mm = tn.matmul(ps_a[0:4, sseg * 512:(sseg + 1) * 512],
                                               ob_sb[:, 0:4],
                                               nrm_sb[:, sseg * 512:(sseg + 1) * 512],
                                               start=True, stop=True)
                                mm.then_inc(sems["pe"], 1)
                            ev.bump("pe", ("pe_knorm", g, rnd, sseg))
                    if do_dv:
                        ev.wait(vv, "pe", EVT[("pe_knorm", g, rnd,
                                               (c1 - c0) // 512 - 1)])
                        if not recording:
                            d = vv.reduce_max(mx_sb[0:1, 8:9],
                                              ps_a[0:1, 0:c1 - c0], axis=AX.X)
                            d.then_inc(sems["dv"], 1)
                        ev.bump("dv", ("dv_kred", g, rnd))
                        if not recording:
                            vv.wait_ge(sems["dv"], cnt["dv"])
                            if rnd == 0:
                                d = vv.tensor_copy(mk2_sb[0:1, g:g + 1],
                                                   mx_sb[0:1, 8:9])
                            else:
                                d = vv.tensor_max(mk2_sb[0:1, g:g + 1],
                                                  mk2_sb[0:1, g:g + 1],
                                                  mx_sb[0:1, 8:9])
                            d.then_inc(sems["dv"], 1)
                        ev.bump("dv", ("dv_kcmb", g, rnd))

                for hh in range(GRP):
                    h = g * GRP + hh
                    if do_dv:
                        ev.swait(vv, EVT[("ld_q", h)])
                        ev.wait(vv, "pe", EVT.get(("pe_qnorm_prev", h), 0))
                        ev.wait(vv, "ac", EVT.get(("ac_head_end", h - 1), 0))
                        if not recording:
                            d = vv.tensor_mul(nrm_sb[:, 0:OWN], qh_sb[:], qh_sb[:])
                            d.then_inc(sems["dv"], 1)
                        ev.bump("dv", ("dv_qsq", h))
                    if do_pe:
                        ev.wait(tn, "dv", EVT[("dv_qsq", h)])
                        ev.wait(tn, "ac", EVT.get(("ac_head_end", h - 1), 0))
                        for sseg in range(OWN // 512):
                            if not recording:
                                mm = tn.matmul(ps_a[0:4, sseg * 512:(sseg + 1) * 512],
                                               ob_sb[:, 0:4],
                                               nrm_sb[:, sseg * 512:(sseg + 1) * 512],
                                               start=True, stop=True)
                                mm.then_inc(sems["pe"], 1)
                            ev.bump("pe", ("pe_qnorm", h, sseg))
                        if recording:
                            EVT[("pe_qnorm_prev", h + 1)] = cnt["pe"]
                    if do_dv:
                        ev.wait(vv, "pe", EVT[("pe_qnorm", h, OWN // 512 - 1)])
                        if not recording:
                            d = vv.reduce_max(mx_sb[0:1, 0:1], ps_a[0:1, 0:OWN],
                                              axis=AX.X)
                            d.then_inc(sems["dv"], 1)
                            vv.wait_ge(sems["dv"], cnt["dv"] + 1)
                            d = vv.tensor_add(mx_sb[0:1, 1:2], mx_sb[0:1, 0:1],
                                              mk2_sb[0:1, g:g + 1])
                            d.then_inc(sems["dv"], 1)
                            vv.wait_ge(sems["dv"], cnt["dv"] + 2)
                            d = vv.tensor_scalar_mul(mx_sb[0:1, 2:3],
                                                     mx_sb[0:1, 1:2],
                                                     -0.5 * ISD)
                            d.then_inc(sems["dv"], 1)
                        ev.bump("dv")
                        ev.bump("dv")
                        ev.bump("dv", ("dv_bval", h))
                    if do_pe:
                        ev.wait(tn, "dv", EVT[("dv_bval", h)])
                        if not recording:
                            mm = tn.matmul(B_r[:, 0:1], of_sb[:, :],
                                           mx_sb[0:1, 2:3], start=True, stop=True)
                            mm.then_inc(sems["pe"], 1)
                        ev.bump("pe", ("pe_bcast", h))
                    if do_dv:
                        ev.wait(vv, "pe", EVT[("pe_bcast", h)])
                        if not recording:
                            d = vv.tensor_copy(bias_sb[:], B_r[:, 0:1])
                            d.then_inc(sems["dv"], 1)
                        ev.bump("dv", ("dv_bias", h))

                    for a in range(NPAIR):
                        L = lmax(a)

                        def emit_avl(kc):
                            N = ncols(a, kc)
                            pos, mk = mask_of(a, kc)
                            if do_pe:
                                if mk is None:
                                    ev.wait(tn, "ac", EVT[("ac_exp", h, a, kc)])
                                else:
                                    ev.wait(tn, "dv", EVT[("dv_mask", h, a, kc)])
                                if kc == 0:
                                    ev.wait(tn, "dv",
                                            EVT.get(("dv_avev", h, a - 1),
                                                    EVT.get(("dv_avev_last", h - 1),
                                                            0)))
                                first = (kc == 0)
                                last = (kc == L - 1)
                                for w0 in range(0, N, 512):
                                    w1 = min(N, w0 + 512)
                                    if not recording:
                                        tn.matmul(
                                            B_av[:, w0:w1],
                                            vtm[:, kc * KVW + g * 128:
                                                kc * KVW + (g + 1) * 128],
                                            pt_sb[kc % 2][:, w0:w1],
                                            start=first, stop=last,
                                            skip_group_check=True)
                                        lrow = (B_l0[0:4, 0:w1 - w0] if w0 == 0
                                                else B_l0[32:36, 0:w1 - w0])
                                        mm = tn.matmul(
                                            lrow, ob_sb[:, 0:4],
                                            pt_sb[kc % 2][:, w0:w1],
                                            start=first, stop=last,
                                            skip_group_check=True)
                                        mm.then_inc(sems["pe"], 1)
                                    ev.bump("pe", ("pe_avl", h, a, kc, w0))
                                if recording:
                                    EVT[("pe_avl_last", h, a, kc)] = cnt["pe"]

                        for kc in range(L):
                            N = ncols(a, kc)
                            if do_pe:
                                if kc >= 2:
                                    ev.wait(tn, "ac", EVT[("ac_exp", h, a, kc - 2)])
                                elif a > 0:
                                    ev.wait(tn, "ac", EVT[("ac_exp", h, a - 1,
                                                           lmax(a - 1) - 1)])
                                for w0 in range(0, N, 512):
                                    w1 = min(N, w0 + 512)
                                    if not recording:
                                        mm = tn.matmul(
                                            B_sT[kc % 2][:, w0:w1],
                                            kfm_g[g % 2][:, kc * 128:
                                                         (kc + 1) * 128],
                                            qh_sb[:, a * 1024 + w0:a * 1024 + w1],
                                            start=True, stop=True)
                                        mm.then_inc(sems["pe"], 1)
                                    ev.bump("pe", ("pe_sc", h, a, kc, w0))
                                if recording:
                                    EVT[("pe_sc_last", h, a, kc)] = cnt["pe"]
                            if do_ac:
                                ev.wait(ss, "pe", EVT[("pe_sc_last", h, a, kc)])
                                if kc >= 2:
                                    ev.wait(ss, "pe", EVT[("pe_avl_last", h, a,
                                                           kc - 2)])
                                elif a > 0:
                                    ev.wait(ss, "pe", EVT[("pe_avl_last", h, a - 1,
                                                           lmax(a - 1) - 1)])
                                elif h > 0:
                                    ev.wait(ss, "pe", EVT[("pe_avl_last", h - 1,
                                                           NPAIR - 1,
                                                           lmax(NPAIR - 1) - 1)])
                                ev.wait(ss, "dv", EVT[("dv_bias", h)])
                                if not recording:
                                    d = ss.activation(pt_sb[kc % 2][:, 0:N],
                                                      B_sT[kc % 2][:, 0:N],
                                                      AF.Exp, bias=bias_sb[:],
                                                      scale=ISD)
                                    d.then_inc(sems["ac"], 1)
                                ev.bump("ac", ("ac_exp", h, a, kc))
                            pos, mk = mask_of(a, kc)
                            if pos is not None and do_dv:
                                ev.wait(vv, "ac", EVT[("ac_exp", h, a, kc)])
                                if not recording:
                                    d = vv.tensor_mul(
                                        pt_sb[kc % 2][:, pos * 128:(pos + 1) * 128],
                                        pt_sb[kc % 2][:, pos * 128:(pos + 1) * 128],
                                        mk[:])
                                    d.then_inc(sems["dv"], 1)
                                ev.bump("dv", ("dv_mask", h, a, kc))
                            if kc >= 1:
                                emit_avl(kc - 1)
                        emit_avl(L - 1)

                        if do_ac:
                            ev.wait(ss, "pe", EVT[("pe_avl_last", h, a, L - 1)])
                            if not recording:
                                d = ss.activation(lr_sb[0:1, 0:512],
                                                  B_l0[0:1, 0:512], AF.Ln)
                                d.then_inc(sems["ac"], 1)
                            ev.bump("ac", ("ac_ln0", h, a))
                            if not recording:
                                d = ss.activation(lr_sb[0:1, 512:1024],
                                                  B_l0[32:33, 0:512], AF.Ln)
                                d.then_inc(sems["ac"], 1)
                            ev.bump("ac", ("ac_ln1", h, a))
                            if not recording:
                                ss.wait_ge(sems["ac"], cnt["ac"])
                                d = ss.activation(rl_sb[0:1, 0:1024],
                                                  lr_sb[0:1, 0:1024], AF.Exp,
                                                  scale=-1.0)
                                d.then_inc(sems["ac"], 1)
                            ev.bump("ac", ("ac_rinv", h, a))
                        if do_pe:
                            ev.wait(tn, "ac", EVT[("ac_rinv", h, a)])
                            ev.wait(tn, "dv", EVT.get(("dv_bias", h), 0))
                            for seg in range(2):
                                if seg == 1:
                                    ev.wait(tn, "dv", EVT[("dv_rcp", h, a, 0)])
                                if not recording:
                                    mm = tn.matmul(
                                        B_r[:, 0:512], of_sb[:, :],
                                        rl_sb[0:1, seg * 512:(seg + 1) * 512],
                                        start=True, stop=True)
                                    mm.then_inc(sems["pe"], 1)
                                ev.bump("pe", ("pe_rb", h, a, seg))
                        if do_dv:
                            for seg in range(2):
                                ev.wait(vv, "pe", EVT[("pe_rb", h, a, seg)])
                                if not recording:
                                    d = vv.tensor_copy(
                                        rb_sb[:, seg * 512:(seg + 1) * 512],
                                        B_r[:, 0:512])
                                    d.then_inc(sems["dv"], 1)
                                ev.bump("dv", ("dv_rcp", h, a, seg))
                            gpair = h * NPAIR + a
                            ev.swait(vv, EVT.get(("gp_av", gpair - 2), None))
                            if not recording:
                                vv.wait_ge(sems["dv"], cnt["dv"])
                                d = vv.tensor_mul(avb_sb[gpair % 2][:],
                                                  B_av[:, :], rb_sb[:])
                                d.then_inc(sems["dv"], 1)
                            ev.bump("dv", ("dv_avev", h, a))
                            if recording and a == NPAIR - 1:
                                EVT[("dv_avev_last", h)] = cnt["dv"]
                        if do_gp:
                            gpair = h * NPAIR + a
                            ev.wait(gp, "dv", EVT[("dv_avev", h, a)])
                            d = (gp.dma_start(
                                out=av_scr[h, :, a * 1024:(a + 1) * 1024],
                                in_=avb_sb[gpair % 2][:])
                                if not recording else None)
                            ev.sfill("avst", gpair % 2, d, ("gp_av", gpair))
                            if not recording and gp is not None \
                                    and gpair % 2 == 0:
                                emit_pchunk(gp)
                    if recording:
                        EVT[("pe_head_end", h)] = cnt["pe"]
                        EVT[("ac_head_end", h)] = cnt["ac"]
                if recording:
                    EVT[("acg", g + 1, 0)] = cnt["ac"]
                    for rnd in range(1, 8):
                        EVT[("acg", g + 1, rnd)] = cnt["ac"]
                    EVT[("peg_prev", g + 1, 0)] = cnt["pe"]
                    for rnd in range(1, 8):
                        EVT[("peg_prev", g + 1, rnd)] = cnt["pe"]
            if not recording and gp is not None:
                while _pq["i"] < len(prechunks):
                    emit_pchunk(gp)
            return cnt

        b_final = emit_B(True)

        @block.sync
        def _(sy):
            emit_B(False, {"sy": sy})

        @block.tensor
        def _(tn):
            emit_B(False, {"tn": tn})

        @block.scalar
        def _(ss):
            emit_B(False, {"s": ss})

        @block.vector
        def _(vv):
            emit_B(False, {"v": vv})

        @block.gpsimd
        def _(gp):
            emit_B(False, {"g": gp})

        for k in CNT:
            CNT[k] = b_final[k]
        b_pe_end = CNT["pe"]
        b_dv_end = CNT["dv"]

        b_ac_end = CNT["ac"]

        esB.close()
        esKV.close()

        mb1_sb = SB("mb1_sb", [128, OWN], F32)
        rb1_sb = SB("rb1_sb", [128, OWN], F32)
        esC = ExitStack()
        wob = [esC.enter_context(nc.sbuf_tensor(f"wob{i}", [128, OC * 128], BF16))
               for i in range(2)]
        xtt = [esC.enter_context(nc.sbuf_tensor(f"xtt{i}", [128, 512], F32))
               for i in range(4)]
        y1t = [esC.enter_context(nc.sbuf_tensor(f"y1t{i}", [128, 512], F32))
               for i in range(4)]
        ybf = [esC.enter_context(nc.sbuf_tensor(f"ybf{i}", [128, 512], BF16))
               for i in range(2)]
        ysq = [esC.enter_context(nc.sbuf_tensor(f"ysq{i}", [128, 512], BF16))
               for i in range(2)]
        mrow = esC.enter_context(nc.sbuf_tensor("mrow", [1, 1024], F32))
        vrow = esC.enter_context(nc.sbuf_tensor("vrow", [1, 1024], F32))
        rrow = esC.enter_context(nc.sbuf_tensor("rrow", [1, 1024], F32))

        def emit_ln_block(qh, scr_y, mb_sb_, rb_sb_, wsrc, which):
            pass

        c_base = dict(CNT)
        CEVT = EvtDict()

        def emit_C(recording, engines=None):
            EvtDict.recording = recording
            cnt = dict(c_base)

            def bump(kind, key=None, k=1):
                cnt[kind] += k
                if recording and key is not None:
                    CEVT[key] = cnt[kind]
                return cnt[kind]

            def wait(eng_obj, kind, val):
                if not recording and eng_obj is not None:
                    eng_obj.wait_ge(sems[kind], val)

            def sfill(name, slot, instr, key=None, k=16):
                skey = ("S", name, slot)
                cnt[skey] += k
                if instr is not None:
                    instr.then_inc(STREAMS[name][slot], k)
                if recording and key is not None:
                    CEVT[key] = (name, slot, cnt[skey])
                return (name, slot, cnt[skey])

            def swait(eng_obj, token):
                if not recording and eng_obj is not None and token:
                    eng_obj.wait_ge(STREAMS[token[0]][token[1]], token[2])

            tn = engines.get("tn") if engines else None
            vv = engines.get("v") if engines else None
            ss = engines.get("s") if engines else None
            sy = engines.get("sy") if engines else None
            gp = engines.get("g") if engines else None
            do_pe = recording or tn is not None
            do_dv = recording or vv is not None
            do_ac = recording or ss is not None
            do_sy = recording or sy is not None
            do_gp = recording or gp is not None

            for qh in range(NQH):
                if do_sy:
                    if qh > 0:
                        wait(sy, "pe", CEVT[("pe_proj", qh - 1, HC - 1, 1)])
                    for oc in range(OC):
                        if qh == 0:
                            for _a in range(NPAIR):
                                swait(sy, EVT[("gp_av", oc * NPAIR + _a)])
                        d = (sy.dma_start(
                            out=avh[:, oc * 1024:(oc + 1) * 1024],
                            in_=av_scr[oc, :, qh * 1024:(qh + 1) * 1024])
                            if not recording else None)
                        sfill("avh", 0, d, ("ld_av", qh, oc))
                    if qh == 0:
                        wait(sy, *cast_done["wo_bf"])
                        wait(sy, "pe", b_pe_end)
                    for hc in range(HC):
                        if hc >= 2:
                            wait(sy, "pe", CEVT[("pe_proj", qh, hc - 2, 1)])
                        d = (sy.dma_start(out=wob[hc % 2][:], in_=wo_bf[hc])
                             if not recording else None)
                        sfill("wo", hc % 2, d, ("ld_wo", qh, hc))
                        for qg in range(2):
                            ti = (hc * 2 + qg) % 4
                            jg = qh * HC * 2 + hc * 2 + qg
                            if jg >= 4:
                                swait(sy, CEVT[("gp_y1j", jg - 4)])
                            d = (sy.dma_start(
                                out=xtt[ti][:],
                                in_=xt_own[:, hc * OWN + qh * 1024 + qg * 512:
                                           hc * OWN + qh * 1024 + qg * 512 + 512])
                                if not recording else None)
                            sfill("xtt", ti, d, ("ld_xt", qh, hc, qg))

                def emit_stats(hc):
                    if not do_pe:
                        return
                    for qg in range(2):
                        wait(tn, "dv", CEVT[("dv_ysq", qh, hc, qg)])
                        for which in range(2):
                            src = (ybf, ysq)[which][(hc * 2 + qg) % 2]
                            rows = (ps_b[0:4, qg * 512:(qg + 1) * 512] if which == 0
                                    else ps_b[32:36, qg * 512:(qg + 1) * 512])
                            if not recording:
                                mm = tn.matmul(rows, ob_sb[:, 0:4], src[:],
                                               start=(hc == 0), stop=(hc == HC - 1))
                                mm.then_inc(sems["pe"], 1)
                            bump("pe", ("pe_st", qh, hc, qg, which))

                if do_pe:
                    for hc in range(HC):
                        swait(tn, CEVT[("ld_wo", qh, hc)])
                        if hc == 0:
                            swait(tn, CEVT[("ld_av", qh, OC - 1)])
                        if hc >= 2:
                            wait(tn, "dv", CEVT[("dv_evac", qh, hc - 2, 1)])
                        elif qh >= 1:
                            wait(tn, "dv", CEVT[("dv_lnb", qh - 1, 1, 1)])
                        for qg in range(2):
                            slot = (hc % 2) * 2 + qg
                            mm = None
                            for oc in range(OC):
                                if not recording:
                                    mm = tn.matmul(
                                        ps_a[:, slot * 512:(slot + 1) * 512],
                                        wob[hc % 2][:, oc * 128:(oc + 1) * 128],
                                        avh[:, oc * 1024 + qg * 512:
                                            oc * 1024 + qg * 512 + 512],
                                        start=(oc == 0), stop=(oc == OC - 1))
                            if not recording:
                                mm.then_inc(sems["pe"], 1)
                            bump("pe", ("pe_proj", qh, hc, qg))
                        if hc >= 1:
                            emit_stats(hc - 1)
                    emit_stats(HC - 1)
                    wait(tn, "dv", CEVT[("dv_lnrow", qh)])
                    for seg in range(2):
                        for which in range(2):
                            src = (mrow, rrow)[which]
                            if not recording:
                                mm = tn.matmul(
                                    ps_a[:, (seg * 2 + which) * 512:
                                         (seg * 2 + which + 1) * 512],
                                    of_sb[:, :],
                                    src[0:1, seg * 512:(seg + 1) * 512],
                                    start=True, stop=True)
                                mm.then_inc(sems["pe"], 1)
                            bump("pe", ("pe_lnb", qh, seg, which))

                if do_dv:
                    for hc in range(HC):
                        for qg in range(2):
                            ti = (hc * 2 + qg) % 4
                            wait(vv, "pe", CEVT[("pe_proj", qh, hc, qg)])
                            swait(vv, CEVT[("ld_xt", qh, hc, qg)])
                            if hc * 2 + qg >= 2:
                                wait(vv, "pe", CEVT.get(("pe_st", qh, hc - 1, qg, 1),
                                                        0))
                            slot = (hc % 2) * 2 + qg
                            if not recording:
                                d = vv.tensor_add(y1t[ti][:],
                                                  ps_a[:, slot * 512:(slot + 1) * 512],
                                                  xtt[ti][:])
                                d.then_inc(sems["dv"], 1)
                            bump("dv", ("dv_evac", qh, hc, qg))
                            if not recording:
                                vv.wait_ge(sems["dv"], cnt["dv"])
                                d = vv.tensor_copy(ybf[(hc * 2 + qg) % 2][:],
                                                   y1t[ti][:])
                                d.then_inc(sems["dv"], 1)
                            bump("dv", ("dv_ybf", qh, hc, qg))
                            if not recording:
                                vv.wait_ge(sems["dv"], cnt["dv"])
                                d = vv.tensor_mul(ysq[(hc * 2 + qg) % 2][:],
                                                  ybf[(hc * 2 + qg) % 2][:],
                                                  ybf[(hc * 2 + qg) % 2][:])
                                d.then_inc(sems["dv"], 1)
                            bump("dv", ("dv_ysq", qh, hc, qg))
                    wait(vv, "pe", CEVT[("pe_st", qh, HC - 1, 1, 1)])
                    if not recording:
                        d = vv.tensor_scalar_mul(mrow[0:1, :], ps_b[0:1, 0:1024],
                                                 1.0 / D)
                        d.then_inc(sems["dv"], 1)
                        vv.wait_ge(sems["dv"], cnt["dv"] + 1)
                        d = vv.tensor_scalar_mul(vrow[0:1, :], ps_b[32:33, 0:1024],
                                                 1.0 / D)
                        d.then_inc(sems["dv"], 1)
                        vv.wait_ge(sems["dv"], cnt["dv"] + 2)
                        d = vv.tensor_mul(rrow[0:1, :], mrow[0:1, :], mrow[0:1, :])
                        d.then_inc(sems["dv"], 1)
                        vv.wait_ge(sems["dv"], cnt["dv"] + 3)
                        d = vv.tensor_sub(vrow[0:1, :], vrow[0:1, :], rrow[0:1, :])
                        d.then_inc(sems["dv"], 1)
                    for _ in range(4):
                        bump("dv")
                    if recording:
                        CEVT[("dv_var", qh)] = cnt["dv"]
                if do_ac:
                    wait(ss, "dv", CEVT[("dv_var", qh)])
                    if not recording:
                        d = ss.activation(rrow[0:1, :], vrow[0:1, :], AF.Ln,
                                          bias=eps_sb[0:1, :])
                        d.then_inc(sems["ac"], 1)
                        ss.wait_ge(sems["ac"], cnt["ac"] + 1)
                        d = ss.activation(rrow[0:1, :], rrow[0:1, :], AF.Exp,
                                          scale=-0.5)
                        d.then_inc(sems["ac"], 1)
                    bump("ac")
                    bump("ac", ("ac_rstd", qh))
                if do_dv:
                    wait(vv, "ac", CEVT[("ac_rstd", qh)])
                    if recording:
                        CEVT[("dv_lnrow", qh)] = cnt["dv"]
                    if not recording:
                        d = vv.tensor_copy(dvm_sb[0:1, 0:1], rrow[0:1, 0:1])
                        d.then_inc(sems["dv"], 1)
                    bump("dv", ("dv_lnrow", qh))
                    for seg in range(2):
                        for which in range(2):
                            wait(vv, "pe", CEVT[("pe_lnb", qh, seg, which)])
                            dstb = (mb1_sb, rb1_sb)[which]
                            if not recording:
                                d = vv.tensor_copy(
                                    dstb[:, qh * 1024 + seg * 512:
                                         qh * 1024 + (seg + 1) * 512],
                                    ps_a[:, (seg * 2 + which) * 512:
                                         (seg * 2 + which + 1) * 512])
                                d.then_inc(sems["dv"], 1)
                            bump("dv", ("dv_lnb", qh, seg, which))

                if do_gp:
                    for hc in range(HC):
                        for qg in range(2):
                            ti = (hc * 2 + qg) % 4
                            wait(gp, "dv", CEVT[("dv_ybf", qh, hc, qg)])
                            d = (gp.dma_start(
                                out=y1_scr[:, hc * OWN + qh * 1024 + qg * 512:
                                           hc * OWN + qh * 1024 + qg * 512 + 512],
                                in_=y1t[ti][:])
                                if not recording else None)
                            tok = sfill("y1st", ti, d, ("gp_y1", qh, hc, qg))
                            if recording:
                                CEVT[("gp_y1j", qh * HC * 2 + hc * 2 + qg)] = tok
            return cnt

        c_final = emit_C(True)

        @block.sync
        def _(sy):
            emit_C(False, {"sy": sy})

        @block.tensor
        def _(tn):
            emit_C(False, {"tn": tn})

        @block.scalar
        def _(ss):
            emit_C(False, {"s": ss})

        @block.vector
        def _(vv):
            emit_C(False, {"v": vv})

        @block.gpsimd
        def _(gp):
            emit_C(False, {"g": gp})

        for k in CNT:
            CNT[k] = c_final[k]
        c_pe_end = CNT["pe"]
        c_dv_end = CNT["dv"]


        if ln1_dbg is not None:
            @block.sync
            def _(sy):
                sy.wait_ge(sems["dv"], c_dv_end)
                sfill_g("ld0", 0, sy.dma_start(out=ln1_dbg[:, 0:OWN], in_=mb1_sb[:]))
                sfill_g("ld0", 0, sy.dma_start(out=ln1_dbg[:, OWN:2 * OWN], in_=rb1_sb[:]))

        esC.close()
        esAV.close()

        esD = ExitStack()
        x1b = esD.enter_context(nc.sbuf_tensor("x1b", [128, HC * OWN], BF16))
        ytmp = [esD.enter_context(nc.sbuf_tensor(f"ytmp{i}", [128, OWN], F32))
                for i in range(2)]
        wgb = [esD.enter_context(nc.sbuf_tensor(f"wgb{i}", [128, HC * 128], BF16))
               for i in range(2)]
        wub = [esD.enter_context(nc.sbuf_tensor(f"wub{i}", [128, HC * 128], BF16))
               for i in range(2)]
        hsil = [esD.enter_context(nc.sbuf_tensor(f"hsil{i}", [128, 1024], BF16))
                for i in range(2)]
        hup = [esD.enter_context(nc.sbuf_tensor(f"hup{i}", [128, 1024], BF16))
               for i in range(2)]
        hout = [esD.enter_context(nc.sbuf_tensor(f"hout{i}", [128, 1024], BF16))
                for i in range(2)]

        d_base = dict(CNT)
        DEVT = EvtDict()

        def emit_D(recording, engines=None):
            EvtDict.recording = recording
            cnt = dict(d_base)

            def bump(kind, key=None, k=1):
                cnt[kind] += k
                if recording and key is not None:
                    DEVT[key] = cnt[kind]
                return cnt[kind]

            def wait(eng_obj, kind, val):
                if not recording and eng_obj is not None:
                    eng_obj.wait_ge(sems[kind], val)

            def sfill(name, slot, instr, key=None, k=16):
                skey = ("S", name, slot)
                cnt[skey] += k
                if instr is not None:
                    instr.then_inc(STREAMS[name][slot], k)
                if recording and key is not None:
                    DEVT[key] = (name, slot, cnt[skey])
                return (name, slot, cnt[skey])

            def swait(eng_obj, token):
                if not recording and eng_obj is not None and token:
                    eng_obj.wait_ge(STREAMS[token[0]][token[1]], token[2])

            tn = engines.get("tn") if engines else None
            vv = engines.get("v") if engines else None
            ss = engines.get("s") if engines else None
            sy = engines.get("sy") if engines else None
            gp = engines.get("g") if engines else None
            do_pe = recording or tn is not None
            do_dv = recording or vv is not None
            do_ac = recording or ss is not None
            do_sy = recording or sy is not None
            do_gp = recording or gp is not None

            if do_sy:
                swait(sy, CEVT[("gp_y1", NQH - 1, HC - 1, 0)])
                swait(sy, CEVT[("gp_y1", NQH - 1, HC - 1, 1)])
                swait(sy, CEVT[("gp_y1", NQH - 1, HC - 2, 0)])
                swait(sy, CEVT[("gp_y1", NQH - 1, HC - 2, 1)])
                wait(sy, "pe", c_pe_end)
                for hc in range(HC):
                    if hc >= 2:
                        wait(sy, "dv", DEVT[("dv_x1", hc - 2)])
                    d = (sy.dma_start(out=ytmp[hc % 2][:],
                                      in_=y1_scr[:, hc * OWN:(hc + 1) * OWN])
                         if not recording else None)
                    sfill("y1d", hc % 2, d, ("ld_y1", hc))
            if do_dv:
                for hc in range(HC):
                    swait(vv, DEVT[("ld_y1", hc)])
                    if hc == 0:
                        wait(vv, "dv", CEVT[("dv_lnb", NQH - 1, 1, 1)])
                    if not recording:
                        if hc >= 1:
                            vv.wait_ge(sems["dv"], cnt["dv"])
                        d = vv.tensor_sub(ytmp[hc % 2][:], ytmp[hc % 2][:],
                                          mb1_sb[:])
                        d.then_inc(sems["dv"], 1)
                    bump("dv")
                    if not recording:
                        vv.wait_ge(sems["dv"], cnt["dv"])
                        d = vv.tensor_mul(x1b[:, hc * OWN:(hc + 1) * OWN],
                                          ytmp[hc % 2][:], rb1_sb[:])
                        d.then_inc(sems["dv"], 1)
                    bump("dv", ("dv_x1", hc))
            if recording:
                DEVT["dv_x1_end"] = cnt["dv"]

            if do_sy:
                wait(sy, *cast_done["wg_bf"])
                wait(sy, *cast_done["wu_bf"])
                for ib in range(IC):
                    if ib >= 2:
                        wait(sy, "pe", DEVT[("pe_blk", ib - 2)])
                    d = (sy.dma_start(out=wgb[ib % 2][:], in_=wg_bf[ib])
                         if not recording else None)
                    sfill("wgu", ib % 2, d)
                    d = (sy.dma_start(out=wub[ib % 2][:], in_=wu_bf[ib])
                         if not recording else None)
                    sfill("wgu", ib % 2, d, ("ld_wu", ib))

            if do_pe:
                def d_block(ib, wv, ev):
                    if wv is None:
                        swait(tn, DEVT[("ld_wu", ib)])
                        if ib == 0:
                            wait(tn, "dv", DEVT["dv_x1_end"])
                    else:
                        if ib % 2 == 0:
                            tn.wait_ge(STREAMS["wgu"][0], wv)
                        else:
                            tn.wait_ge(STREAMS["wgu"][1], wv)
                            nc.regs_alu(wv, wv, 32, op=ALU.add)
                    for qh2 in range(NQH):
                        it = NQH * ib + qh2
                        par = it % 2
                        pg = (ps_a, ps_b)[par]
                        if ev is None:
                            if it >= 2:
                                wait(tn, "dv", DEVT[("dv_hact", it - 2)])
                        else:
                            tn.wait_ge(sems["dv"], ev)
                            nc.regs_alu(ev, ev, 2, op=ALU.add)
                        mm = None
                        for hc in range(HC):
                            for qg in range(2):
                                if not recording:
                                    mm = tn.matmul(
                                        pg[:, qg * 512:(qg + 1) * 512],
                                        wgb[ib % 2][:, hc * 128:(hc + 1) * 128],
                                        x1b[:, hc * OWN + qh2 * 1024 + qg * 512:
                                            hc * OWN + qh2 * 1024 + qg * 512 + 512],
                                        start=(hc == 0), stop=(hc == HC - 1))
                        for hc in range(HC):
                            for qg in range(2):
                                if not recording:
                                    mm = tn.matmul(
                                        pg[:, 1024 + qg * 512:1024 + (qg + 1) * 512],
                                        wub[ib % 2][:, hc * 128:(hc + 1) * 128],
                                        x1b[:, hc * OWN + qh2 * 1024 + qg * 512:
                                            hc * OWN + qh2 * 1024 + qg * 512 + 512],
                                        start=(hc == 0), stop=(hc == HC - 1))
                        if not recording:
                            mm.then_inc(sems["pe"], 1)
                        bump("pe", ("pe_dit", it))
                    if recording:
                        DEVT[("pe_blk", ib)] = cnt["pe"]

                d_block(0, None, None)
                if IC > 1:
                    d_block(1, None, None)
                npr = max(0, (IC - 2) // 2)
                if recording:
                    for ib in range(2, IC):
                        d_block(ib, None, None)
                else:
                    if npr > 0:
                        wv = nc.alloc_registers("d_wv", engines=[PE])
                        ev = nc.alloc_registers("d_ev", engines=[PE])
                        nc.regs_mov(wv, DEVT[("ld_wu", 2)][2])
                        nc.regs_mov(ev, DEVT[("dv_hact", 2 * NQH - 2)])
                        with kfori(nc, npr, PE):
                            d_block(2, wv, ev)
                            d_block(3, wv, ev)
                        for r in (wv, ev):
                            for h in r:
                                nc.free_register(h)
                    for ib in range(2 + 2 * npr, IC):
                        d_block(ib, None, None)

            for ib in range(IC):
                for qh2 in range(NQH):
                    it = NQH * ib + qh2
                    par = it % 2
                    pg = (ps_a, ps_b)[par]
                    if do_ac:
                        wait(ss, "pe", DEVT[("pe_dit", it)])
                        if it >= 2:
                            wait(ss, "dv", DEVT[("dv_hup", it - 2)])
                        if not recording:
                            d = ss.activation(hsil[par][:], pg[:, 0:1024], AF.Silu,
                                              scale=sg_sb[:, ib:ib + 1])
                            d.then_inc(sems["ac"], 1)
                        bump("ac", ("ac_sil", it))
                    if do_dv:
                        wait(vv, "pe", DEVT[("pe_dit", it)])
                        if it >= 2:
                            swait(vv, DEVT[("gp_h", it - 2)] if it >= 2 else None)
                        if not recording:
                            d = vv.tensor_scalar_mul(hup[par][:], pg[:, 1024:2048],
                                                     su_sb[:, ib:ib + 1])
                            d.then_inc(sems["dv"], 1)
                        bump("dv", ("dv_hup", it))
                        wait(vv, "ac", DEVT[("ac_sil", it)])
                        if not recording:
                            vv.wait_ge(sems["dv"], cnt["dv"])
                            d = vv.tensor_mul(hout[it % 2][:], hsil[par][:],
                                              hup[par][:])
                            d.then_inc(sems["dv"], 1)
                        bump("dv", ("dv_hact", it))
                    if do_gp:
                        wait(gp, "dv", DEVT[("dv_hact", it)])
                        d = (gp.dma_start(
                            out=hact_scr[ib, :, qh2 * 1024:(qh2 + 1) * 1024],
                            in_=hout[it % 2][:])
                            if not recording else None)
                        sfill("hst", it % 2, d, ("gp_h", it))
            return cnt

        d_final = emit_D(True)

        @block.sync
        def _(sy):
            emit_D(False, {"sy": sy})

        @block.tensor
        def _(tn):
            emit_D(False, {"tn": tn})

        @block.scalar
        def _(ss):
            emit_D(False, {"s": ss})

        @block.vector
        def _(vv):
            emit_D(False, {"v": vv})

        @block.gpsimd
        def _(gp):
            emit_D(False, {"g": gp})

        for k in CNT:
            CNT[k] = d_final[k]
        d_pe_end = CNT["pe"]


        esD.close()

        mb2_sb = SB("mb2_sb", [128, OWN], F32)
        rb2_sb = SB("rb2_sb", [128, OWN], F32)
        esE = ExitStack()
        NSL = 2 * HP4
        NIDXQ = HC // HP4
        wdb = esE.enter_context(nc.sbuf_tensor("wdb", [128, IC * HP4 * 128], BF16))
        hat = [esE.enter_context(nc.sbuf_tensor(f"hatg{i}", [128, 8 * 1024], BF16))
               for i in range(2)]
        y1e = [esE.enter_context(nc.sbuf_tensor(f"y1e{i}", [128, 512], F32))
               for i in range(4)]
        x1e = esE.enter_context(nc.sbuf_tensor("x1e", [128, 512], F32))
        y2t = [esE.enter_context(nc.sbuf_tensor(f"y2t{i}", [128, 512], F32))
               for i in range(4)]
        y2b = [esE.enter_context(nc.sbuf_tensor(f"y2b{i}", [128, 512], BF16))
               for i in range(2)]
        y2s = [esE.enter_context(nc.sbuf_tensor(f"y2s{i}", [128, 512], BF16))
               for i in range(2)]
        mrow2 = esE.enter_context(nc.sbuf_tensor("mrow2", [1, 1024], F32))
        vrow2 = esE.enter_context(nc.sbuf_tensor("vrow2", [1, 1024], F32))
        rrow2 = esE.enter_context(nc.sbuf_tensor("rrow2", [1, 1024], F32))
        macc = esE.enter_context(nc.sbuf_tensor("macc", [1, 1024], F32))
        vacc = esE.enter_context(nc.sbuf_tensor("vacc", [1, 1024], F32))
        x2f = [esE.enter_context(nc.sbuf_tensor(f"x2f{i}", [128, 1024], F32))
               for i in range(2)]
        ostg = [esE.enter_context(nc.sbuf_tensor(f"ostg{i}", [128, 128], F32))
                for i in range(2)]

        IGS = [(i0, min(IC, i0 + 8)) for i0 in range(0, IC, 8)]
        NIG = len(IGS)
        WCH = [(c0, min(IC, c0 + (IC + 3) // 4)) for c0 in
               range(0, IC, (IC + 3) // 4)]

        def psv(s):
            return (ps_a, ps_b)[s // 4][:, (s % 4) * 512:(s % 4) * 512 + 512]

        e_base = dict(CNT)
        EEVT = EvtDict()

        def emit_E(recording, engines=None):
            EvtDict.recording = recording
            cnt = dict(e_base)

            def bump(kind, key=None, k=1):
                cnt[kind] += k
                if recording and key is not None:
                    EEVT[key] = cnt[kind]
                return cnt[kind]

            def wait(eng_obj, kind, val):
                if not recording and eng_obj is not None:
                    eng_obj.wait_ge(sems[kind], val)

            def sfill(name, slot, instr, key=None, k=16):
                skey = ("S", name, slot)
                cnt[skey] += k
                if instr is not None:
                    instr.then_inc(STREAMS[name][slot], k)
                if recording and key is not None:
                    EEVT[key] = (name, slot, cnt[skey])
                return (name, slot, cnt[skey])

            def swait(eng_obj, token):
                if not recording and eng_obj is not None and token:
                    eng_obj.wait_ge(STREAMS[token[0]][token[1]], token[2])

            tn = engines.get("tn") if engines else None
            vv = engines.get("v") if engines else None
            ss = engines.get("s") if engines else None
            sy = engines.get("sy") if engines else None
            gp = engines.get("g") if engines else None
            do_pe = recording or tn is not None
            do_dv = recording or vv is not None
            do_ac = recording or ss is not None
            do_sy = recording or sy is not None
            do_gp = recording or gp is not None

            for qh in range(NQH):
                for hpr in range(NIDXQ):
                    idx = qh * NIDXQ + hpr
                    if do_sy:
                        if idx == 0:
                            wait(sy, *cast_done["wd_bf"])
                            wait(sy, "pe", d_pe_end)
                            for _s in range(2):
                                _tok = DEVT.get(("gp_h", NQH * IC - 1 - _s), None)
                                swait(sy, _tok)
                        for c, (c0, c1) in enumerate(WCH):
                            if idx > 0:
                                wait(sy, "pe", EEVT[("pe_wch", idx - 1, c)])
                            d = (sy.dma_start(
                                out=wdb[:, c0 * HP4 * 128:c1 * HP4 * 128],
                                in_=wd_bf[hpr][:, c0 * HP4 * 128:c1 * HP4 * 128])
                                if not recording else None)
                            sfill("wd", c % 4, d, ("ld_wd", idx, c))
                        for s in range(NSL):
                            jj = idx * NSL + s
                            ti = jj % 4
                            if jj >= 4:
                                wait(sy, "dv", EEVT[("dv_y2", jj - 4)])
                            hc = HP4 * hpr + s // 2
                            qg = s % 2
                            d = (sy.dma_start(
                                out=y1e[ti][:],
                                in_=y1_scr[:, hc * OWN + qh * 1024 + qg * 512:
                                           hc * OWN + qh * 1024 + qg * 512 + 512])
                                if not recording else None)
                            sfill("y1e", ti, d, ("ld_y1e", jj))

                    if do_gp:
                        if idx == 0:
                            for _s in range(2):
                                _tok = DEVT.get(("gp_h", NQH * IC - 1 - _s), None)
                                swait(gp, _tok)
                        for ig in range(NIG):
                            gi = idx * NIG + ig
                            i0, i1 = IGS[ig]
                            if gi >= 2:
                                wait(gp, "pe", EEVT[("pe_ig", gi - 2)])
                            src2 = hact_scr[i0:i1, :, qh * 1024:(qh + 1) * 1024]
                            src2 = src2.rearrange("i p q -> p i q")
                            d = (gp.dma_start(
                                out=hat[gi % 2][:, 0:(i1 - i0) * 1024]
                                .rearrange("p (i q) -> p i q", q=1024),
                                in_=src2)
                                if not recording else None)
                            sfill("hat", gi % 2, d, ("ld_ha", gi))

                    if do_pe:
                        for ig in range(NIG):
                            gi = idx * NIG + ig
                            i0, i1 = IGS[ig]
                            swait(tn, EEVT[("ld_ha", gi)])
                            mm = None
                            for ic in range(i0, i1):
                                for c, (c0, c1) in enumerate(WCH):
                                    if ic == c0:
                                        swait(tn, EEVT[("ld_wd", idx, c)])
                                for s in range(NSL):
                                    hc4 = s // 2
                                    qg = s % 2
                                    if ic == 0:
                                        if idx > 0:
                                            wait(tn, "dv",
                                                 EEVT[("dv_esta", idx - 1, s)])
                                        if hpr == 0 and qh > 0:
                                            ntr = HC * (NT // NQH)
                                            cands = [t for t in
                                                     range((qh - 1) * 0, qh * ntr)
                                                     if t % 4 == s % 4]
                                            if cands:
                                                wait(tn, "dv",
                                                     EEVT[("dv_ost", cands[-1])])
                                    if not recording:
                                        mm = tn.matmul(
                                            psv(s),
                                            wdb[:, ic * HP4 * 128 + hc4 * 128:
                                                ic * HP4 * 128 + hc4 * 128 + 128],
                                            hat[gi % 2][:, (ic - i0) * 1024
                                                        + qg * 512:
                                                        (ic - i0) * 1024
                                                        + qg * 512 + 512],
                                            start=(ic == 0), stop=(ic == IC - 1))
                                if not recording:
                                    mm.then_inc(sems["pe"], 1)
                                bump("pe", ("pe_ig", gi))
                        if recording:
                            for c in range(len(WCH)):
                                EEVT[("pe_wch", idx, c)] = cnt["pe"]
                        for s in range(NSL):
                            jj = idx * NSL + s
                            wait(tn, "dv", EEVT[("dv_y2s", jj)])
                            for which in range(2):
                                srcb = (y2b, y2s)[which][jj % 2]
                                rows = (psv(s)[0:4, :] if which == 0
                                        else psv(s)[32:36, :])
                                if not recording:
                                    mm = tn.matmul(rows, ob_sb[:, 0:4], srcb[:],
                                                   start=True, stop=True)
                                    mm.then_inc(sems["pe"], 1)
                                bump("pe", ("pe_est", jj, which))
                        if recording:
                            EEVT[("pe_idx", idx)] = cnt["pe"]

                    if do_dv:
                        for s in range(NSL):
                            jj = idx * NSL + s
                            ti = jj % 4
                            hc = HP4 * hpr + s // 2
                            wait(vv, "pe", EEVT[("pe_ig", idx * NIG + NIG - 1)])
                            if jj >= 2:
                                wait(vv, "pe", EEVT[("pe_est", jj - 2, 1)])
                            swait(vv, EEVT[("ld_y1e", jj)])
                            if jj >= 4:
                                swait(vv, EEVT[("gp_y2j", jj - 4)])
                            qg = s % 2
                            if not recording:
                                if jj > 0:
                                    vv.wait_ge(sems["dv"], cnt["dv"])
                                d = vv.tensor_sub(x1e[:], y1e[ti][:],
                                                  mb1_sb[:, qh * 1024 + qg * 512:
                                                         qh * 1024 + qg * 512
                                                         + 512])
                                d.then_inc(sems["dv"], 1)
                            bump("dv")
                            if not recording:
                                vv.wait_ge(sems["dv"], cnt["dv"])
                                d = vv.tensor_mul(x1e[:], x1e[:],
                                                  rb1_sb[:, qh * 1024 + qg * 512:
                                                         qh * 1024 + qg * 512
                                                         + 512])
                                d.then_inc(sems["dv"], 1)
                            bump("dv")
                            if not recording:
                                vv.wait_ge(sems["dv"], cnt["dv"])
                                d = vv.scalar_tensor_tensor(
                                    y2t[ti][:], psv(s), sd_sb[:, hc:hc + 1],
                                    x1e[:], ALU.mult, ALU.add)
                                d.then_inc(sems["dv"], 1)
                            bump("dv", ("dv_y2", jj))
                            if not recording:
                                vv.wait_ge(sems["dv"], cnt["dv"])
                                d = vv.tensor_copy(y2b[jj % 2][:], y2t[ti][:])
                                d.then_inc(sems["dv"], 1)
                            bump("dv", ("dv_y2b", jj))
                            if not recording:
                                vv.wait_ge(sems["dv"], cnt["dv"])
                                d = vv.tensor_mul(y2s[jj % 2][:], y2b[jj % 2][:],
                                                  y2b[jj % 2][:])
                                d.then_inc(sems["dv"], 1)
                            bump("dv", ("dv_y2s", jj))
                        for s in range(NSL):
                            jj = idx * NSL + s
                            qg = s % 2
                            col = qg * 512
                            wait(vv, "pe", EEVT[("pe_est", jj, 1)])
                            first = (hpr == 0 and s < 2)
                            if not recording:
                                vv.wait_ge(sems["dv"], cnt["dv"])
                                if first:
                                    d = vv.tensor_copy(macc[0:1, col:col + 512],
                                                       psv(s)[0:1, :])
                                else:
                                    d = vv.tensor_add(macc[0:1, col:col + 512],
                                                      macc[0:1, col:col + 512],
                                                      psv(s)[0:1, :])
                                d.then_inc(sems["dv"], 1)
                            bump("dv")
                            if not recording:
                                vv.wait_ge(sems["dv"], cnt["dv"])
                                if first:
                                    d = vv.tensor_copy(vacc[0:1, col:col + 512],
                                                       psv(s)[32:33, :])
                                else:
                                    d = vv.tensor_add(vacc[0:1, col:col + 512],
                                                      vacc[0:1, col:col + 512],
                                                      psv(s)[32:33, :])
                                d.then_inc(sems["dv"], 1)
                            bump("dv", ("dv_esta", idx, s))

                    if do_gp:
                        for s in range(NSL):
                            jj = idx * NSL + s
                            ti = jj % 4
                            hc = HP4 * hpr + s // 2
                            qg = s % 2
                            wait(gp, "dv", EEVT[("dv_y2b", jj)])
                            d = (gp.dma_start(
                                out=y2_scr[:, hc * OWN + qh * 1024 + qg * 512:
                                           hc * OWN + qh * 1024 + qg * 512 + 512],
                                in_=y2t[ti][:])
                                if not recording else None)
                            tok = sfill("y2st", jj % 4, d, ("gp_y2", qh, hc, qg))
                            if recording:
                                EEVT[("gp_y2j", jj)] = tok

                if do_dv:
                    if not recording:
                        vv.wait_ge(sems["dv"], cnt["dv"])
                        d = vv.tensor_scalar_mul(mrow2[0:1, :], macc[0:1, :],
                                                 1.0 / D)
                        d.then_inc(sems["dv"], 1)
                        vv.wait_ge(sems["dv"], cnt["dv"] + 1)
                        d = vv.tensor_scalar_mul(vrow2[0:1, :], vacc[0:1, :],
                                                 1.0 / D)
                        d.then_inc(sems["dv"], 1)
                        vv.wait_ge(sems["dv"], cnt["dv"] + 2)
                        d = vv.tensor_mul(rrow2[0:1, :], mrow2[0:1, :],
                                          mrow2[0:1, :])
                        d.then_inc(sems["dv"], 1)
                        vv.wait_ge(sems["dv"], cnt["dv"] + 3)
                        d = vv.tensor_sub(vrow2[0:1, :], vrow2[0:1, :],
                                          rrow2[0:1, :])
                        d.then_inc(sems["dv"], 1)
                    for _ in range(4):
                        bump("dv")
                    if recording:
                        EEVT[("dv_var2", qh)] = cnt["dv"]
                if do_ac:
                    wait(ss, "dv", EEVT[("dv_var2", qh)])
                    if not recording:
                        d = ss.activation(rrow2[0:1, :], vrow2[0:1, :], AF.Ln,
                                          bias=eps_sb[0:1, :])
                        d.then_inc(sems["ac"], 1)
                        ss.wait_ge(sems["ac"], cnt["ac"] + 1)
                        d = ss.activation(rrow2[0:1, :], rrow2[0:1, :], AF.Exp,
                                          scale=-0.5)
                        d.then_inc(sems["ac"], 1)
                    bump("ac")
                    bump("ac", ("ac_rstd2", qh))
                if do_dv:
                    wait(vv, "ac", EEVT[("ac_rstd2", qh)])
                    if not recording:
                        d = vv.tensor_copy(dvm_sb[0:1, 1:2], rrow2[0:1, 0:1])
                        d.then_inc(sems["dv"], 1)
                    bump("dv", ("dv_lnrow2", qh))
                if do_pe:
                    wait(tn, "dv", EEVT[("dv_lnrow2", qh)])
                    for seg in range(2):
                        for which in range(2):
                            srcr = (mrow2, rrow2)[which]
                            if not recording:
                                mm = tn.matmul(
                                    ps_a[:, (seg * 2 + which) * 512:
                                         (seg * 2 + which + 1) * 512],
                                    of_sb[:, :],
                                    srcr[0:1, seg * 512:(seg + 1) * 512],
                                    start=True, stop=True)
                                mm.then_inc(sems["pe"], 1)
                            bump("pe", ("pe_lnb2", qh, seg, which))
                if do_dv:
                    for seg in range(2):
                        for which in range(2):
                            wait(vv, "pe", EEVT[("pe_lnb2", qh, seg, which)])
                            dstb = (mb2_sb, rb2_sb)[which]
                            if not recording:
                                d = vv.tensor_copy(
                                    dstb[:, qh * 1024 + seg * 512:
                                         qh * 1024 + (seg + 1) * 512],
                                    ps_a[:, (seg * 2 + which) * 512:
                                         (seg * 2 + which + 1) * 512])
                                d.then_inc(sems["dv"], 1)
                            bump("dv", ("dv_lnb2", qh, seg, which))

                for hc in range(HC):
                    hcg = qh * HC + hc
                    if do_sy:
                        if hcg >= 2:
                            wait(sy, "dv", EEVT[("dv_x2", hcg - 2)])
                            wait(sy, "pe", EEVT[("pe_tr", hcg - 2, NT // NQH - 1)])
                        swait(sy, EEVT[("gp_y2", qh, hc, 0)])
                        swait(sy, EEVT[("gp_y2", qh, hc, 1)])
                        d = (sy.dma_start(
                            out=x2f[hcg % 2][:, 0:1024],
                            in_=y2_scr[:, hc * OWN + qh * 1024:
                                       hc * OWN + qh * 1024 + 1024])
                            if not recording else None)
                        sfill("y2f", hcg % 2, d, ("ld_y2f", hcg))
                    if do_dv:
                        swait(vv, EEVT[("ld_y2f", hcg)])
                        wait(vv, "dv", EEVT[("dv_lnb2", qh, 1, 1)])
                        if not recording:
                            vv.wait_ge(sems["dv"], cnt["dv"])
                            d = vv.tensor_sub(x2f[hcg % 2][:, 0:1024],
                                              x2f[hcg % 2][:, 0:1024],
                                              mb2_sb[:, qh * 1024:
                                                     qh * 1024 + 1024])
                            d.then_inc(sems["dv"], 1)
                        bump("dv")
                        if not recording:
                            vv.wait_ge(sems["dv"], cnt["dv"])
                            d = vv.tensor_mul(x2f[hcg % 2][:, 0:1024],
                                              x2f[hcg % 2][:, 0:1024],
                                              rb2_sb[:, qh * 1024:
                                                     qh * 1024 + 1024])
                            d.then_inc(sems["dv"], 1)
                        bump("dv", ("dv_x2", hcg))
                    for qt in range(NT // NQH):
                        qtg = qh * (NT // NQH) + qt
                        tr_i = hcg * (NT // NQH) + qt
                        slot = tr_i % 4
                        oslot = tr_i % 2
                        pvv = (ps_a, ps_b)[slot // 2][:, (slot % 2) * 512:
                                                      (slot % 2) * 512 + 128]
                        if do_pe:
                            wait(tn, "dv", EEVT[("dv_x2", hcg)])
                            if tr_i >= 4:
                                wait(tn, "dv", EEVT[("dv_ost", tr_i - 4)])
                            if hc == 0 and qt < 4:
                                wait(tn, "dv", EEVT[("dv_lnb2", qh, 1, 1)])
                            if not recording:
                                mm = tn.transpose(pvv,
                                                  x2f[hcg % 2][:, qt * 128:
                                                               (qt + 1) * 128],
                                                  idf[:])
                                mm.then_inc(sems["pe"], 1)
                            bump("pe", ("pe_tr", hcg, qt))
                        if do_dv:
                            wait(vv, "pe", EEVT[("pe_tr", hcg, qt)])
                            if tr_i >= 2:
                                swait(vv, EEVT[("gp_out", tr_i - 2)])
                            if not recording:
                                d = vv.tensor_copy(ostg[oslot][:], pvv)
                                d.then_inc(sems["dv"], 1)
                            bump("dv", ("dv_ost", tr_i))
                        if do_gp:
                            wait(gp, "dv", EEVT[("dv_ost", tr_i)])
                            d = (gp.dma_start(
                                out=out[qtg * 128:(qtg + 1) * 128,
                                        hc * 128:(hc + 1) * 128],
                                in_=ostg[oslot][:])
                                if not recording else None)
                            sfill("ost", oslot, d, ("gp_out", tr_i))
            if do_gp and not recording:
                for slot in range(2):
                    gp.wait_ge(STREAMS["ost"][slot], cnt[("S", "ost", slot)])
            return cnt

        e_final = emit_E(True)

        @block.sync
        def _(sy):
            emit_E(False, {"sy": sy})

        @block.tensor
        def _(tn):
            emit_E(False, {"tn": tn})

        @block.scalar
        def _(ss):
            emit_E(False, {"s": ss})

        @block.vector
        def _(vv):
            emit_E(False, {"v": vv})

        @block.gpsimd
        def _(gp):
            emit_E(False, {"g": gp})

        esE.close()

    return nc


def _rope_tables(S, dtype=np.float32):
    pos = np.arange(S, dtype=np.float64)[:, None]
    freqs = np.exp(-np.arange(0, 128, 2, dtype=np.float64)
                   * (math.log(ROPE_THETA) / 128.0))
    ang = pos * freqs
    return np.cos(ang).astype(dtype), np.sin(ang).astype(dtype)


def host_prep(cfg, inputs, core):
    S, D, NH, NKV, I = cfg["S"], cfg["D"], cfg["NH"], cfg["NKV"], cfg["I"]
    HC = D // 128
    ST = S // 128
    OWN = S // 2
    NT = OWN // 128
    OC = NH
    IC = I // 128
    KVW = NKV * 128
    NKVP = cfg.get("NKVP", min(4, NKV))
    KVQ = KVW // NKVP
    NGK = KVQ // 128
    KW = 2 * KVQ

    b, j = core // 2, core % 2
    x = np.asarray(inputs["L0_attn_hidden_states"][b])
    order = own_tile_order(NT)
    own_abs_tiles = [2 * t + j for t in order]
    own_cols = np.concatenate([np.arange(ta * 128, (ta + 1) * 128)
                               for ta in own_abs_tiles])

    xt = np.ascontiguousarray(x.T)
    xt_full = np.ascontiguousarray(
        xt.reshape(HC, 128, ST, 128).transpose(2, 1, 0, 3)
    ).reshape(ST, 128, HC * 128)
    xo = xt[:, own_cols]
    xt_own = np.ascontiguousarray(
        xo.reshape(HC, 128, OWN).transpose(1, 0, 2)).reshape(128, HC * OWN)

    wq = np.asarray(inputs["L0_attn_Wq"])
    wqT = np.ascontiguousarray(wq.T)
    wq_t = np.ascontiguousarray(
        wqT.reshape(HC, 128, OC // 2, 256).transpose(2, 1, 0, 3)
    ).reshape(OC // 2, 128, HC * 256)

    wk = np.asarray(inputs["L0_attn_Wk"])
    wv = np.asarray(inputs["L0_attn_Wv"])
    wkT, wvT = wk.T, wv.T
    wkv_t = np.empty((NKVP, 128, HC * KW), np.float32)
    for p in range(NKVP):
        blkk = wkT[:, p * KVQ:(p + 1) * KVQ]
        blkv = wvT[:, p * KVQ:(p + 1) * KVQ]
        cat = np.concatenate([blkk, blkv], axis=1)
        wkv_t[p] = np.ascontiguousarray(
            cat.reshape(HC, 128, KW).transpose(1, 0, 2)).reshape(128, HC * KW)

    wo = np.asarray(inputs["L0_attn_Wo"])
    woT = np.ascontiguousarray(wo.T)
    wo_t = np.ascontiguousarray(
        woT.reshape(OC, 128, HC, 128).transpose(2, 1, 0, 3)
    ).reshape(HC, 128, OC * 128)

    wg = np.asarray(inputs["L0_ffn_w_gate"])
    wu = np.asarray(inputs["L0_ffn_w_up"])
    wgT, wuT = wg.T, wu.T
    wg_t = np.ascontiguousarray(
        wgT.reshape(HC, 128, IC, 128).transpose(2, 1, 0, 3)
    ).reshape(IC, 128, HC * 128).astype(np.int32)
    wu_t = np.ascontiguousarray(
        wuT.reshape(HC, 128, IC, 128).transpose(2, 1, 0, 3)
    ).reshape(IC, 128, HC * 128).astype(np.int32)

    wd = np.asarray(inputs["L0_ffn_w_down"])
    wdT = np.ascontiguousarray(wd.T)
    HP4 = 4 if HC % 4 == 0 else 2
    wd_t = np.ascontiguousarray(
        wdT.reshape(IC, 128, HC // HP4, HP4 * 128).transpose(2, 1, 0, 3)
    ).reshape(HC // HP4, 128, IC * HP4 * 128).astype(np.int32)

    sg = np.ascontiguousarray(
        np.asarray(inputs["L0_ffn_s_gate"]).reshape(IC, 128).T)
    su = np.ascontiguousarray(
        np.asarray(inputs["L0_ffn_s_up"]).reshape(IC, 128).T)
    sd = np.ascontiguousarray(
        np.asarray(inputs["L0_ffn_s_down"]).reshape(HC, 128).T)

    cosf, sinf = _rope_tables(S)
    cosk = np.ascontiguousarray(np.tile(cosf, (1, NGK)))
    sink = np.ascontiguousarray(np.tile(sinf, (1, NGK)))
    cosq = np.ascontiguousarray(cosf[own_cols].T)
    sinq = np.ascontiguousarray(sinf[own_cols].T)

    tri = np.tril(np.ones((128, 128), np.float32))
    m_tril = np.triu(np.ones((128, 128), np.float32)).T
    m_tril = (np.arange(128)[:, None] <= np.arange(128)[None, :]).astype(np.float32)
    import ml_dtypes
    if j == 0:
        maskA = m_tril.astype(ml_dtypes.bfloat16)
        maskB = np.zeros((128, 128), ml_dtypes.bfloat16)
    else:
        maskA = np.ones((128, 128), ml_dtypes.bfloat16)
        maskB = m_tril.astype(ml_dtypes.bfloat16)

    identf = np.eye(128, dtype=np.float32)
    onesb = np.ones((128, 8), ml_dtypes.bfloat16)
    onesf = np.ones((1, 128), np.float32)

    return {
        "xt_full": xt_full, "xt_own": xt_own, "wq_t": wq_t, "wkv_t": wkv_t,
        "wo_t": wo_t, "wg_t": wg_t, "wu_t": wu_t, "wd_t": wd_t,
        "sg": sg, "su": su, "sd": sd,
        "cosk": cosk, "sink": sink, "cosq": cosq, "sinq": sinq,
        "maskA": np.asarray(maskA), "maskB": np.asarray(maskB),
        "identf": identf, "onesb": np.asarray(onesb), "onesf": onesf,
        "epsv": np.full((1, 1), LN_EPS, np.float32),
    }, own_abs_tiles


def kernel(**inputs):
    cfg = dict(CFG_FULL)
    S, D = cfg["S"], cfg["D"]
    OWN = S // 2
    NT = OWN // 128
    nc = build_program(cfg)
    in_maps = []
    tiles_per_core = []
    for core in range(8):
        m, own_abs = host_prep(cfg, inputs, core)
        in_maps.append(m)
        tiles_per_core.append(own_abs)
    res = run_bass_kernel_spmd(nc, in_maps, core_ids=list(range(8)))
    NB = cfg["NB"]
    outp = np.empty((NB, S, D), np.float32)
    for core in range(8):
        b = core // 2
        o = res.results[core]["out"]
        for idx, ta in enumerate(tiles_per_core[core]):
            outp[b, ta * 128:(ta + 1) * 128, :] = o[idx * 128:(idx + 1) * 128, :]
    return outp


# version 07
# speedup vs baseline: 1.2121x; 1.0983x over previous
import math
from contextlib import ExitStack, contextmanager

import numpy as np
import concourse.bass as bass
import concourse.mybir as mybir
from concourse.bass_utils import run_bass_kernel_spmd

F32 = mybir.dt.float32
BF16 = mybir.dt.bfloat16
I32 = mybir.dt.int32
AF = mybir.ActivationFunctionType
ALU = mybir.AluOpType
AX = mybir.AxisListType
PE = mybir.EngineType.PE

ISD = 1.0 / math.sqrt(128.0)
LN_EPS = 1e-5
ROPE_THETA = 10000.0

CFG_FULL = dict(S=4096, D=4096, NH=32, NKV=8, I=11008, NB=4)


@contextmanager
def kfori(nc, n_iters, engine_type):
    engines = [engine_type]
    loop_id = nc.next_id()
    loop_start = f"kfori_{loop_id}_loop"
    loop_end = f"kfori_{loop_id}_end"
    regs = nc.alloc_registers(f"kfori_{loop_id}_i", engines=engines)
    nc.regs_mov(regs, 0)
    nc.br(loop_start, engines=engines)
    with nc.body(loop_start, valid_engines=engines):
        yield None
        nc.regs_alu(regs, regs, 1, op=ALU.add)
        nc.br_lt(regs, n_iters, on_true=loop_start, on_false=loop_end,
                 engines=engines)
    nc.switch_bb(loop_end)
    for h in regs:
        nc.free_register(h)


class EvtDict(dict):
    recording = False

    def __missing__(self, key):
        if EvtDict.recording:
            return 0
        raise KeyError(key)


def own_tile_order(NT):
    order = []
    a = 0
    while a * 8 < NT:
        order.extend(range(8 * a + 7, 8 * a - 1, -1))
        a += 1
    return order


def build_program(cfg, debug=()):
    S, D, NH, NKV, I = cfg["S"], cfg["D"], cfg["NH"], cfg["NKV"], cfg["I"]
    HC = D // 128
    ST = S // 128
    OWN = S // 2
    NT = OWN // 128
    OC = NH
    IC = I // 128
    QGN = OWN // 512
    NPAIR = NT // 8
    GRP = NH // NKV
    assert NT % 8 == 0 and OC % 2 == 0 and HC % 2 == 0
    KVW = NKV * 128
    NKVP = cfg.get("NKVP", min(4, NKV))
    KVQ = KVW // NKVP
    NGK = KVQ // 128
    KW = 2 * KVQ
    NSEG_K = S // 512
    NQH = max(1, OWN // 1024)

    nc = bass.Bass("TRN2", target_bir_lowering=False, debug=False)

    def din(name, shape, dt):
        return nc.dram_tensor(name, shape, dt, kind="ExternalInput").ap()

    xt_full = din("xt_full", [ST, 128, HC * 128], F32)
    xt_own = din("xt_own", [128, HC * OWN], F32)
    wq_t = din("wq_t", [OC // 2, 128, HC * 256], F32)
    wkv_t = din("wkv_t", [NKVP, 128, HC * KW], F32)
    wo_t = din("wo_t", [HC, 128, OC * 128], F32)
    wg_t = din("wg_t", [IC, 128, HC * 128], I32)
    wu_t = din("wu_t", [IC, 128, HC * 128], I32)
    HP4 = 4 if HC % 4 == 0 else 2
    wd_t = din("wd_t", [HC // HP4, 128, IC * HP4 * 128], I32)
    sg_in = din("sg", [128, IC], F32)
    su_in = din("su", [128, IC], F32)
    sd_in = din("sd", [128, HC], F32)
    cosk = din("cosk", [S, NGK * 64], F32)
    sink = din("sink", [S, NGK * 64], F32)
    cosq = din("cosq", [64, OWN], F32)
    sinq = din("sinq", [64, OWN], F32)
    maskA_in = din("maskA", [128, 128], BF16)
    maskB_in = din("maskB", [128, 128], BF16)
    identf_in = din("identf", [128, 128], F32)
    onesb_in = din("onesb", [128, 8], BF16)
    onesf_in = din("onesf", [1, 128], F32)
    onesfb_in = din("onesfb", [1, 128], BF16)
    eps_in = din("epsv", [1, 1], F32)

    out = nc.dram_tensor("out", [OWN, D], F32, kind="ExternalOutput").ap()

    def scr(name, shape, dt):
        kind = "ExternalOutput" if name in debug else "Internal"
        return nc.dram_tensor(name, shape, dt, kind=kind).ap()

    wo_bf = scr("wo_bf", [HC, 128, OC * 128], BF16)
    wg_bf = scr("wg_bf", [IC, 128, HC * 128], BF16)
    wu_bf = scr("wu_bf", [IC, 128, HC * 128], BF16)
    wd_bf = scr("wd_bf", [HC // HP4, 128, IC * HP4 * 128], BF16)
    q_scr = scr("q_scr", [NH, 128, OWN], BF16)
    ktm_scr = scr("ktm_scr", [NKV, S, 128], BF16)
    av_scr = scr("av_scr", [NH, 128, OWN], BF16)
    y1_scr = scr("y1_scr", [128, HC * OWN], F32)
    hact_scr = scr("hact_scr", [IC, 128, OWN], BF16)
    y2_scr = scr("y2_scr", [128, HC * OWN], F32)
    kfm_dbg = None
    vtm_dbg = scr("vtm_dbg", [128, ST * KVW], BF16) if "vtm_dbg" in debug else None
    ln1_dbg = scr("ln1_dbg", [128, 2 * OWN], F32) if "ln1_dbg" in debug else None

    es = ExitStack()
    with es:
        SB = lambda n, s, d: es.enter_context(nc.sbuf_tensor(n, s, d))

        mA = SB("mA", [128, 128], BF16)
        mB = SB("mB", [128, 128], BF16)
        idf = SB("idf", [128, 128], F32)
        ob_sb = SB("ob_sb", [128, 8], BF16)
        of_sb = SB("of_sb", [1, 128], F32)
        of_bb = SB("of_bb", [1, 128], BF16)
        sd_sb = SB("sd_sb", [128, HC], F32)
        sg_sb = SB("sg_sb", [128, IC], F32)
        su_sb = SB("su_sb", [128, IC], F32)
        eps_sb = SB("eps_sb", [1, 1], F32)
        dvm_sb = SB("dvm_sb", [1, 8], F32)

        ps_a = es.enter_context(nc.psum_tensor("ps_a", [128, 2048], F32))
        ps_b = es.enter_context(nc.psum_tensor("ps_b", [128, 2048], F32))

        sems = {}
        for nm in ("cast", "pe", "dv", "ac", "cwo", "cwg", "cwu", "cwd"):
            sems[nm] = es.enter_context(nc.semaphore("s_" + nm))
        CNT = {k: 0 for k in sems}

        def inc(kind, instr, k=1):
            instr.then_inc(sems[kind], k)
            CNT[kind] += k
            return CNT[kind]

        STREAMS = {}

        def mkstream(name, nslots):
            STREAMS[name] = [es.enter_context(nc.semaphore(f"q_{name}{i}"))
                             for i in range(nslots)]
            for i in range(nslots):
                CNT[("S", name, i)] = 0

        for _nm, _ns in (("ld0", 1), ("xtq", 1), ("wq", 2), ("wkv", 1), ("xtb", 2),
                         ("cksk", 2), ("qh", 1), ("kfg", 2), ("wo", 2), ("xtt", 4),
                         ("y1d", 2), ("wgu", 2), ("wd", 4), ("hat", 2),
                         ("y1e", 4), ("y2f", 2), ("avh", 1),
                         ("qst", 2), ("kstst", 2), ("avst", 2), ("y1st", 4),
                         ("hst", 4), ("y2st", 4), ("ost", 4)):
            mkstream(_nm, _ns)

        def sfill_g(name, slot, instr, k=16):
            key = ("S", name, slot)
            CNT[key] += k
            instr.then_inc(STREAMS[name][slot], k)
            return (name, slot, CNT[key])

        def swait_g(eng_obj, token):
            if token:
                name, slot, v = token
                eng_obj.wait_ge(STREAMS[name][slot], v)

        block = es.enter_context(nc.Block())

        esA = ExitStack()
        xtq = esA.enter_context(nc.sbuf_tensor("xtq", [128, HC * OWN], BF16))
        wqb = [esA.enter_context(nc.sbuf_tensor(f"wqb{i}", [128, HC * 256], BF16))
               for i in range(2)]
        qstage = [esA.enter_context(nc.sbuf_tensor(f"qstage{i}", [128, OWN], BF16))
                  for i in range(2)]
        ropeq = esA.enter_context(nc.sbuf_tensor("ropeq", [64, 2 * OWN], F32))
        cq_sb = esA.enter_context(nc.sbuf_tensor("cq_sb", [64, OWN], F32))
        sq_sb = esA.enter_context(nc.sbuf_tensor("sq_sb", [64, OWN], F32))
        AQ_XTQ = [xtq[:]]
        AQ_WQB = wqb

        precast_list = [
            ("wo_bf", "cwo", wo_bf, wo_t, 4),
            ("wg_bf", "cwg", wg_bf, wg_t, 8),
            ("wu_bf", "cwu", wu_bf, wu_t, 8),
            ("wd_bf", "cwd", wd_bf, wd_t, 8),
        ]
        prechunks = []
        cast_done = {}
        for _name, _sem, _dst, _src, _nch in precast_list:
            _n0 = _src.shape[0]
            _step = max(1, -(-_n0 // _nch))
            _cc = 0
            for _i0 in range(0, _n0, _step):
                prechunks.append((_sem, _dst, _src, _i0, min(_n0, _i0 + _step)))
                _cc += 16
            cast_done[_name] = (_sem, _cc)

        @block.gpsimd
        def _(g):
            sfill_g("xtq", 0, g.dma_start(out=AQ_XTQ[0], in_=xt_own[:]))
            for ob in range(OC // 2):
                if ob >= 2:
                    g.wait_ge(sems["pe"], (ob - 2) * 2 + 2)
                sfill_g("wq", ob % 2, g.dma_start(out=AQ_WQB[ob % 2][:],
                                                  in_=wq_t[ob]))

        @block.sync
        def _(sy):
            for dst, src in ((mA[:], maskA_in), (mB[:], maskB_in),
                             (idf[:], identf_in), (ob_sb[:], onesb_in),
                             (of_sb[:], onesf_in), (of_bb[:], onesfb_in),
                             (eps_sb[:], eps_in),
                             (sd_sb[:], sd_in),
                             (sg_sb[:], sg_in), (su_sb[:], su_in)):
                sfill_g("ld0", 0, sy.dma_start(out=dst, in_=src))


        @block.sync
        def _(sy):
            sfill_g("ld0", 0, sy.dma_start(out=cq_sb[:], in_=cosq))
            sfill_g("ld0", 0, sy.dma_start(out=sq_sb[:], in_=sinq))
        aq_ld0_tok = ("ld0", 0, CNT[("S", "ld0", 0)])



        @block.tensor
        def _(tn):
            def aq_block(ob, wv, ev):
                if wv is None:
                    if ob < 2:
                        swait_g(tn, aq_ld0_tok)
                        swait_g(tn, ("xtq", 0, 16))
                    swait_g(tn, ("wq", ob % 2, 16 * (ob // 2 + 1)))
                else:
                    if ob % 2 == 0:
                        tn.wait_ge(STREAMS["wq"][0], wv)
                    else:
                        tn.wait_ge(STREAMS["wq"][1], wv)
                        nc.regs_alu(wv, wv, 16, op=ALU.add)
                for oc in range(2):
                    gidx = 2 * ob + oc
                    pg = (ps_a, ps_b)[oc]
                    if ev is None:
                        if gidx >= 2:
                            tn.wait_ge(sems["dv"], 4 * (gidx - 2) + 3)
                    else:
                        tn.wait_ge(sems["dv"], ev)
                        nc.regs_alu(ev, ev, 4, op=ALU.add)
                    mm = None
                    for hc in range(HC):
                        for qg in range(QGN):
                            mm = tn.matmul(
                                pg[:, qg * 512:(qg + 1) * 512],
                                wqb[ob % 2][:, hc * 256 + oc * 128:
                                            hc * 256 + oc * 128 + 128],
                                xtq[:, hc * OWN + qg * 512:
                                    hc * OWN + (qg + 1) * 512],
                                start=(hc == 0), stop=(hc == HC - 1))
                    inc("pe", mm, 1)

            NOB = OC // 2
            aq_block(0, None, None)
            if NOB > 1:
                aq_block(1, None, None)
            npairs = max(0, (NOB - 2) // 2)
            if npairs > 0:
                wv = nc.alloc_registers("aq_wv", engines=[PE])
                ev = nc.alloc_registers("aq_ev", engines=[PE])
                nc.regs_mov(wv, 16 * 2)
                nc.regs_mov(ev, 11)
                pe_before = CNT["pe"]
                with kfori(nc, npairs, PE):
                    aq_block(2, wv, ev)
                    aq_block(3, wv, ev)
                CNT["pe"] = pe_before + 4 * npairs
                for r in (wv, ev):
                    for h in r:
                        nc.free_register(h)
            for ob in range(2 + 2 * npairs, NOB):
                aq_block(ob, None, None)

        aq_pe_end = CNT["pe"]
        assert aq_pe_end == OC, (aq_pe_end, OC)

        T1 = ropeq[:, 0 * OWN:1 * OWN]
        T2 = ropeq[:, 1 * OWN:2 * OWN]

        @block.vector
        def _(v):
            for gidx in range(OC):
                pg = (ps_a, ps_b)[gidx % 2]
                qs = qstage[gidx % 2]
                v.wait_ge(sems["pe"], gidx + 1)
                if gidx >= 2:
                    swait_g(v, ("qst", gidx % 2, 16 * ((gidx - 2) // 2 + 1)))
                if gidx >= 1:
                    v.wait_ge(sems["dv"], 4 * gidx)
                v.tensor_mul(T1, pg[0:64, 0:OWN], cq_sb[:])
                m2 = v.tensor_mul(T2, pg[64:128, 0:OWN], sq_sb[:])
                inc("dv", m2, 1)
                v.wait_ge(sems["dv"], CNT["dv"])
                c = v.tensor_sub(qs[0:64, :], T1, T2)
                inc("dv", c, 1)
                v.wait_ge(sems["dv"], CNT["dv"])
                v.tensor_mul(T1, pg[64:128, 0:OWN], cq_sb[:])
                m4 = v.tensor_mul(T2, pg[0:64, 0:OWN], sq_sb[:])
                inc("dv", m4, 1)
                v.wait_ge(sems["dv"], CNT["dv"])
                a2 = v.tensor_add(qs[64:128, :], T1, T2)
                inc("dv", a2, 1)

        aq_dv_end = CNT["dv"]

        @block.sync
        def _(sy):
            for gidx in range(OC):
                sy.wait_ge(sems["dv"], 4 * gidx + 4)
                sfill_g("qst", gidx % 2,
                        sy.dma_start(out=q_scr[gidx], in_=qstage[gidx % 2][:]))

        esA.close()

        esKV = ExitStack()
        vtm = esKV.enter_context(nc.sbuf_tensor("vtm", [128, ST * KVW], BF16))
        esK = ExitStack()
        wkvb = esK.enter_context(nc.sbuf_tensor("wkvb", [128, HC * KW], BF16))
        xtb = [esK.enter_context(nc.sbuf_tensor(f"xtb{i}", [128, HC * 128], BF16))
               for i in range(2)]
        kst = [esK.enter_context(nc.sbuf_tensor(f"kst{i}", [128, KVQ], BF16))
               for i in range(2)]
        ck_sb = [esK.enter_context(nc.sbuf_tensor(f"ck{i}", [128, NGK * 64], F32))
                 for i in range(2)]
        sk_sb = [esK.enter_context(nc.sbuf_tensor(f"sk{i}", [128, NGK * 64], F32))
                 for i in range(2)]
        kvt = esK.enter_context(nc.sbuf_tensor("kvt", [128, 4 * KVQ], F32))

        @block.sync
        def _(sy):
            sy.wait_ge(sems["pe"], aq_pe_end)
            sy.wait_ge(sems["dv"], aq_dv_end)
            swait_g(sy, ("qst", 0, 16 * ((OC + 1) // 2)))
            swait_g(sy, ("qst", 1, 16 * (OC // 2)))
            for p in range(NKVP):
                for st in range(ST):
                    k = p * ST + st
                    if k >= 2:
                        sy.wait_ge(sems["dv"], aq_dv_end + 2 * k - 2)
                    sfill_g("cksk", k % 2,
                            sy.dma_start(out=ck_sb[k % 2][:],
                                         in_=cosk[st * 128:(st + 1) * 128, :]))
                    sfill_g("cksk", k % 2,
                            sy.dma_start(out=sk_sb[k % 2][:],
                                         in_=sink[st * 128:(st + 1) * 128, :]))

        @block.tensor
        def _(tn):
            def akv_tile(p, st, wv, ev):
                k = p * ST + st
                pg = (ps_a, ps_b)[k % 2]
                if wv is None:
                    swait_g(tn, ("wkv", 0, 16 * (p + 1)))
                    swait_g(tn, ("xtb", k % 2, 16 * (k // 2 + 1)))
                else:
                    if k % 2 == 0:
                        tn.wait_ge(STREAMS["xtb"][0], wv)
                    else:
                        tn.wait_ge(STREAMS["xtb"][1], wv)
                        nc.regs_alu(wv, wv, 16, op=ALU.add)
                if ev is None:
                    if k >= 2:
                        tn.wait_ge(sems["dv"], aq_dv_end + 2 * (k - 2) + 1)
                else:
                    tn.wait_ge(sems["dv"], ev)
                    nc.regs_alu(ev, ev, 2, op=ALU.add)
                mm = None
                for hc in range(HC):
                    for w0 in range(0, KW, 512):
                        w1 = min(KW, w0 + 512)
                        mm = tn.matmul(
                            pg[:, w0:w1],
                            xtb[k % 2][:, hc * 128:(hc + 1) * 128],
                            wkvb[:, hc * KW + w0: hc * KW + w1],
                            start=(hc == 0), stop=(hc == HC - 1))
                inc("pe", mm, 1)

            for p in range(NKVP):
                akv_tile(p, 0, None, None)
                if ST > 1:
                    akv_tile(p, 1, None, None)
                nst = max(0, (ST - 2) // 2)
                if nst > 0:
                    wv = nc.alloc_registers(f"akv_wv{p}", engines=[PE])
                    ev = nc.alloc_registers(f"akv_ev{p}", engines=[PE])
                    nc.regs_mov(wv, 16 * (p * ST // 2 + 2))
                    nc.regs_mov(ev, aq_dv_end + 2 * (p * ST) + 1)
                    pe_before = CNT["pe"]
                    tn.wait_ge(STREAMS["wkv"][0], 16 * (p + 1))
                    with kfori(nc, nst, PE):
                        akv_tile(p, 2, wv, ev)
                        akv_tile(p, 3, wv, ev)
                    CNT["pe"] = pe_before + 2 * nst
                    for r in (wv, ev):
                        for h in r:
                            nc.free_register(h)
                for st in range(2 + 2 * max(0, (ST - 2) // 2), ST):
                    akv_tile(p, st, None, None)

        akv_pe_end = CNT["pe"]
        assert akv_pe_end == aq_pe_end + NKVP * ST

        @block.vector
        def _(v):
            for p in range(NKVP):
                for st in range(ST):
                    k = p * ST + st
                    pg = (ps_a, ps_b)[k % 2]
                    ks = kst[k % 2]
                    v.wait_ge(sems["pe"], aq_pe_end + k + 1)
                    swait_g(v, ("cksk", k % 2, 32 * (k // 2 + 1)))
                    if k >= 2:
                        swait_g(v, ("kstst", k % 2,
                                    16 * NGK * ((k - 2) // 2 + 1)))
                    if k >= 1:
                        v.wait_ge(sems["dv"], aq_dv_end + 2 * k)
                    v.tensor_copy(vtm[:, st * KVW + p * KVQ:
                                      st * KVW + (p + 1) * KVQ],
                                  pg[:, KVQ:KW])
                    pk = pg[:, 0:KVQ].rearrange("p (g d) -> p g d", d=128)
                    t = kvt[:, 0:4 * NGK * 64].rearrange("p (i g d) -> p i g d", i=4, d=64)
                    ck3 = ck_sb[k % 2][:, :].rearrange("p (g d) -> p g d", d=64)
                    sk3 = sk_sb[k % 2][:, :].rearrange("p (g d) -> p g d", d=64)
                    ko = ks[:, :].rearrange("p (g d) -> p g d", d=128)
                    v.tensor_mul(t[:, 0], pk[:, :, 0:64], ck3)
                    v.tensor_mul(t[:, 1], pk[:, :, 64:128], sk3)
                    v.tensor_mul(t[:, 2], pk[:, :, 64:128], ck3)
                    m4 = v.tensor_mul(t[:, 3], pk[:, :, 0:64], sk3)
                    inc("dv", m4, 1)
                    v.wait_ge(sems["dv"], CNT["dv"])
                    v.tensor_sub(ko[:, :, 0:64], t[:, 0], t[:, 1])
                    a2 = v.tensor_add(ko[:, :, 64:128], t[:, 2], t[:, 3])
                    inc("dv", a2, 1)

        akv_dv_end = CNT["dv"]

        @block.gpsimd
        def _(g):
            def kst_store(g, j):
                pj, stj = divmod(j, ST)
                for gg in range(NGK):
                    gabs = pj * NGK + gg
                    sfill_g("kstst", j % 2, g.dma_start(
                        out=ktm_scr[gabs, stj * 128:(stj + 1) * 128, :],
                        in_=kst[j % 2][:, gg * 128:(gg + 1) * 128]))

            g.wait_ge(sems["pe"], aq_pe_end)
            g.wait_ge(sems["dv"], aq_dv_end)
            swait_g(g, ("qst", 0, 16 * ((OC + 1) // 2)))
            swait_g(g, ("qst", 1, 16 * (OC // 2)))
            for p in range(NKVP):
                if p > 0:
                    g.wait_ge(sems["pe"], aq_pe_end + p * ST)
                sfill_g("wkv", 0, g.dma_start(out=wkvb[:], in_=wkv_t[p]))
                for st in range(ST):
                    k = p * ST + st
                    if k >= 2:
                        g.wait_ge(sems["pe"], aq_pe_end + k - 1)
                        g.wait_ge(sems["dv"], aq_dv_end + 2 * k - 2)
                    sfill_g("xtb", k % 2,
                            g.dma_start(out=xtb[k % 2][:], in_=xt_full[st]))
                    if k >= 2:
                        kst_store(g, k - 2)
            for j in (NKVP * ST - 2, NKVP * ST - 1):
                if j >= 0:
                    g.wait_ge(sems["dv"], aq_dv_end + 2 * j + 2)
                    kst_store(g, j)
        kst_tok = [("kstst", i, CNT[("S", "kstst", i)]) for i in range(2)]

        akv_kfm_tok = None

        esK.close()

        if vtm_dbg is not None:
            @block.sync
            def _(sy):
                swait_g(sy, kst_tok[0])
                swait_g(sy, kst_tok[1])
                sy.wait_ge(sems["dv"], akv_dv_end)
                sfill_g("ld0", 0, sy.dma_start(out=vtm_dbg, in_=vtm[:]))
                sy.wait_ge(STREAMS["ld0"][0], CNT[("S", "ld0", 0)])

        esAV = ExitStack()
        avh = esAV.enter_context(nc.sbuf_tensor("avh", [128, OC * 1024], BF16, side="right"))
        esB = ExitStack()
        kfm_g = [esB.enter_context(nc.sbuf_tensor(f"kfm_g{i}", [128, S], BF16))
                 for i in range(2)]
        qh_sb = esB.enter_context(nc.sbuf_tensor("qh_sb", [128, OWN], BF16))
        pt_sb = [esB.enter_context(nc.sbuf_tensor(f"pt{i}", [128, 1024], BF16))
                 for i in range(2)]
        avb_sb = [esB.enter_context(nc.sbuf_tensor(f"avb{i}", [128, 1024], BF16))
                  for i in range(2)]
        rb_sb = esB.enter_context(nc.sbuf_tensor("rb_sb", [128, 1024], F32))
        lr_sb = esB.enter_context(nc.sbuf_tensor("lr_sb", [1, 1024], F32))
        rl_bb = esB.enter_context(nc.sbuf_tensor("rl_bb", [1, 1024], BF16))
        nrm_sb = esB.enter_context(nc.sbuf_tensor("nrm_sb", [128, 2048], BF16))
        mx_sb = esB.enter_context(nc.sbuf_tensor("mx_sb", [1, 64], F32))
        bias_sb = esB.enter_context(nc.sbuf_tensor("bias_sb", [128, 1], F32))
        mk2_sb = esB.enter_context(nc.sbuf_tensor("mk2_sb", [1, NKV + 1], F32))

        B_sT = [ps_a[:, 0:1024], ps_a[:, 1024:2048]]
        B_av = ps_b[:, 0:1024]
        B_l0 = ps_b[0:36, 1024:1536]
        B_r = ps_b[:, 1536:2048]

        def pair_tiles(a):
            return list(range(8 * a + 7, 8 * a - 1, -1))

        def lmax(a):
            return 2 * (8 * a + 7) + 2

        def ncols(a, kc):
            return 128 * sum(1 for t in pair_tiles(a) if 2 * t + 2 > kc)

        def mask_of(a, kc):
            t = kc // 2
            if t in pair_tiles(a):
                return pair_tiles(a).index(t), (mA if kc % 2 == 0 else mB)
            return None, None

        b_base = dict(CNT)
        EVT = EvtDict()

        _pq = {"i": 0}

        def emit_pchunk(gp):
            if _pq["i"] >= len(prechunks):
                return
            sm, dst, src, i0, i1 = prechunks[_pq["i"]]
            gp.dma_start(out=dst[i0:i1],
                         in_=src[i0:i1]).then_inc(sems[sm], 16)
            _pq["i"] += 1

        def emit_B(recording, engines=None):
            EvtDict.recording = recording
            cnt = dict(b_base)

            class E:
                def wait(self2, eng_obj, kind, val):
                    if not recording and eng_obj is not None:
                        eng_obj.wait_ge(sems[kind], val)

                def bump(self2, kind, key=None, k=1):
                    cnt[kind] += k
                    if recording and key is not None:
                        EVT[key] = cnt[kind]
                    return cnt[kind]

                def sfill(self2, name, slot, instr, key=None, k=16):
                    skey = ("S", name, slot)
                    cnt[skey] += k
                    if instr is not None:
                        instr.then_inc(STREAMS[name][slot], k)
                    if recording and key is not None:
                        EVT[key] = (name, slot, cnt[skey])
                    return (name, slot, cnt[skey])

                def swait(self2, eng_obj, token):
                    if not recording and eng_obj is not None and token:
                        eng_obj.wait_ge(STREAMS[token[0]][token[1]], token[2])

            ev = E()

            tn = engines.get("tn") if engines else None
            vv = engines.get("v") if engines else None
            ss = engines.get("s") if engines else None
            sy = engines.get("sy") if engines else None
            gp = engines.get("g") if engines else None

            do_pe = recording or tn is not None
            do_dv = recording or vv is not None
            do_ac = recording or ss is not None
            do_sy = recording or sy is not None
            do_gp = recording or gp is not None

            if do_sy:
                def ld_kf(g):
                    if g == 0:
                        ev.swait(sy, kst_tok[0] if not recording else None)
                        ev.swait(sy, kst_tok[1] if not recording else None)
                    d = (sy.dma_start(out=kfm_g[g % 2][:], in_=ktm_scr[g],
                                      transpose=True)
                         if not recording else None)
                    ev.sfill("kfg", g % 2, d, ("ld_kf", g))

                ld_kf(0)
                if NKV > 1:
                    ld_kf(1)
                for h in range(NH):
                    if h > 0:
                        ev.wait(sy, "pe", EVT.get(("pe_head_end", h - 1), 0)
                                if not recording else 0)
                        if h % GRP == 0 and h // GRP + 1 < NKV:
                            ld_kf(h // GRP + 1)
                    d = (sy.dma_start(out=qh_sb[:], in_=q_scr[h])
                         if not recording else None)
                    ev.sfill("qh", 0, d, ("ld_q", h))

            pendq = []

            def tail_part(ph, pa, seg):
                def go():
                    ev.wait(tn, "ac", EVT[("ac_rinv", ph, pa)])
                    if seg == 1:
                        ev.wait(tn, "dv", EVT[("dv_rcp", ph, pa, 0)])
                    if not recording:
                        mm = tn.matmul(B_r[:, 0:512], of_bb[:, :],
                                       rl_bb[0:1, seg * 512:(seg + 1) * 512],
                                       start=True, stop=True)
                        mm.then_inc(sems["pe"], 1)
                    ev.bump("pe", ("pe_rb", ph, pa, seg))
                return go

            def flush_one():
                if pendq:
                    pendq.pop(0)()

            for g in range(NKV):
                kslice = kfm_g[g % 2]
                nround = NSEG_K // 4 if NSEG_K % 4 == 0 else (NSEG_K + 3) // 4
                for rnd in range(nround):
                    c0 = rnd * 2048
                    c1 = min(S, c0 + 2048)
                    if do_dv:
                        if rnd == 0:
                            ev.swait(vv, EVT[("ld_kf", g)])
                        ev.wait(vv, "ac", EVT.get(("acg", g, rnd), 0))
                        ev.wait(vv, "pe", EVT.get(("peg_prev", g, rnd), 0))
                        if not recording:
                            d = vv.tensor_mul(nrm_sb[:, 0:c1 - c0],
                                              kslice[:, c0:c1], kslice[:, c0:c1])
                            d.then_inc(sems["dv"], 1)
                        ev.bump("dv", ("dv_ksq", g, rnd))
                    if do_pe:
                        if rnd == 0:
                            flush_one()
                            flush_one()
                        ev.wait(tn, "dv", EVT[("dv_ksq", g, rnd)])
                        for sseg in range((c1 - c0) // 512):
                            if not recording:
                                mm = tn.matmul(ps_a[0:4, sseg * 512:(sseg + 1) * 512],
                                               ob_sb[:, 0:4],
                                               nrm_sb[:, sseg * 512:(sseg + 1) * 512],
                                               start=True, stop=True)
                                mm.then_inc(sems["pe"], 1)
                            ev.bump("pe", ("pe_knorm", g, rnd, sseg))
                    if do_dv:
                        ev.wait(vv, "pe", EVT[("pe_knorm", g, rnd,
                                               (c1 - c0) // 512 - 1)])
                        if not recording:
                            d = vv.reduce_max(mx_sb[0:1, 8:9],
                                              ps_a[0:1, 0:c1 - c0], axis=AX.X)
                            d.then_inc(sems["dv"], 1)
                        ev.bump("dv", ("dv_kred", g, rnd))
                        if not recording:
                            vv.wait_ge(sems["dv"], cnt["dv"])
                            if rnd == 0:
                                d = vv.tensor_copy(mk2_sb[0:1, g:g + 1],
                                                   mx_sb[0:1, 8:9])
                            else:
                                d = vv.tensor_max(mk2_sb[0:1, g:g + 1],
                                                  mk2_sb[0:1, g:g + 1],
                                                  mx_sb[0:1, 8:9])
                            d.then_inc(sems["dv"], 1)
                        ev.bump("dv", ("dv_kcmb", g, rnd))

                for hh in range(GRP):
                    h = g * GRP + hh
                    if do_dv:
                        ev.swait(vv, EVT[("ld_q", h)])
                        ev.wait(vv, "pe", EVT.get(("pe_qnorm_prev", h), 0))
                        ev.wait(vv, "ac", EVT.get(("ac_head_end", h - 1), 0))
                        if not recording:
                            d = vv.tensor_mul(nrm_sb[:, 0:OWN], qh_sb[:], qh_sb[:])
                            d.then_inc(sems["dv"], 1)
                        ev.bump("dv", ("dv_qsq", h))
                    if do_pe:
                        flush_one()
                        flush_one()
                        ev.wait(tn, "dv", EVT[("dv_qsq", h)])
                        ev.wait(tn, "ac", EVT.get(("ac_head_end", h - 1), 0))
                        for sseg in range(OWN // 512):
                            if not recording:
                                mm = tn.matmul(ps_a[0:4, sseg * 512:(sseg + 1) * 512],
                                               ob_sb[:, 0:4],
                                               nrm_sb[:, sseg * 512:(sseg + 1) * 512],
                                               start=True, stop=True)
                                mm.then_inc(sems["pe"], 1)
                            ev.bump("pe", ("pe_qnorm", h, sseg))
                        if recording:
                            EVT[("pe_qnorm_prev", h + 1)] = cnt["pe"]
                    if do_dv:
                        ev.wait(vv, "pe", EVT[("pe_qnorm", h, OWN // 512 - 1)])
                        if not recording:
                            d = vv.reduce_max(mx_sb[0:1, 0:1], ps_a[0:1, 0:OWN],
                                              axis=AX.X)
                            d.then_inc(sems["dv"], 1)
                            vv.wait_ge(sems["dv"], cnt["dv"] + 1)
                            d = vv.tensor_add(mx_sb[0:1, 1:2], mx_sb[0:1, 0:1],
                                              mk2_sb[0:1, g:g + 1])
                            d.then_inc(sems["dv"], 1)
                            vv.wait_ge(sems["dv"], cnt["dv"] + 2)
                            d = vv.tensor_scalar_mul(mx_sb[0:1, 2:3],
                                                     mx_sb[0:1, 1:2],
                                                     -0.5 * ISD)
                            d.then_inc(sems["dv"], 1)
                        ev.bump("dv")
                        ev.bump("dv")
                        ev.bump("dv", ("dv_bval", h))
                    if do_pe:
                        ev.wait(tn, "dv", EVT[("dv_bval", h)])
                        if not recording:
                            mm = tn.matmul(B_r[:, 0:1], of_sb[:, :],
                                           mx_sb[0:1, 2:3], start=True, stop=True)
                            mm.then_inc(sems["pe"], 1)
                        ev.bump("pe", ("pe_bcast", h))
                    if do_dv:
                        ev.wait(vv, "pe", EVT[("pe_bcast", h)])
                        if not recording:
                            d = vv.tensor_copy(bias_sb[:], B_r[:, 0:1])
                            d.then_inc(sems["dv"], 1)
                        ev.bump("dv", ("dv_bias", h))

                    for a in range(NPAIR):
                        L = lmax(a)

                        def emit_avl(kc):
                            N = ncols(a, kc)
                            pos, mk = mask_of(a, kc)
                            if do_pe:
                                if mk is None:
                                    ev.wait(tn, "ac", EVT[("ac_exp", h, a, kc)])
                                else:
                                    ev.wait(tn, "dv", EVT[("dv_mask", h, a, kc)])
                                if kc == 0:
                                    ev.wait(tn, "dv",
                                            EVT.get(("dv_avev", h, a - 1),
                                                    EVT.get(("dv_avev_last", h - 1),
                                                            0)))
                                first = (kc == 0)
                                last = (kc == L - 1)
                                for w0 in range(0, N, 512):
                                    w1 = min(N, w0 + 512)
                                    if not recording:
                                        tn.matmul(
                                            B_av[:, w0:w1],
                                            vtm[:, kc * KVW + g * 128:
                                                kc * KVW + (g + 1) * 128],
                                            pt_sb[kc % 2][:, w0:w1],
                                            start=first, stop=last,
                                            skip_group_check=True)
                                        lrow = (B_l0[0:4, 0:w1 - w0] if w0 == 0
                                                else B_l0[32:36, 0:w1 - w0])
                                        mm = tn.matmul(
                                            lrow, ob_sb[:, 0:4],
                                            pt_sb[kc % 2][:, w0:w1],
                                            start=first, stop=last,
                                            skip_group_check=True)
                                        mm.then_inc(sems["pe"], 1)
                                    ev.bump("pe", ("pe_avl", h, a, kc, w0))
                                if recording:
                                    EVT[("pe_avl_last", h, a, kc)] = cnt["pe"]

                        for kc in range(L):
                            N = ncols(a, kc)
                            if do_pe:
                                if kc >= 2:
                                    ev.wait(tn, "ac", EVT[("ac_exp", h, a, kc - 2)])
                                elif a > 0:
                                    ev.wait(tn, "ac", EVT[("ac_exp", h, a - 1,
                                                           lmax(a - 1) - 1)])
                                for w0 in range(0, N, 512):
                                    w1 = min(N, w0 + 512)
                                    if not recording:
                                        mm = tn.matmul(
                                            B_sT[kc % 2][:, w0:w1],
                                            kfm_g[g % 2][:, kc * 128:
                                                         (kc + 1) * 128],
                                            qh_sb[:, a * 1024 + w0:a * 1024 + w1],
                                            start=True, stop=True)
                                        mm.then_inc(sems["pe"], 1)
                                    ev.bump("pe", ("pe_sc", h, a, kc, w0))
                                if recording:
                                    EVT[("pe_sc_last", h, a, kc)] = cnt["pe"]
                                if kc <= 1:
                                    flush_one()
                            if do_ac:
                                ev.wait(ss, "pe", EVT[("pe_sc_last", h, a, kc)])
                                if kc >= 2:
                                    ev.wait(ss, "pe", EVT[("pe_avl_last", h, a,
                                                           kc - 2)])
                                elif a > 0:
                                    ev.wait(ss, "pe", EVT[("pe_avl_last", h, a - 1,
                                                           lmax(a - 1) - 1)])
                                elif h > 0:
                                    ev.wait(ss, "pe", EVT[("pe_avl_last", h - 1,
                                                           NPAIR - 1,
                                                           lmax(NPAIR - 1) - 1)])
                                ev.wait(ss, "dv", EVT[("dv_bias", h)])
                                if not recording:
                                    d = ss.activation(pt_sb[kc % 2][:, 0:N],
                                                      B_sT[kc % 2][:, 0:N],
                                                      AF.Exp, bias=bias_sb[:],
                                                      scale=ISD)
                                    d.then_inc(sems["ac"], 1)
                                ev.bump("ac", ("ac_exp", h, a, kc))
                            pos, mk = mask_of(a, kc)
                            if pos is not None and do_dv:
                                ev.wait(vv, "ac", EVT[("ac_exp", h, a, kc)])
                                if not recording:
                                    d = vv.tensor_mul(
                                        pt_sb[kc % 2][:, pos * 128:(pos + 1) * 128],
                                        pt_sb[kc % 2][:, pos * 128:(pos + 1) * 128],
                                        mk[:])
                                    d.then_inc(sems["dv"], 1)
                                ev.bump("dv", ("dv_mask", h, a, kc))
                            if kc >= 1:
                                emit_avl(kc - 1)
                        emit_avl(L - 1)

                        if do_ac:
                            ev.wait(ss, "pe", EVT[("pe_avl_last", h, a, L - 1)])
                            if not recording:
                                d = ss.activation(lr_sb[0:1, 0:512],
                                                  B_l0[0:1, 0:512], AF.Ln)
                                d.then_inc(sems["ac"], 1)
                            ev.bump("ac", ("ac_ln0", h, a))
                            if not recording:
                                d = ss.activation(lr_sb[0:1, 512:1024],
                                                  B_l0[32:33, 0:512], AF.Ln)
                                d.then_inc(sems["ac"], 1)
                            ev.bump("ac", ("ac_ln1", h, a))
                            if not recording:
                                ss.wait_ge(sems["ac"], cnt["ac"])
                                d = ss.activation(rl_bb[0:1, 0:1024],
                                                  lr_sb[0:1, 0:1024], AF.Exp,
                                                  scale=-1.0)
                                d.then_inc(sems["ac"], 1)
                            ev.bump("ac", ("ac_rinv", h, a))
                        if do_pe:
                            pendq.append(tail_part(h, a, 0))
                            pendq.append(tail_part(h, a, 1))
                        if do_dv:
                            for seg in range(2):
                                ev.wait(vv, "pe", EVT[("pe_rb", h, a, seg)])
                                if not recording:
                                    d = vv.tensor_copy(
                                        rb_sb[:, seg * 512:(seg + 1) * 512],
                                        B_r[:, 0:512])
                                    d.then_inc(sems["dv"], 1)
                                ev.bump("dv", ("dv_rcp", h, a, seg))
                            gpair = h * NPAIR + a
                            ev.swait(vv, EVT.get(("gp_av", gpair - 2), None))
                            if not recording:
                                vv.wait_ge(sems["dv"], cnt["dv"])
                                d = vv.tensor_mul(avb_sb[gpair % 2][:],
                                                  B_av[:, :], rb_sb[:])
                                d.then_inc(sems["dv"], 1)
                            ev.bump("dv", ("dv_avev", h, a))
                            if recording and a == NPAIR - 1:
                                EVT[("dv_avev_last", h)] = cnt["dv"]
                        if do_gp:
                            gpair = h * NPAIR + a
                            ev.wait(gp, "dv", EVT[("dv_avev", h, a)])
                            d = (gp.dma_start(
                                out=av_scr[h, :, a * 1024:(a + 1) * 1024],
                                in_=avb_sb[gpair % 2][:])
                                if not recording else None)
                            ev.sfill("avst", gpair % 2, d, ("gp_av", gpair))
                            if not recording and gp is not None \
                                    and gpair % 2 == 0:
                                emit_pchunk(gp)
                    if recording:
                        EVT[("pe_head_end", h)] = cnt["pe"]
                        EVT[("ac_head_end", h)] = cnt["ac"]
                if recording:
                    EVT[("acg", g + 1, 0)] = cnt["ac"]
                    for rnd in range(1, 8):
                        EVT[("acg", g + 1, rnd)] = cnt["ac"]
                    EVT[("peg_prev", g + 1, 0)] = cnt["pe"]
                    for rnd in range(1, 8):
                        EVT[("peg_prev", g + 1, rnd)] = cnt["pe"]
            if do_pe:
                flush_one()
                flush_one()
            if not recording and gp is not None:
                while _pq["i"] < len(prechunks):
                    emit_pchunk(gp)
            return cnt

        b_final = emit_B(True)

        @block.sync
        def _(sy):
            emit_B(False, {"sy": sy})

        @block.tensor
        def _(tn):
            emit_B(False, {"tn": tn})

        @block.scalar
        def _(ss):
            emit_B(False, {"s": ss})

        @block.vector
        def _(vv):
            emit_B(False, {"v": vv})

        @block.gpsimd
        def _(gp):
            emit_B(False, {"g": gp})

        for k in CNT:
            CNT[k] = b_final[k]
        b_pe_end = CNT["pe"]
        b_dv_end = CNT["dv"]

        b_ac_end = CNT["ac"]

        esB.close()
        esKV.close()

        mb1_sb = SB("mb1_sb", [128, OWN], F32)
        rb1_sb = SB("rb1_sb", [128, OWN], F32)
        esC = ExitStack()
        wob = [esC.enter_context(nc.sbuf_tensor(f"wob{i}", [128, OC * 128], BF16))
               for i in range(2)]
        xtt = [esC.enter_context(nc.sbuf_tensor(f"xtt{i}", [128, 512], F32))
               for i in range(4)]
        y1t = [esC.enter_context(nc.sbuf_tensor(f"y1t{i}", [128, 512], F32))
               for i in range(4)]
        ybf = [esC.enter_context(nc.sbuf_tensor(f"ybf{i}", [128, 512], BF16))
               for i in range(2)]
        ysq = [esC.enter_context(nc.sbuf_tensor(f"ysq{i}", [128, 512], BF16))
               for i in range(2)]
        mrow = esC.enter_context(nc.sbuf_tensor("mrow", [1, 1024], F32))
        vrow = esC.enter_context(nc.sbuf_tensor("vrow", [1, 1024], F32))
        rrow = esC.enter_context(nc.sbuf_tensor("rrow", [1, 1024], F32))

        def emit_ln_block(qh, scr_y, mb_sb_, rb_sb_, wsrc, which):
            pass

        c_base = dict(CNT)
        CEVT = EvtDict()

        def emit_C(recording, engines=None):
            EvtDict.recording = recording
            cnt = dict(c_base)

            def bump(kind, key=None, k=1):
                cnt[kind] += k
                if recording and key is not None:
                    CEVT[key] = cnt[kind]
                return cnt[kind]

            def wait(eng_obj, kind, val):
                if not recording and eng_obj is not None:
                    eng_obj.wait_ge(sems[kind], val)

            def sfill(name, slot, instr, key=None, k=16):
                skey = ("S", name, slot)
                cnt[skey] += k
                if instr is not None:
                    instr.then_inc(STREAMS[name][slot], k)
                if recording and key is not None:
                    CEVT[key] = (name, slot, cnt[skey])
                return (name, slot, cnt[skey])

            def swait(eng_obj, token):
                if not recording and eng_obj is not None and token:
                    eng_obj.wait_ge(STREAMS[token[0]][token[1]], token[2])

            tn = engines.get("tn") if engines else None
            vv = engines.get("v") if engines else None
            ss = engines.get("s") if engines else None
            sy = engines.get("sy") if engines else None
            gp = engines.get("g") if engines else None
            do_pe = recording or tn is not None
            do_dv = recording or vv is not None
            do_ac = recording or ss is not None
            do_sy = recording or sy is not None
            do_gp = recording or gp is not None

            for qh in range(NQH):
                if do_sy:
                    if qh > 0:
                        wait(sy, "pe", CEVT[("pe_proj", qh - 1, HC - 1, 1)])
                    for oc in range(OC):
                        if qh == 0:
                            for _a in range(NPAIR):
                                swait(sy, EVT[("gp_av", oc * NPAIR + _a)])
                        d = (sy.dma_start(
                            out=avh[:, oc * 1024:(oc + 1) * 1024],
                            in_=av_scr[oc, :, qh * 1024:(qh + 1) * 1024])
                            if not recording else None)
                        sfill("avh", 0, d, ("ld_av", qh, oc))
                    if qh == 0:
                        wait(sy, *cast_done["wo_bf"])
                        wait(sy, "pe", b_pe_end)
                    for hc in range(HC):
                        if hc >= 2:
                            wait(sy, "pe", CEVT[("pe_proj", qh, hc - 2, 1)])
                        d = (sy.dma_start(out=wob[hc % 2][:], in_=wo_bf[hc])
                             if not recording else None)
                        sfill("wo", hc % 2, d, ("ld_wo", qh, hc))
                        for qg in range(2):
                            ti = (hc * 2 + qg) % 4
                            jg = qh * HC * 2 + hc * 2 + qg
                            if jg >= 4:
                                swait(sy, CEVT[("gp_y1j", jg - 4)])
                            d = (sy.dma_start(
                                out=xtt[ti][:],
                                in_=xt_own[:, hc * OWN + qh * 1024 + qg * 512:
                                           hc * OWN + qh * 1024 + qg * 512 + 512])
                                if not recording else None)
                            sfill("xtt", ti, d, ("ld_xt", qh, hc, qg))

                def emit_stats(hc):
                    if not do_pe:
                        return
                    for qg in range(2):
                        wait(tn, "dv", CEVT[("dv_ysq", qh, hc, qg)])
                        for which in range(2):
                            src = (ybf, ysq)[which][(hc * 2 + qg) % 2]
                            rows = (ps_b[0:4, qg * 512:(qg + 1) * 512] if which == 0
                                    else ps_b[32:36, qg * 512:(qg + 1) * 512])
                            if not recording:
                                mm = tn.matmul(rows, ob_sb[:, 0:4], src[:],
                                               start=(hc == 0), stop=(hc == HC - 1))
                                mm.then_inc(sems["pe"], 1)
                            bump("pe", ("pe_st", qh, hc, qg, which))

                if do_pe:
                    for hc in range(HC):
                        swait(tn, CEVT[("ld_wo", qh, hc)])
                        if hc == 0:
                            swait(tn, CEVT[("ld_av", qh, OC - 1)])
                        if hc >= 2:
                            wait(tn, "dv", CEVT[("dv_evac", qh, hc - 2, 1)])
                        elif qh >= 1:
                            wait(tn, "dv", CEVT[("dv_lnb", qh - 1, 1, 1)])
                        for qg in range(2):
                            slot = (hc % 2) * 2 + qg
                            mm = None
                            for oc in range(OC):
                                if not recording:
                                    mm = tn.matmul(
                                        ps_a[:, slot * 512:(slot + 1) * 512],
                                        wob[hc % 2][:, oc * 128:(oc + 1) * 128],
                                        avh[:, oc * 1024 + qg * 512:
                                            oc * 1024 + qg * 512 + 512],
                                        start=(oc == 0), stop=(oc == OC - 1))
                            if not recording:
                                mm.then_inc(sems["pe"], 1)
                            bump("pe", ("pe_proj", qh, hc, qg))
                        if hc >= 1:
                            emit_stats(hc - 1)
                    emit_stats(HC - 1)
                    wait(tn, "dv", CEVT[("dv_lnrow", qh)])
                    for seg in range(2):
                        for which in range(2):
                            src = (mrow, rrow)[which]
                            if not recording:
                                mm = tn.matmul(
                                    ps_a[:, (seg * 2 + which) * 512:
                                         (seg * 2 + which + 1) * 512],
                                    of_sb[:, :],
                                    src[0:1, seg * 512:(seg + 1) * 512],
                                    start=True, stop=True)
                                mm.then_inc(sems["pe"], 1)
                            bump("pe", ("pe_lnb", qh, seg, which))

                if do_dv:
                    for hc in range(HC):
                        for qg in range(2):
                            ti = (hc * 2 + qg) % 4
                            wait(vv, "pe", CEVT[("pe_proj", qh, hc, qg)])
                            swait(vv, CEVT[("ld_xt", qh, hc, qg)])
                            if hc * 2 + qg >= 2:
                                wait(vv, "pe", CEVT.get(("pe_st", qh, hc - 1, qg, 1),
                                                        0))
                            slot = (hc % 2) * 2 + qg
                            if not recording:
                                d = vv.tensor_add(y1t[ti][:],
                                                  ps_a[:, slot * 512:(slot + 1) * 512],
                                                  xtt[ti][:])
                                d.then_inc(sems["dv"], 1)
                            bump("dv", ("dv_evac", qh, hc, qg))
                            if not recording:
                                vv.wait_ge(sems["dv"], cnt["dv"])
                                d = vv.tensor_copy(ybf[(hc * 2 + qg) % 2][:],
                                                   y1t[ti][:])
                                d.then_inc(sems["dv"], 1)
                            bump("dv", ("dv_ybf", qh, hc, qg))
                            if not recording:
                                vv.wait_ge(sems["dv"], cnt["dv"])
                                d = vv.tensor_mul(ysq[(hc * 2 + qg) % 2][:],
                                                  ybf[(hc * 2 + qg) % 2][:],
                                                  ybf[(hc * 2 + qg) % 2][:])
                                d.then_inc(sems["dv"], 1)
                            bump("dv", ("dv_ysq", qh, hc, qg))
                    wait(vv, "pe", CEVT[("pe_st", qh, HC - 1, 1, 1)])
                    if not recording:
                        d = vv.tensor_scalar_mul(mrow[0:1, :], ps_b[0:1, 0:1024],
                                                 1.0 / D)
                        d.then_inc(sems["dv"], 1)
                        vv.wait_ge(sems["dv"], cnt["dv"] + 1)
                        d = vv.tensor_scalar_mul(vrow[0:1, :], ps_b[32:33, 0:1024],
                                                 1.0 / D)
                        d.then_inc(sems["dv"], 1)
                        vv.wait_ge(sems["dv"], cnt["dv"] + 2)
                        d = vv.tensor_mul(rrow[0:1, :], mrow[0:1, :], mrow[0:1, :])
                        d.then_inc(sems["dv"], 1)
                        vv.wait_ge(sems["dv"], cnt["dv"] + 3)
                        d = vv.tensor_sub(vrow[0:1, :], vrow[0:1, :], rrow[0:1, :])
                        d.then_inc(sems["dv"], 1)
                    for _ in range(4):
                        bump("dv")
                    if recording:
                        CEVT[("dv_var", qh)] = cnt["dv"]
                if do_ac:
                    wait(ss, "dv", CEVT[("dv_var", qh)])
                    if not recording:
                        d = ss.activation(rrow[0:1, :], vrow[0:1, :], AF.Ln,
                                          bias=eps_sb[0:1, :])
                        d.then_inc(sems["ac"], 1)
                        ss.wait_ge(sems["ac"], cnt["ac"] + 1)
                        d = ss.activation(rrow[0:1, :], rrow[0:1, :], AF.Exp,
                                          scale=-0.5)
                        d.then_inc(sems["ac"], 1)
                    bump("ac")
                    bump("ac", ("ac_rstd", qh))
                if do_dv:
                    wait(vv, "ac", CEVT[("ac_rstd", qh)])
                    if recording:
                        CEVT[("dv_lnrow", qh)] = cnt["dv"]
                    if not recording:
                        d = vv.tensor_copy(dvm_sb[0:1, 0:1], rrow[0:1, 0:1])
                        d.then_inc(sems["dv"], 1)
                    bump("dv", ("dv_lnrow", qh))
                    for seg in range(2):
                        for which in range(2):
                            wait(vv, "pe", CEVT[("pe_lnb", qh, seg, which)])
                            dstb = (mb1_sb, rb1_sb)[which]
                            if not recording:
                                d = vv.tensor_copy(
                                    dstb[:, qh * 1024 + seg * 512:
                                         qh * 1024 + (seg + 1) * 512],
                                    ps_a[:, (seg * 2 + which) * 512:
                                         (seg * 2 + which + 1) * 512])
                                d.then_inc(sems["dv"], 1)
                            bump("dv", ("dv_lnb", qh, seg, which))

                if do_gp:
                    for hc in range(HC):
                        for qg in range(2):
                            ti = (hc * 2 + qg) % 4
                            wait(gp, "dv", CEVT[("dv_ybf", qh, hc, qg)])
                            d = (gp.dma_start(
                                out=y1_scr[:, hc * OWN + qh * 1024 + qg * 512:
                                           hc * OWN + qh * 1024 + qg * 512 + 512],
                                in_=y1t[ti][:])
                                if not recording else None)
                            tok = sfill("y1st", ti, d, ("gp_y1", qh, hc, qg))
                            if recording:
                                CEVT[("gp_y1j", qh * HC * 2 + hc * 2 + qg)] = tok
            return cnt

        c_final = emit_C(True)

        @block.sync
        def _(sy):
            emit_C(False, {"sy": sy})

        @block.tensor
        def _(tn):
            emit_C(False, {"tn": tn})

        @block.scalar
        def _(ss):
            emit_C(False, {"s": ss})

        @block.vector
        def _(vv):
            emit_C(False, {"v": vv})

        @block.gpsimd
        def _(gp):
            emit_C(False, {"g": gp})

        for k in CNT:
            CNT[k] = c_final[k]
        c_pe_end = CNT["pe"]
        c_dv_end = CNT["dv"]


        if ln1_dbg is not None:
            @block.sync
            def _(sy):
                sy.wait_ge(sems["dv"], c_dv_end)
                sfill_g("ld0", 0, sy.dma_start(out=ln1_dbg[:, 0:OWN], in_=mb1_sb[:]))
                sfill_g("ld0", 0, sy.dma_start(out=ln1_dbg[:, OWN:2 * OWN], in_=rb1_sb[:]))

        esC.close()
        esAV.close()

        esD = ExitStack()
        x1b = esD.enter_context(nc.sbuf_tensor("x1b", [128, HC * OWN], BF16))
        ytmp = [esD.enter_context(nc.sbuf_tensor(f"ytmp{i}", [128, OWN], F32))
                for i in range(2)]
        wgb = [esD.enter_context(nc.sbuf_tensor(f"wgb{i}", [128, HC * 128], BF16))
               for i in range(2)]
        wub = [esD.enter_context(nc.sbuf_tensor(f"wub{i}", [128, HC * 128], BF16))
               for i in range(2)]
        hsil = [esD.enter_context(nc.sbuf_tensor(f"hsil{i}", [128, 1024], BF16))
                for i in range(2)]
        hup = [esD.enter_context(nc.sbuf_tensor(f"hup{i}", [128, 1024], BF16))
               for i in range(2)]
        hout = [esD.enter_context(nc.sbuf_tensor(f"hout{i}", [128, 1024], BF16))
                for i in range(2)]

        d_base = dict(CNT)
        DEVT = EvtDict()

        def emit_D(recording, engines=None):
            EvtDict.recording = recording
            cnt = dict(d_base)

            def bump(kind, key=None, k=1):
                cnt[kind] += k
                if recording and key is not None:
                    DEVT[key] = cnt[kind]
                return cnt[kind]

            def wait(eng_obj, kind, val):
                if not recording and eng_obj is not None:
                    eng_obj.wait_ge(sems[kind], val)

            def sfill(name, slot, instr, key=None, k=16):
                skey = ("S", name, slot)
                cnt[skey] += k
                if instr is not None:
                    instr.then_inc(STREAMS[name][slot], k)
                if recording and key is not None:
                    DEVT[key] = (name, slot, cnt[skey])
                return (name, slot, cnt[skey])

            def swait(eng_obj, token):
                if not recording and eng_obj is not None and token:
                    eng_obj.wait_ge(STREAMS[token[0]][token[1]], token[2])

            tn = engines.get("tn") if engines else None
            vv = engines.get("v") if engines else None
            ss = engines.get("s") if engines else None
            sy = engines.get("sy") if engines else None
            gp = engines.get("g") if engines else None
            do_pe = recording or tn is not None
            do_dv = recording or vv is not None
            do_ac = recording or ss is not None
            do_sy = recording or sy is not None
            do_gp = recording or gp is not None

            if do_sy:
                swait(sy, CEVT[("gp_y1", NQH - 1, HC - 1, 0)])
                swait(sy, CEVT[("gp_y1", NQH - 1, HC - 1, 1)])
                swait(sy, CEVT[("gp_y1", NQH - 1, HC - 2, 0)])
                swait(sy, CEVT[("gp_y1", NQH - 1, HC - 2, 1)])
                wait(sy, "pe", c_pe_end)
                for hc in range(HC):
                    if hc >= 2:
                        wait(sy, "dv", DEVT[("dv_x1", hc - 2)])
                    d = (sy.dma_start(out=ytmp[hc % 2][:],
                                      in_=y1_scr[:, hc * OWN:(hc + 1) * OWN])
                         if not recording else None)
                    sfill("y1d", hc % 2, d, ("ld_y1", hc))
            if do_dv:
                for hc in range(HC):
                    swait(vv, DEVT[("ld_y1", hc)])
                    if hc == 0:
                        wait(vv, "dv", CEVT[("dv_lnb", NQH - 1, 1, 1)])
                    if not recording:
                        if hc >= 1:
                            vv.wait_ge(sems["dv"], cnt["dv"])
                        d = vv.tensor_sub(ytmp[hc % 2][:], ytmp[hc % 2][:],
                                          mb1_sb[:])
                        d.then_inc(sems["dv"], 1)
                    bump("dv")
                    if not recording:
                        vv.wait_ge(sems["dv"], cnt["dv"])
                        d = vv.tensor_mul(x1b[:, hc * OWN:(hc + 1) * OWN],
                                          ytmp[hc % 2][:], rb1_sb[:])
                        d.then_inc(sems["dv"], 1)
                    bump("dv", ("dv_x1", hc))
            if recording:
                DEVT["dv_x1_end"] = cnt["dv"]

            if do_sy:
                wait(sy, *cast_done["wg_bf"])
                wait(sy, *cast_done["wu_bf"])
                for ib in range(IC):
                    if ib >= 2:
                        wait(sy, "pe", DEVT[("pe_blk", ib - 2)])
                    d = (sy.dma_start(out=wgb[ib % 2][:], in_=wg_bf[ib])
                         if not recording else None)
                    sfill("wgu", ib % 2, d)
                    d = (sy.dma_start(out=wub[ib % 2][:], in_=wu_bf[ib])
                         if not recording else None)
                    sfill("wgu", ib % 2, d, ("ld_wu", ib))

            if do_pe:
                def d_block(ib, wv, ev):
                    if wv is None:
                        swait(tn, DEVT[("ld_wu", ib)])
                        if ib == 0:
                            wait(tn, "dv", DEVT["dv_x1_end"])
                    else:
                        if ib % 2 == 0:
                            tn.wait_ge(STREAMS["wgu"][0], wv)
                        else:
                            tn.wait_ge(STREAMS["wgu"][1], wv)
                            nc.regs_alu(wv, wv, 32, op=ALU.add)
                    for qh2 in range(NQH):
                        it = NQH * ib + qh2
                        par = it % 2
                        pg = (ps_a, ps_b)[par]
                        if ev is None:
                            if it >= 2:
                                wait(tn, "dv", DEVT[("dv_hact", it - 2)])
                        else:
                            tn.wait_ge(sems["dv"], ev)
                            nc.regs_alu(ev, ev, 2, op=ALU.add)
                        mm = None
                        for hc in range(HC):
                            for qg in range(2):
                                if not recording:
                                    mm = tn.matmul(
                                        pg[:, qg * 512:(qg + 1) * 512],
                                        wgb[ib % 2][:, hc * 128:(hc + 1) * 128],
                                        x1b[:, hc * OWN + qh2 * 1024 + qg * 512:
                                            hc * OWN + qh2 * 1024 + qg * 512 + 512],
                                        start=(hc == 0), stop=(hc == HC - 1))
                        for hc in range(HC):
                            for qg in range(2):
                                if not recording:
                                    mm = tn.matmul(
                                        pg[:, 1024 + qg * 512:1024 + (qg + 1) * 512],
                                        wub[ib % 2][:, hc * 128:(hc + 1) * 128],
                                        x1b[:, hc * OWN + qh2 * 1024 + qg * 512:
                                            hc * OWN + qh2 * 1024 + qg * 512 + 512],
                                        start=(hc == 0), stop=(hc == HC - 1))
                        if not recording:
                            mm.then_inc(sems["pe"], 1)
                        bump("pe", ("pe_dit", it))
                    if recording:
                        DEVT[("pe_blk", ib)] = cnt["pe"]

                d_block(0, None, None)
                if IC > 1:
                    d_block(1, None, None)
                npr = max(0, (IC - 2) // 2)
                if recording:
                    for ib in range(2, IC):
                        d_block(ib, None, None)
                else:
                    if npr > 0:
                        wv = nc.alloc_registers("d_wv", engines=[PE])
                        ev = nc.alloc_registers("d_ev", engines=[PE])
                        nc.regs_mov(wv, DEVT[("ld_wu", 2)][2])
                        nc.regs_mov(ev, DEVT[("dv_hact", 2 * NQH - 2)])
                        with kfori(nc, npr, PE):
                            d_block(2, wv, ev)
                            d_block(3, wv, ev)
                        for r in (wv, ev):
                            for h in r:
                                nc.free_register(h)
                    for ib in range(2 + 2 * npr, IC):
                        d_block(ib, None, None)

            for ib in range(IC):
                for qh2 in range(NQH):
                    it = NQH * ib + qh2
                    par = it % 2
                    pg = (ps_a, ps_b)[par]
                    if do_ac:
                        wait(ss, "pe", DEVT[("pe_dit", it)])
                        if it >= 2:
                            wait(ss, "dv", DEVT[("dv_hup", it - 2)])
                        if not recording:
                            d = ss.activation(hsil[par][:], pg[:, 0:1024], AF.Silu,
                                              scale=sg_sb[:, ib:ib + 1])
                            d.then_inc(sems["ac"], 1)
                        bump("ac", ("ac_sil", it))
                    if do_dv:
                        wait(vv, "pe", DEVT[("pe_dit", it)])
                        if it >= 2:
                            swait(vv, DEVT[("gp_h", it - 2)] if it >= 2 else None)
                        if not recording:
                            d = vv.tensor_scalar_mul(hup[par][:], pg[:, 1024:2048],
                                                     su_sb[:, ib:ib + 1])
                            d.then_inc(sems["dv"], 1)
                        bump("dv", ("dv_hup", it))
                        wait(vv, "ac", DEVT[("ac_sil", it)])
                        if not recording:
                            vv.wait_ge(sems["dv"], cnt["dv"])
                            d = vv.tensor_mul(hout[it % 2][:], hsil[par][:],
                                              hup[par][:])
                            d.then_inc(sems["dv"], 1)
                        bump("dv", ("dv_hact", it))
                    if do_gp:
                        wait(gp, "dv", DEVT[("dv_hact", it)])
                        d = (gp.dma_start(
                            out=hact_scr[ib, :, qh2 * 1024:(qh2 + 1) * 1024],
                            in_=hout[it % 2][:])
                            if not recording else None)
                        sfill("hst", it % 2, d, ("gp_h", it))
            return cnt

        d_final = emit_D(True)

        @block.sync
        def _(sy):
            emit_D(False, {"sy": sy})

        @block.tensor
        def _(tn):
            emit_D(False, {"tn": tn})

        @block.scalar
        def _(ss):
            emit_D(False, {"s": ss})

        @block.vector
        def _(vv):
            emit_D(False, {"v": vv})

        @block.gpsimd
        def _(gp):
            emit_D(False, {"g": gp})

        for k in CNT:
            CNT[k] = d_final[k]
        d_pe_end = CNT["pe"]


        esD.close()

        mb2_sb = SB("mb2_sb", [128, OWN], F32)
        rb2_sb = SB("rb2_sb", [128, OWN], F32)
        esE = ExitStack()
        NSL = 2 * HP4
        NIDXQ = HC // HP4
        wdb = esE.enter_context(nc.sbuf_tensor("wdb", [128, IC * HP4 * 128], BF16))
        hat = [esE.enter_context(nc.sbuf_tensor(f"hatg{i}", [128, 8 * 1024], BF16))
               for i in range(2)]
        y1e = [esE.enter_context(nc.sbuf_tensor(f"y1e{i}", [128, 512], F32))
               for i in range(4)]
        x1e = esE.enter_context(nc.sbuf_tensor("x1e", [128, 512], F32))
        y2t = [esE.enter_context(nc.sbuf_tensor(f"y2t{i}", [128, 512], F32))
               for i in range(4)]
        y2b = [esE.enter_context(nc.sbuf_tensor(f"y2b{i}", [128, 512], BF16))
               for i in range(2)]
        y2s = [esE.enter_context(nc.sbuf_tensor(f"y2s{i}", [128, 512], BF16))
               for i in range(2)]
        mrow2 = esE.enter_context(nc.sbuf_tensor("mrow2", [1, 1024], F32))
        vrow2 = esE.enter_context(nc.sbuf_tensor("vrow2", [1, 1024], F32))
        rrow2 = esE.enter_context(nc.sbuf_tensor("rrow2", [1, 1024], F32))
        macc = esE.enter_context(nc.sbuf_tensor("macc", [1, 1024], F32))
        vacc = esE.enter_context(nc.sbuf_tensor("vacc", [1, 1024], F32))
        x2f = [esE.enter_context(nc.sbuf_tensor(f"x2f{i}", [128, 1024], F32))
               for i in range(2)]
        ostg = [esE.enter_context(nc.sbuf_tensor(f"ostg{i}", [128, 128], F32))
                for i in range(2)]

        IGS = [(i0, min(IC, i0 + 8)) for i0 in range(0, IC, 8)]
        NIG = len(IGS)
        WCH = [(c0, min(IC, c0 + (IC + 3) // 4)) for c0 in
               range(0, IC, (IC + 3) // 4)]

        def psv(s):
            return (ps_a, ps_b)[s // 4][:, (s % 4) * 512:(s % 4) * 512 + 512]

        e_base = dict(CNT)
        EEVT = EvtDict()

        def emit_E(recording, engines=None):
            EvtDict.recording = recording
            cnt = dict(e_base)

            def bump(kind, key=None, k=1):
                cnt[kind] += k
                if recording and key is not None:
                    EEVT[key] = cnt[kind]
                return cnt[kind]

            def wait(eng_obj, kind, val):
                if not recording and eng_obj is not None:
                    eng_obj.wait_ge(sems[kind], val)

            def sfill(name, slot, instr, key=None, k=16):
                skey = ("S", name, slot)
                cnt[skey] += k
                if instr is not None:
                    instr.then_inc(STREAMS[name][slot], k)
                if recording and key is not None:
                    EEVT[key] = (name, slot, cnt[skey])
                return (name, slot, cnt[skey])

            def swait(eng_obj, token):
                if not recording and eng_obj is not None and token:
                    eng_obj.wait_ge(STREAMS[token[0]][token[1]], token[2])

            tn = engines.get("tn") if engines else None
            vv = engines.get("v") if engines else None
            ss = engines.get("s") if engines else None
            sy = engines.get("sy") if engines else None
            gp = engines.get("g") if engines else None
            do_pe = recording or tn is not None
            do_dv = recording or vv is not None
            do_ac = recording or ss is not None
            do_sy = recording or sy is not None
            do_gp = recording or gp is not None

            for qh in range(NQH):
                for hpr in range(NIDXQ):
                    idx = qh * NIDXQ + hpr
                    if do_sy:
                        if idx == 0:
                            wait(sy, *cast_done["wd_bf"])
                            wait(sy, "pe", d_pe_end)
                            for _s in range(2):
                                _tok = DEVT.get(("gp_h", NQH * IC - 1 - _s), None)
                                swait(sy, _tok)
                        for c, (c0, c1) in enumerate(WCH):
                            if idx > 0:
                                wait(sy, "pe", EEVT[("pe_wch", idx - 1, c)])
                            d = (sy.dma_start(
                                out=wdb[:, c0 * HP4 * 128:c1 * HP4 * 128],
                                in_=wd_bf[hpr][:, c0 * HP4 * 128:c1 * HP4 * 128])
                                if not recording else None)
                            sfill("wd", c % 4, d, ("ld_wd", idx, c))
                        for s in range(NSL):
                            jj = idx * NSL + s
                            ti = jj % 4
                            if jj >= 4:
                                wait(sy, "dv", EEVT[("dv_y2", jj - 4)])
                            hc = HP4 * hpr + s // 2
                            qg = s % 2
                            d = (sy.dma_start(
                                out=y1e[ti][:],
                                in_=y1_scr[:, hc * OWN + qh * 1024 + qg * 512:
                                           hc * OWN + qh * 1024 + qg * 512 + 512])
                                if not recording else None)
                            sfill("y1e", ti, d, ("ld_y1e", jj))

                    if do_gp:
                        if idx == 0:
                            for _s in range(2):
                                _tok = DEVT.get(("gp_h", NQH * IC - 1 - _s), None)
                                swait(gp, _tok)
                        for ig in range(NIG):
                            gi = idx * NIG + ig
                            i0, i1 = IGS[ig]
                            if gi >= 2:
                                wait(gp, "pe", EEVT[("pe_ig", gi - 2)])
                            src2 = hact_scr[i0:i1, :, qh * 1024:(qh + 1) * 1024]
                            src2 = src2.rearrange("i p q -> p i q")
                            d = (gp.dma_start(
                                out=hat[gi % 2][:, 0:(i1 - i0) * 1024]
                                .rearrange("p (i q) -> p i q", q=1024),
                                in_=src2)
                                if not recording else None)
                            sfill("hat", gi % 2, d, ("ld_ha", gi))

                    if do_pe:
                        for ig in range(NIG):
                            gi = idx * NIG + ig
                            i0, i1 = IGS[ig]
                            swait(tn, EEVT[("ld_ha", gi)])
                            mm = None
                            for ic in range(i0, i1):
                                for c, (c0, c1) in enumerate(WCH):
                                    if ic == c0:
                                        swait(tn, EEVT[("ld_wd", idx, c)])
                                for s in range(NSL):
                                    hc4 = s // 2
                                    qg = s % 2
                                    if ic == 0:
                                        if idx > 0:
                                            wait(tn, "dv",
                                                 EEVT[("dv_esta", idx - 1, s)])
                                        if hpr == 0 and qh > 0:
                                            ntr = HC * (NT // NQH)
                                            cands = [t for t in
                                                     range((qh - 1) * 0, qh * ntr)
                                                     if t % 4 == s % 4]
                                            if cands:
                                                wait(tn, "dv",
                                                     EEVT[("dv_ost", cands[-1])])
                                    if not recording:
                                        mm = tn.matmul(
                                            psv(s),
                                            wdb[:, ic * HP4 * 128 + hc4 * 128:
                                                ic * HP4 * 128 + hc4 * 128 + 128],
                                            hat[gi % 2][:, (ic - i0) * 1024
                                                        + qg * 512:
                                                        (ic - i0) * 1024
                                                        + qg * 512 + 512],
                                            start=(ic == 0), stop=(ic == IC - 1))
                                if not recording:
                                    mm.then_inc(sems["pe"], 1)
                                bump("pe", ("pe_ig", gi))
                        if recording:
                            for c in range(len(WCH)):
                                EEVT[("pe_wch", idx, c)] = cnt["pe"]
                        for s in range(NSL):
                            jj = idx * NSL + s
                            wait(tn, "dv", EEVT[("dv_y2s", jj)])
                            for which in range(2):
                                srcb = (y2b, y2s)[which][jj % 2]
                                rows = (psv(s)[0:4, :] if which == 0
                                        else psv(s)[32:36, :])
                                if not recording:
                                    mm = tn.matmul(rows, ob_sb[:, 0:4], srcb[:],
                                                   start=True, stop=True)
                                    mm.then_inc(sems["pe"], 1)
                                bump("pe", ("pe_est", jj, which))
                        if recording:
                            EEVT[("pe_idx", idx)] = cnt["pe"]

                    if do_dv:
                        for s in range(NSL):
                            jj = idx * NSL + s
                            ti = jj % 4
                            hc = HP4 * hpr + s // 2
                            wait(vv, "pe", EEVT[("pe_ig", idx * NIG + NIG - 1)])
                            if jj >= 2:
                                wait(vv, "pe", EEVT[("pe_est", jj - 2, 1)])
                            swait(vv, EEVT[("ld_y1e", jj)])
                            if jj >= 4:
                                swait(vv, EEVT[("gp_y2j", jj - 4)])
                            qg = s % 2
                            if not recording:
                                if jj > 0:
                                    vv.wait_ge(sems["dv"], cnt["dv"])
                                d = vv.tensor_sub(x1e[:], y1e[ti][:],
                                                  mb1_sb[:, qh * 1024 + qg * 512:
                                                         qh * 1024 + qg * 512
                                                         + 512])
                                d.then_inc(sems["dv"], 1)
                            bump("dv")
                            if not recording:
                                vv.wait_ge(sems["dv"], cnt["dv"])
                                d = vv.tensor_mul(x1e[:], x1e[:],
                                                  rb1_sb[:, qh * 1024 + qg * 512:
                                                         qh * 1024 + qg * 512
                                                         + 512])
                                d.then_inc(sems["dv"], 1)
                            bump("dv")
                            if not recording:
                                vv.wait_ge(sems["dv"], cnt["dv"])
                                d = vv.scalar_tensor_tensor(
                                    y2t[ti][:], psv(s), sd_sb[:, hc:hc + 1],
                                    x1e[:], ALU.mult, ALU.add)
                                d.then_inc(sems["dv"], 1)
                            bump("dv", ("dv_y2", jj))
                            if not recording:
                                vv.wait_ge(sems["dv"], cnt["dv"])
                                d = vv.tensor_copy(y2b[jj % 2][:], y2t[ti][:])
                                d.then_inc(sems["dv"], 1)
                            bump("dv", ("dv_y2b", jj))
                            if not recording:
                                vv.wait_ge(sems["dv"], cnt["dv"])
                                d = vv.tensor_mul(y2s[jj % 2][:], y2b[jj % 2][:],
                                                  y2b[jj % 2][:])
                                d.then_inc(sems["dv"], 1)
                            bump("dv", ("dv_y2s", jj))
                        for s in range(NSL):
                            jj = idx * NSL + s
                            qg = s % 2
                            col = qg * 512
                            wait(vv, "pe", EEVT[("pe_est", jj, 1)])
                            first = (hpr == 0 and s < 2)
                            if not recording:
                                vv.wait_ge(sems["dv"], cnt["dv"])
                                if first:
                                    d = vv.tensor_copy(macc[0:1, col:col + 512],
                                                       psv(s)[0:1, :])
                                else:
                                    d = vv.tensor_add(macc[0:1, col:col + 512],
                                                      macc[0:1, col:col + 512],
                                                      psv(s)[0:1, :])
                                d.then_inc(sems["dv"], 1)
                            bump("dv")
                            if not recording:
                                vv.wait_ge(sems["dv"], cnt["dv"])
                                if first:
                                    d = vv.tensor_copy(vacc[0:1, col:col + 512],
                                                       psv(s)[32:33, :])
                                else:
                                    d = vv.tensor_add(vacc[0:1, col:col + 512],
                                                      vacc[0:1, col:col + 512],
                                                      psv(s)[32:33, :])
                                d.then_inc(sems["dv"], 1)
                            bump("dv", ("dv_esta", idx, s))

                    if do_gp:
                        for s in range(NSL):
                            jj = idx * NSL + s
                            ti = jj % 4
                            hc = HP4 * hpr + s // 2
                            qg = s % 2
                            wait(gp, "dv", EEVT[("dv_y2b", jj)])
                            d = (gp.dma_start(
                                out=y2_scr[:, hc * OWN + qh * 1024 + qg * 512:
                                           hc * OWN + qh * 1024 + qg * 512 + 512],
                                in_=y2t[ti][:])
                                if not recording else None)
                            tok = sfill("y2st", jj % 4, d, ("gp_y2", qh, hc, qg))
                            if recording:
                                EEVT[("gp_y2j", jj)] = tok

                if do_dv:
                    if not recording:
                        vv.wait_ge(sems["dv"], cnt["dv"])
                        d = vv.tensor_scalar_mul(mrow2[0:1, :], macc[0:1, :],
                                                 1.0 / D)
                        d.then_inc(sems["dv"], 1)
                        vv.wait_ge(sems["dv"], cnt["dv"] + 1)
                        d = vv.tensor_scalar_mul(vrow2[0:1, :], vacc[0:1, :],
                                                 1.0 / D)
                        d.then_inc(sems["dv"], 1)
                        vv.wait_ge(sems["dv"], cnt["dv"] + 2)
                        d = vv.tensor_mul(rrow2[0:1, :], mrow2[0:1, :],
                                          mrow2[0:1, :])
                        d.then_inc(sems["dv"], 1)
                        vv.wait_ge(sems["dv"], cnt["dv"] + 3)
                        d = vv.tensor_sub(vrow2[0:1, :], vrow2[0:1, :],
                                          rrow2[0:1, :])
                        d.then_inc(sems["dv"], 1)
                    for _ in range(4):
                        bump("dv")
                    if recording:
                        EEVT[("dv_var2", qh)] = cnt["dv"]
                if do_ac:
                    wait(ss, "dv", EEVT[("dv_var2", qh)])
                    if not recording:
                        d = ss.activation(rrow2[0:1, :], vrow2[0:1, :], AF.Ln,
                                          bias=eps_sb[0:1, :])
                        d.then_inc(sems["ac"], 1)
                        ss.wait_ge(sems["ac"], cnt["ac"] + 1)
                        d = ss.activation(rrow2[0:1, :], rrow2[0:1, :], AF.Exp,
                                          scale=-0.5)
                        d.then_inc(sems["ac"], 1)
                    bump("ac")
                    bump("ac", ("ac_rstd2", qh))
                if do_dv:
                    wait(vv, "ac", EEVT[("ac_rstd2", qh)])
                    if not recording:
                        d = vv.tensor_copy(dvm_sb[0:1, 1:2], rrow2[0:1, 0:1])
                        d.then_inc(sems["dv"], 1)
                    bump("dv", ("dv_lnrow2", qh))
                if do_pe:
                    wait(tn, "dv", EEVT[("dv_lnrow2", qh)])
                    for seg in range(2):
                        for which in range(2):
                            srcr = (mrow2, rrow2)[which]
                            if not recording:
                                mm = tn.matmul(
                                    ps_a[:, (seg * 2 + which) * 512:
                                         (seg * 2 + which + 1) * 512],
                                    of_sb[:, :],
                                    srcr[0:1, seg * 512:(seg + 1) * 512],
                                    start=True, stop=True)
                                mm.then_inc(sems["pe"], 1)
                            bump("pe", ("pe_lnb2", qh, seg, which))
                if do_dv:
                    for seg in range(2):
                        for which in range(2):
                            wait(vv, "pe", EEVT[("pe_lnb2", qh, seg, which)])
                            dstb = (mb2_sb, rb2_sb)[which]
                            if not recording:
                                d = vv.tensor_copy(
                                    dstb[:, qh * 1024 + seg * 512:
                                         qh * 1024 + (seg + 1) * 512],
                                    ps_a[:, (seg * 2 + which) * 512:
                                         (seg * 2 + which + 1) * 512])
                                d.then_inc(sems["dv"], 1)
                            bump("dv", ("dv_lnb2", qh, seg, which))

                for hc in range(HC):
                    hcg = qh * HC + hc
                    if do_sy:
                        if hcg >= 2:
                            wait(sy, "dv", EEVT[("dv_x2", hcg - 2)])
                            wait(sy, "pe", EEVT[("pe_tr", hcg - 2, NT // NQH - 1)])
                        swait(sy, EEVT[("gp_y2", qh, hc, 0)])
                        swait(sy, EEVT[("gp_y2", qh, hc, 1)])
                        d = (sy.dma_start(
                            out=x2f[hcg % 2][:, 0:1024],
                            in_=y2_scr[:, hc * OWN + qh * 1024:
                                       hc * OWN + qh * 1024 + 1024])
                            if not recording else None)
                        sfill("y2f", hcg % 2, d, ("ld_y2f", hcg))
                    if do_dv:
                        swait(vv, EEVT[("ld_y2f", hcg)])
                        wait(vv, "dv", EEVT[("dv_lnb2", qh, 1, 1)])
                        if not recording:
                            vv.wait_ge(sems["dv"], cnt["dv"])
                            d = vv.tensor_sub(x2f[hcg % 2][:, 0:1024],
                                              x2f[hcg % 2][:, 0:1024],
                                              mb2_sb[:, qh * 1024:
                                                     qh * 1024 + 1024])
                            d.then_inc(sems["dv"], 1)
                        bump("dv")
                        if not recording:
                            vv.wait_ge(sems["dv"], cnt["dv"])
                            d = vv.tensor_mul(x2f[hcg % 2][:, 0:1024],
                                              x2f[hcg % 2][:, 0:1024],
                                              rb2_sb[:, qh * 1024:
                                                     qh * 1024 + 1024])
                            d.then_inc(sems["dv"], 1)
                        bump("dv", ("dv_x2", hcg))
                    for qt in range(NT // NQH):
                        qtg = qh * (NT // NQH) + qt
                        tr_i = hcg * (NT // NQH) + qt
                        slot = tr_i % 4
                        oslot = tr_i % 2
                        pvv = (ps_a, ps_b)[slot // 2][:, (slot % 2) * 512:
                                                      (slot % 2) * 512 + 128]
                        if do_pe:
                            wait(tn, "dv", EEVT[("dv_x2", hcg)])
                            if tr_i >= 4:
                                wait(tn, "dv", EEVT[("dv_ost", tr_i - 4)])
                            if hc == 0 and qt < 4:
                                wait(tn, "dv", EEVT[("dv_lnb2", qh, 1, 1)])
                            if not recording:
                                mm = tn.transpose(pvv,
                                                  x2f[hcg % 2][:, qt * 128:
                                                               (qt + 1) * 128],
                                                  idf[:])
                                mm.then_inc(sems["pe"], 1)
                            bump("pe", ("pe_tr", hcg, qt))
                        if do_dv:
                            wait(vv, "pe", EEVT[("pe_tr", hcg, qt)])
                            if tr_i >= 2:
                                swait(vv, EEVT[("gp_out", tr_i - 2)])
                            if not recording:
                                d = vv.tensor_copy(ostg[oslot][:], pvv)
                                d.then_inc(sems["dv"], 1)
                            bump("dv", ("dv_ost", tr_i))
                        if do_gp:
                            wait(gp, "dv", EEVT[("dv_ost", tr_i)])
                            d = (gp.dma_start(
                                out=out[qtg * 128:(qtg + 1) * 128,
                                        hc * 128:(hc + 1) * 128],
                                in_=ostg[oslot][:])
                                if not recording else None)
                            sfill("ost", oslot, d, ("gp_out", tr_i))
            if do_gp and not recording:
                for slot in range(2):
                    gp.wait_ge(STREAMS["ost"][slot], cnt[("S", "ost", slot)])
            return cnt

        e_final = emit_E(True)

        @block.sync
        def _(sy):
            emit_E(False, {"sy": sy})

        @block.tensor
        def _(tn):
            emit_E(False, {"tn": tn})

        @block.scalar
        def _(ss):
            emit_E(False, {"s": ss})

        @block.vector
        def _(vv):
            emit_E(False, {"v": vv})

        @block.gpsimd
        def _(gp):
            emit_E(False, {"g": gp})

        esE.close()

    return nc


def _rope_tables(S, dtype=np.float32):
    pos = np.arange(S, dtype=np.float64)[:, None]
    freqs = np.exp(-np.arange(0, 128, 2, dtype=np.float64)
                   * (math.log(ROPE_THETA) / 128.0))
    ang = pos * freqs
    return np.cos(ang).astype(dtype), np.sin(ang).astype(dtype)


def host_prep(cfg, inputs, core):
    S, D, NH, NKV, I = cfg["S"], cfg["D"], cfg["NH"], cfg["NKV"], cfg["I"]
    HC = D // 128
    ST = S // 128
    OWN = S // 2
    NT = OWN // 128
    OC = NH
    IC = I // 128
    KVW = NKV * 128
    NKVP = cfg.get("NKVP", min(4, NKV))
    KVQ = KVW // NKVP
    NGK = KVQ // 128
    KW = 2 * KVQ

    b, j = core // 2, core % 2
    x = np.asarray(inputs["L0_attn_hidden_states"][b])
    order = own_tile_order(NT)
    own_abs_tiles = [2 * t + j for t in order]
    own_cols = np.concatenate([np.arange(ta * 128, (ta + 1) * 128)
                               for ta in own_abs_tiles])

    xt = np.ascontiguousarray(x.T)
    xt_full = np.ascontiguousarray(
        xt.reshape(HC, 128, ST, 128).transpose(2, 1, 0, 3)
    ).reshape(ST, 128, HC * 128)
    xo = xt[:, own_cols]
    xt_own = np.ascontiguousarray(
        xo.reshape(HC, 128, OWN).transpose(1, 0, 2)).reshape(128, HC * OWN)

    wq = np.asarray(inputs["L0_attn_Wq"])
    wqT = np.ascontiguousarray(wq.T)
    wq_t = np.ascontiguousarray(
        wqT.reshape(HC, 128, OC // 2, 256).transpose(2, 1, 0, 3)
    ).reshape(OC // 2, 128, HC * 256)

    wk = np.asarray(inputs["L0_attn_Wk"])
    wv = np.asarray(inputs["L0_attn_Wv"])
    wkT, wvT = wk.T, wv.T
    wkv_t = np.empty((NKVP, 128, HC * KW), np.float32)
    for p in range(NKVP):
        blkk = wkT[:, p * KVQ:(p + 1) * KVQ]
        blkv = wvT[:, p * KVQ:(p + 1) * KVQ]
        cat = np.concatenate([blkk, blkv], axis=1)
        wkv_t[p] = np.ascontiguousarray(
            cat.reshape(HC, 128, KW).transpose(1, 0, 2)).reshape(128, HC * KW)

    wo = np.asarray(inputs["L0_attn_Wo"])
    woT = np.ascontiguousarray(wo.T)
    wo_t = np.ascontiguousarray(
        woT.reshape(OC, 128, HC, 128).transpose(2, 1, 0, 3)
    ).reshape(HC, 128, OC * 128)

    wg = np.asarray(inputs["L0_ffn_w_gate"])
    wu = np.asarray(inputs["L0_ffn_w_up"])
    wgT, wuT = wg.T, wu.T
    wg_t = np.ascontiguousarray(
        wgT.reshape(HC, 128, IC, 128).transpose(2, 1, 0, 3)
    ).reshape(IC, 128, HC * 128).astype(np.int32)
    wu_t = np.ascontiguousarray(
        wuT.reshape(HC, 128, IC, 128).transpose(2, 1, 0, 3)
    ).reshape(IC, 128, HC * 128).astype(np.int32)

    wd = np.asarray(inputs["L0_ffn_w_down"])
    wdT = np.ascontiguousarray(wd.T)
    HP4 = 4 if HC % 4 == 0 else 2
    wd_t = np.ascontiguousarray(
        wdT.reshape(IC, 128, HC // HP4, HP4 * 128).transpose(2, 1, 0, 3)
    ).reshape(HC // HP4, 128, IC * HP4 * 128).astype(np.int32)

    sg = np.ascontiguousarray(
        np.asarray(inputs["L0_ffn_s_gate"]).reshape(IC, 128).T)
    su = np.ascontiguousarray(
        np.asarray(inputs["L0_ffn_s_up"]).reshape(IC, 128).T)
    sd = np.ascontiguousarray(
        np.asarray(inputs["L0_ffn_s_down"]).reshape(HC, 128).T)

    cosf, sinf = _rope_tables(S)
    cosk = np.ascontiguousarray(np.tile(cosf, (1, NGK)))
    sink = np.ascontiguousarray(np.tile(sinf, (1, NGK)))
    cosq = np.ascontiguousarray(cosf[own_cols].T)
    sinq = np.ascontiguousarray(sinf[own_cols].T)

    tri = np.tril(np.ones((128, 128), np.float32))
    m_tril = np.triu(np.ones((128, 128), np.float32)).T
    m_tril = (np.arange(128)[:, None] <= np.arange(128)[None, :]).astype(np.float32)
    import ml_dtypes
    if j == 0:
        maskA = m_tril.astype(ml_dtypes.bfloat16)
        maskB = np.zeros((128, 128), ml_dtypes.bfloat16)
    else:
        maskA = np.ones((128, 128), ml_dtypes.bfloat16)
        maskB = m_tril.astype(ml_dtypes.bfloat16)

    identf = np.eye(128, dtype=np.float32)
    onesb = np.ones((128, 8), ml_dtypes.bfloat16)
    onesf = np.ones((1, 128), np.float32)

    return {
        "xt_full": xt_full, "xt_own": xt_own, "wq_t": wq_t, "wkv_t": wkv_t,
        "wo_t": wo_t, "wg_t": wg_t, "wu_t": wu_t, "wd_t": wd_t,
        "sg": sg, "su": su, "sd": sd,
        "cosk": cosk, "sink": sink, "cosq": cosq, "sinq": sinq,
        "maskA": np.asarray(maskA), "maskB": np.asarray(maskB),
        "identf": identf, "onesb": np.asarray(onesb), "onesf": onesf,
        "onesfb": np.asarray(onesf.astype(ml_dtypes.bfloat16)),
        "epsv": np.full((1, 1), LN_EPS, np.float32),
    }, own_abs_tiles


def kernel(**inputs):
    cfg = dict(CFG_FULL)
    S, D = cfg["S"], cfg["D"]
    OWN = S // 2
    NT = OWN // 128
    nc = build_program(cfg)
    in_maps = []
    tiles_per_core = []
    for core in range(8):
        m, own_abs = host_prep(cfg, inputs, core)
        in_maps.append(m)
        tiles_per_core.append(own_abs)
    res = run_bass_kernel_spmd(nc, in_maps, core_ids=list(range(8)))
    NB = cfg["NB"]
    outp = np.empty((NB, S, D), np.float32)
    for core in range(8):
        b = core // 2
        o = res.results[core]["out"]
        for idx, ta in enumerate(tiles_per_core[core]):
            outp[b, ta * 128:(ta + 1) * 128, :] = o[idx * 128:(idx + 1) * 128, :]
    return outp
